# version 04
# baseline (speedup 1.0000x reference)
import contextlib

import numpy as np
import ml_dtypes

import concourse.bass as bass
import concourse.mybir as mybir
from concourse.bass_utils import run_bass_kernel_spmd

F32 = mybir.dt.float32
BF16 = mybir.dt.bfloat16
F8 = mybir.dt.float8e5
AF = mybir.ActivationFunctionType
ALU = mybir.AluOpType

D = 1024
DC = 8
HID = 4096
HC = 32
T = 2048
ST = 512
NST = T // ST
QT = 256
NQT = T // QT
NH = 16
HD = 64
EPS_LN = 1e-5
EPS_ATTN = 1e-6
INV_D = 1.0 / D

ENGS = ("sync", "vector", "scalar", "tensor", "gpsimd")


class Prog:

    def __init__(self):
        self.ops = {e: [] for e in ENGS}
        self.sems = {}
        self.cnt = {}

    def add_sem(self, name, handle):
        self.sems[name] = handle
        self.cnt[name] = 0

    def wait(self, eng, sname, val):
        if val <= 0:
            return
        h = self.sems[sname]
        self.ops[eng].append(lambda e, h=h, v=val: e.wait_ge(h, v))

    def wait_ev(self, eng, ev):
        if ev is not None:
            self.wait(eng, ev[0], ev[1])

    def do(self, eng, fn, inc=None, k=1):
        if inc is None:
            self.ops[eng].append(fn)
            return None
        h = self.sems[inc]
        self.cnt[inc] += k
        def g(e, fn=fn, h=h, k=k):
            fn(e).then_inc(h, k)
        self.ops[eng].append(g)
        return (inc, self.cnt[inc])

    def now(self, sname):
        return (sname, self.cnt[sname])


def build_nc(phases=("ffn1", "attn", "ffn2")):
    nc = bass.Bass("TRN2", target_bir_lowering=False, debug=False,
                   detect_race_conditions=False)

    xT = nc.dram_tensor("xT", [D, T], F32, kind="ExternalInput").ap()
    w1t = nc.dram_tensor("w1t", [D, HID], F32, kind="ExternalInput").ap()
    w2t = nc.dram_tensor("w2t", [HID, D], F32, kind="ExternalInput").ap()
    wqt = nc.dram_tensor("wqt", [D, D], F32, kind="ExternalInput").ap()
    wkt = nc.dram_tensor("wkt", [D, D], F32, kind="ExternalInput").ap()
    wvt = nc.dram_tensor("wvt", [D, D], F32, kind="ExternalInput").ap()
    wot = nc.dram_tensor("wot", [D, D], F32, kind="ExternalInput").ap()
    b1_pp = nc.dram_tensor("b1_pp", [128, HC], F32, kind="ExternalInput").ap()
    b2_pp = nc.dram_tensor("b2_pp", [128, DC], F32, kind="ExternalInput").ap()
    bq_d = nc.dram_tensor("bq_d", [128, DC], F32, kind="ExternalInput").ap()
    bo_d = nc.dram_tensor("bo_d", [128, DC], F32, kind="ExternalInput").ap()
    rb_d = nc.dram_tensor("rb_d", [1, 2 * D], F32, kind="ExternalInput").ap()
    ebc_d = nc.dram_tensor("ebc_d", [16, D], F32, kind="ExternalInput").ap()
    maskT = nc.dram_tensor("maskT", [D, T], F8, kind="ExternalInput").ap()
    outT = nc.dram_tensor("outT", [D, T], F32, kind="ExternalOutput").ap()
    cc_in = nc.dram_tensor("cc_in", [128, 520], F32)
    cc_out = nc.dram_tensor("cc_out", [128, 520], F32)

    p = Prog()
    es0 = contextlib.ExitStack()
    with es0:
        def sbuf(name, shape, dt, stack=es0):
            return stack.enter_context(nc.sbuf_tensor(name, shape, dt))

        def sem(name):
            h = es0.enter_context(nc.semaphore(name))
            p.add_sem(name, h)
            return h

        sX = sbuf("sX", [128, DC * T], F32)
        sW1 = sbuf("sW1", [128, DC * HID], F8)
        wslotA = sbuf("wslotA", [128, DC * D], BF16)
        b1q8 = sbuf("b1q8", [128, HC], F8)
        b2q8 = sbuf("b2q8", [128, DC], F8)
        b1qf = sbuf("b1qf", [128, HC], F32)
        b2qf = sbuf("b2qf", [128, DC], F32)
        bq_pp = sbuf("bq_pp", [128, DC], F32)
        bo_pp = sbuf("bo_pp", [128, DC], F32)
        ebc = sbuf("ebc", [16, D], F32)
        onec_bf = sbuf("onec_bf", [128, 1], BF16)
        oner_f = sbuf("oner_f", [1, 128], F32)
        oner_bf = sbuf("oner_bf", [1, 128], BF16)
        skv = sbuf("skv", [128, 520], F32)
        statrow = sbuf("statrow", [1, 4 * 512], F32)

        for s in ("din", "dw", "v", "s", "t", "g", "do", "cc"):
            sem(s)

        pbank = [es0.enter_context(nc.psum_tensor(f"pb{i}", [128, 512], F32))
                 for i in range(8)]
        bank_free = [None] * 8

        x_ready = []
        for s in range(NST):
            for k in range(DC):
                p.do("sync", lambda e, k=k, s=s: e.dma_start(
                    out=sX[:, T * k + ST * s: T * k + ST * (s + 1)],
                    in_=xT[128 * k:128 * (k + 1), ST * s:ST * (s + 1)]),
                    inc="din", k=16)
            x_ready.append(p.cnt["din"])
            p.wait("sync", "din", p.cnt["din"])

        p.do("sync", lambda e: e.dma_start(out=bq_pp[:], in_=bq_d),
             inc="din", k=16)
        p.do("sync", lambda e: e.dma_start(out=bo_pp[:], in_=bo_d),
             inc="din", k=16)
        p.wait("sync", "din", p.cnt["din"])
        ev_bq = ev_bo = p.now("din")

        for k in range(DC):
            p.do("gpsimd", lambda e, k=k: e.dma_start(
                out=sW1[:, HID * k:HID * (k + 1)],
                in_=w1t[128 * k:128 * (k + 1), :]), inc="dw", k=16)
        p.wait("gpsimd", "dw", p.cnt["dw"])
        ev_w1 = p.now("dw")
        w2_in = w2t.rearrange("(c p) f -> p c f", p=128)
        p.do("gpsimd", lambda e: e.dma_start(out=b1q8[:], in_=b1_pp),
             inc="dw", k=16)
        p.do("gpsimd", lambda e: e.dma_start(out=b2q8[:], in_=b2_pp),
             inc="dw", k=16)
        p.do("gpsimd", lambda e: e.dma_start(out=ebc[:], in_=ebc_d),
             inc="dw", k=16)
        p.wait("gpsimd", "dw", p.cnt["dw"])
        ev_b1q = ev_b2q = ev_rb = ev_ebc = p.now("dw")

        p.wait_ev("vector", ev_b1q)
        p.wait_ev("vector", ev_b2q)
        p.do("vector", lambda e: e.memset(onec_bf[:], 1.0))
        p.do("vector", lambda e: e.memset(oner_f[:], 1.0))
        p.do("vector", lambda e: e.memset(oner_bf[:], 1.0))
        p.do("vector", lambda e: e.tensor_copy(out=b1qf[:], in_=b1q8[:]))
        ev_binit = p.do("vector", lambda e: e.tensor_copy(
            out=b2qf[:], in_=b2q8[:]), inc="v")

        ln_state = {"statrow_free": None}

        def emit_ln(tok0, ntok, xbfc, sqc, lnaux, tmpf, out_tile, out_col0,
                    x_wait_din=None, slotw=512):
            m_row = statrow[0:1, 0:ntok]
            veA = statrow[0:1, 512:512 + ntok]
            Brow = statrow[0:1, 1024:1024 + ntok]
            msq = Brow
            nrow = statrow[0:1, 1536:1536 + ntok]
            n2row = Brow

            for b in (4, 5):
                p.wait_ev("tensor", bank_free[b])
            xlc, s32, sqlc = lnaux
            ev_mm_list = [None] * DC
            for k in range(DC):
                sl = k % 2
                if x_wait_din is not None and k == 0:
                    p.wait("vector", "din", x_wait_din)
                if k >= 2:
                    p.wait_ev("vector", ev_mm_list[k - 2])
                xh = xbfc[:, slotw * sl:slotw * sl + ntok]
                xl = xlc[:, slotw * sl:slotw * sl + ntok]
                s3 = s32[:, 0:ntok]
                sqh = sqc[:, slotw * sl:slotw * sl + ntok]
                sql = sqlc[:, slotw * sl:slotw * sl + ntok]
                xs = sX[:, T * k + tok0:T * k + tok0 + ntok]
                p.do("vector", lambda e, xh=xh, xs=xs: e.tensor_copy(
                    out=xh, in_=xs))
                p.do("vector", lambda e, xl=xl, xs=xs, xh=xh: e.tensor_sub(
                    xl, xs, xh))
                p.do("vector", lambda e, s3=s3, xs=xs: e.tensor_mul(
                    s3, xs, xs))
                p.do("vector", lambda e, sqh=sqh, s3=s3: e.tensor_copy(
                    out=sqh, in_=s3))
                ev_cast = p.do("vector", lambda e, sql=sql, s3=s3, sqh=sqh:
                    e.tensor_sub(sql, s3, sqh), inc="v")
                p.wait_ev("tensor", ev_cast)
                p.do("tensor", lambda e, k=k, xh=xh: e.matmul(
                    pbank[4][0:1, 0:ntok], onec_bf[:], xh,
                    start=(k == 0), stop=False))
                p.do("tensor", lambda e, k=k, xl=xl: e.matmul(
                    pbank[4][0:1, 0:ntok], onec_bf[:], xl,
                    start=False, stop=(k == DC - 1)))
                p.do("tensor", lambda e, k=k, sqh=sqh: e.matmul(
                    pbank[5][0:1, 0:ntok], onec_bf[:], sqh,
                    start=(k == 0), stop=False))
                ev_mm_list[k] = p.do("tensor", lambda e, k=k, sql=sql: e.matmul(
                    pbank[5][0:1, 0:ntok], onec_bf[:], sql,
                    start=False, stop=(k == DC - 1)), inc="t")
            ev_mm = ev_mm_list[DC - 1]
            p.wait_ev("vector", ev_mm)
            p.wait_ev("vector", ln_state["statrow_free"])
            p.do("vector", lambda e: e.tensor_single_scalar(
                out=m_row, in_=pbank[4][0:1, 0:ntok], scalar=INV_D,
                op=ALU.mult))
            p.do("vector", lambda e: e.tensor_scalar(
                out=veA, in0=pbank[5][0:1, 0:ntok], scalar1=INV_D,
                scalar2=EPS_LN, op0=ALU.mult, op1=ALU.add))
            p.do("vector", lambda e: e.tensor_mul(msq, m_row, m_row))
            ev_ve = p.do("vector", lambda e: e.tensor_sub(veA, veA, msq),
                         inc="v")
            bank_free[4] = ev_ve
            bank_free[5] = ev_ve
            p.wait_ev("scalar", ev_ve)
            p.do("scalar", lambda e: e.activation(out=nrow, in_=veA, func=AF.Ln))
            ev_A = p.do("scalar", lambda e: e.activation(
                out=nrow, in_=nrow, func=AF.Exp, scale=-0.5), inc="s")
            p.wait_ev("vector", ev_A)
            p.do("vector", lambda e: e.tensor_mul(n2row, nrow, nrow))
            p.do("vector", lambda e: e.tensor_mul(n2row, n2row, veA))
            p.do("vector", lambda e: e.tensor_scalar(
                out=n2row, in0=n2row, scalar1=-0.5, scalar2=1.5,
                op0=ALU.mult, op1=ALU.add))
            ev_A = p.do("vector", lambda e: e.tensor_mul(veA, nrow, n2row),
                        inc="v")
            ev_B = p.do("vector", lambda e: e.scalar_tensor_tensor(
                out=Brow, in0=veA, scalar=-1.0, in1=m_row,
                op0=ALU.mult, op1=ALU.mult), inc="v")
            p.wait_ev("tensor", bank_free[6])
            p.wait_ev("tensor", bank_free[7])
            p.wait_ev("tensor", ev_A)
            p.wait_ev("tensor", ev_B)
            p.do("tensor", lambda e: e.matmul(
                pbank[6][:, 0:ntok], oner_f[:], veA, start=True, stop=True))
            ev_bc = p.do("tensor", lambda e: e.matmul(
                pbank[7][:, 0:ntok], oner_f[:], Brow, start=True, stop=True),
                inc="t")
            p.wait_ev("vector", ev_bc)
            ev_ap = None
            for k in range(DC):
                p.do("vector", lambda e, k=k: e.tensor_mul(
                    tmpf[:, 0:ntok],
                    sX[:, T * k + tok0:T * k + tok0 + ntok],
                    pbank[6][:, 0:ntok]))
                ev_ap = p.do("vector", lambda e, k=k: e.tensor_add(
                    out_tile[:, out_col0 + ntok * k:out_col0 + ntok * (k + 1)],
                    tmpf[:, 0:ntok],
                    pbank[7][:, 0:ntok]), inc="v")
            bank_free[6] = ev_ap
            bank_free[7] = ev_ap
            ln_state["statrow_free"] = ev_bc
            return ev_ap

        def emit_ffn(first):
            if not first:
                p.wait("sync", "v", p.cnt["v"])
                p.wait("sync", "t", p.cnt["t"])
            esf = contextlib.ExitStack()
            with esf:
                sW2 = sbuf("sW2F" + str(first), [128, HC * D], F8, esf)
                mask2 = sbuf("maskF" + str(first), [128, DC * ST], F8, esf)
                xnq = sbuf("xnqF" + str(first), [128, 2 * DC * ST], F8, esf)
                hbf = sbuf("hbfF" + str(first), [128, 2 * 512], BF16, esf)
                hq8 = sbuf("hq8F" + str(first), [128, HC * ST], F8, esf)
                y2b = sbuf("y2bF" + str(first), [128, 2 * 512], BF16, esf)
                tmpf = sbuf("tmpfF" + str(first), [128, 512], F32, esf)
                xbfc = sbuf("xbfF" + str(first), [128, 2 * 512], BF16, esf)
                sqc = sbuf("sqcF" + str(first), [128, 2 * 512], BF16, esf)
                lnaux = (sbuf("xlF" + str(first), [128, 2 * 512], BF16, esf),
                         sbuf("s32F" + str(first), [128, 512], F32, esf),
                         sbuf("sqlF" + str(first), [128, 2 * 512], BF16, esf))

                if not first:
                    p.wait("gpsimd", "v", p.cnt["v"])
                    p.wait("gpsimd", "t", p.cnt["t"])
                p.do("gpsimd", lambda e: e.dma_start(
                    out=sW2[:].rearrange("p (c f) -> p c f", c=HC), in_=w2_in),
                    inc="dw", k=16)
                p.wait("gpsimd", "dw", p.cnt["dw"])
                ev_w2 = p.now("dw")

                mask_free = [None]
                xnq_free = [None, None]
                hq8_free = None
                hbf_free = [None, None]
                y2_free = [None, None]

                ev_res = None
                for s in range(NST):
                    msl = 0
                    p.wait_ev("sync", mask_free[msl])
                    p.do("sync", lambda e, s=s, msl=msl: e.dma_start(
                        out=mask2[:, DC * ST * msl:DC * ST * (msl + 1)]
                            .rearrange("p (c f) -> p c f", c=DC),
                        in_=maskT.rearrange("(c p) f -> p c f", p=128)
                            [:, :, ST * s:ST * (s + 1)]), inc="din", k=16)
                    p.wait("sync", "din", p.cnt["din"])
                    ev_mask = p.now("din")

                    p.wait_ev("vector", xnq_free[msl])
                    ev_xnq = emit_ln(
                        ST * s, ST, xbfc, sqc, lnaux, tmpf,
                        xnq, DC * ST * msl,
                        x_wait_din=(x_ready[s] if first else None))

                    p.wait_ev("tensor", ev_w1)
                    ev_cast32 = None
                    for m in range(HC):
                        b = m % 2
                        p.wait_ev("tensor", bank_free[b])
                        p.wait_ev("tensor", ev_xnq)
                        for k in range(DC):
                            evg = p.do("tensor", lambda e, m=m, k=k, b=b, msl=msl: e.matmul(
                                pbank[b][:],
                                sW1[:, HID * k + 128 * m:HID * k + 128 * (m + 1)],
                                xnq[:, DC * ST * msl + ST * k:DC * ST * msl + ST * (k + 1)],
                                start=(k == 0), stop=(k == DC - 1)),
                                inc=("t" if k == DC - 1 else None))
                        hsl = m % 2
                        p.wait_ev("scalar", evg)
                        p.wait_ev("scalar", hbf_free[hsl])
                        if first and m == 0 and s == 0:
                            p.wait_ev("scalar", ev_binit)
                        ev_ev = p.do("scalar", lambda e, m=m, b=b, hsl=hsl: e.activation(
                            out=hbf[:, 512 * hsl:512 * (hsl + 1)],
                            in_=pbank[b][:], func=AF.Relu,
                            bias=b1qf[:, m:m + 1]), inc="s")
                        bank_free[b] = ev_ev
                        p.wait_ev("vector", ev_ev)
                        if hq8_free is not None and m == 0:
                            p.wait_ev("vector", hq8_free)
                        ev_cast32 = p.do("vector", lambda e, m=m, hsl=hsl: e.tensor_copy(
                            out=hq8[:, ST * m:ST * (m + 1)],
                            in_=hbf[:, 512 * hsl:512 * (hsl + 1)]), inc="v")
                        hbf_free[hsl] = ev_cast32

                    p.wait_ev("tensor", ev_w2)
                    for n in range(DC):
                        b = 2 + n % 2
                        p.wait_ev("tensor", bank_free[b])
                        p.wait_ev("tensor", ev_cast32)
                        for j in range(HC):
                            evg = p.do("tensor", lambda e, n=n, j=j, b=b: e.matmul(
                                pbank[b][:],
                                sW2[:, D * j + 128 * n:D * j + 128 * (n + 1)],
                                hq8[:, ST * j:ST * (j + 1)],
                                start=(j == 0), stop=(j == HC - 1)),
                                inc=("t" if j == HC - 1 else None))
                        if n == DC - 1:
                            hq8_free = evg
                        ysl = n % 2
                        p.wait_ev("scalar", evg)
                        p.wait_ev("scalar", y2_free[ysl])
                        ev_y = p.do("scalar", lambda e, n=n, b=b, ysl=ysl: e.activation(
                            out=y2b[:, 512 * ysl:512 * (ysl + 1)],
                            in_=pbank[b][:], func=AF.Identity,
                            bias=b2qf[:, n:n + 1]), inc="s")
                        bank_free[b] = ev_y
                        p.wait_ev("vector", ev_y)
                        p.wait_ev("vector", ev_mask)
                        p.do("vector", lambda e, n=n, ysl=ysl, msl=msl: e.tensor_mul(
                            tmpf[:, 0:512],
                            y2b[:, 512 * ysl:512 * (ysl + 1)],
                            mask2[:, DC * ST * msl + ST * n:DC * ST * msl + ST * (n + 1)]))
                        ev_res = p.do("vector", lambda e, n=n, s=s: e.tensor_add(
                            sX[:, T * n + ST * s:T * n + ST * (s + 1)],
                            sX[:, T * n + ST * s:T * n + ST * (s + 1)],
                            tmpf[:, 0:512]), inc="v")
                        y2_free[ysl] = ev_res
                        if n >= DC - 2:
                            mask_free[msl] = ev_res
                        xnq_free[msl] = evg
            return ev_res

        ev_ffn1 = emit_ffn(first=True) if "ffn1" in phases else None

        wk_in = wkt.rearrange("(c p) f -> p c f", p=128)
        p.do("gpsimd", lambda e: e.dma_start(
            out=wslotA[:].rearrange("p (c f) -> p c f", c=DC), in_=wk_in),
            inc="dw", k=16)
        p.wait("gpsimd", "dw", p.cnt["dw"])
        ev_wk = p.now("dw")

        if "attn" in phases:
            es_a = contextlib.ExitStack()
            with es_a:
                wslotB = sbuf("wslotB", [128, DC * D], BF16, es_a)

                es_kv = contextlib.ExitStack()
                with es_kv:
                    bcK = sbuf("bcK", [128, D], F32, es_kv)
                    bcV = sbuf("bcV", [128, D], F32, es_kv)
                    es_rb = contextlib.ExitStack()
                    rbf32 = sbuf("rbf32", [1, 2 * D], F32, es_rb)

                    p.wait_ev("gpsimd", ev_ffn1)
                    p.wait("gpsimd", "t", p.cnt["t"])
                    wv_in = wvt.rearrange("(c p) f -> p c f", p=128)
                    p.do("gpsimd", lambda e: e.dma_start(out=rbf32[:], in_=rb_d),
                         inc="dw", k=16)
                    p.do("gpsimd", lambda e: e.dma_start(
                        out=wslotB[:].rearrange("p (c f) -> p c f", c=DC),
                        in_=wv_in), inc="dw", k=16)
                    p.wait("gpsimd", "dw", p.cnt["dw"])
                    ev_wv = p.now("dw")
                    p.wait_ev("tensor", ev_wv)
                    for b in (0, 1, 2, 3):
                        p.wait_ev("tensor", bank_free[b])
                    for half in range(2):
                        p.do("tensor", lambda e, half=half: e.matmul(
                            pbank[half][:], oner_f[:],
                            rbf32[0:1, 512 * half:512 * (half + 1)],
                            start=True, stop=True))
                        ev_bc0 = p.do("tensor", lambda e, half=half: e.matmul(
                            pbank[2 + half][:], oner_f[:],
                            rbf32[0:1, D + 512 * half:D + 512 * (half + 1)],
                            start=True, stop=True), inc="t")
                    p.wait_ev("vector", ev_bc0)
                    for half in range(2):
                        p.do("vector", lambda e, half=half: e.tensor_copy(
                            out=bcK[:, 512 * half:512 * (half + 1)],
                            in_=pbank[half][:]))
                        ev_bcv = p.do("vector", lambda e, half=half: e.tensor_copy(
                            out=bcV[:, 512 * half:512 * (half + 1)],
                            in_=pbank[2 + half][:]), inc="v")
                    for b in (0, 1, 2, 3):
                        bank_free[b] = ev_bcv
                    es_rb.close()
                    xn2 = sbuf("xn2", [128, DC * ST], BF16, es_kv)
                    ktm = sbuf("ktm", [128, D], BF16, es_kv)
                    v65 = sbuf("v65", [128, NH * 65], BF16, es_kv)
                    xbfA = sbuf("xbfA", [128, 2 * 512], BF16, es_kv)
                    sqA = sbuf("sqA", [128, 2 * 512], BF16, es_kv)
                    lnauxA = (sbuf("xlA", [128, 2 * 512], BF16, es_kv),
                              sbuf("s32A", [128, 512], F32, es_kv),
                              sbuf("sqlA", [128, 2 * 512], BF16, es_kv))
                    tmpa = sbuf("tmpa", [128, 512], F32, es_kv)
                    tmpb = lnauxA[1]

                    p.do("vector", lambda e: e.memset(
                        v65[:].rearrange("p (h c) -> p h c", c=65)[:, :, 64:65], 1.0))

                    xn2_free = None
                    ktm_free = None
                    v65_free = None
                    ev_kvadd = None
                    for s in range(NST):
                        p.wait_ev("vector", xn2_free)
                        p.wait_ev("vector", ev_ffn1)
                        ev_xn2 = emit_ln(ST * s, ST, xbfA, sqA, lnauxA, tmpa, xn2, 0)
                        for t4 in range(4):
                            t_ = 4 * s + t4
                            p.wait_ev("tensor", ev_wk)
                            p.wait_ev("tensor", ev_xn2)
                            for half in range(2):
                                b = half
                                p.wait_ev("tensor", bank_free[b])
                                for k in range(DC):
                                    evg = p.do("tensor", lambda e, k=k, b=b, t4=t4, half=half: e.matmul(
                                        pbank[b][:],
                                        xn2[:, ST * k + 128 * t4:ST * k + 128 * (t4 + 1)],
                                        wslotA[:, D * k + 512 * half:D * k + 512 * (half + 1)],
                                        start=(k == 0), stop=(k == DC - 1)),
                                        inc=("t" if k == DC - 1 else None))
                                p.wait_ev("vector", evg)
                                p.wait_ev("vector", ktm_free)
                                p.do("vector", lambda e, b=b, half=half: e.tensor_add(
                                    tmpa[:], pbank[b][:],
                                    bcK[:, 512 * half:512 * (half + 1)]))
                                ev_mn = p.do("vector", lambda e: e.tensor_scalar_min(
                                    out=tmpb[:], in0=tmpa[:], scalar1=0.0), inc="v")
                                p.wait_ev("scalar", ev_mn)
                                ev_ex = p.do("scalar", lambda e: e.activation(
                                    out=tmpb[:], in_=tmpb[:], func=AF.Exp), inc="s")
                                p.wait_ev("vector", ev_ex)
                                ev_k = p.do("vector", lambda e, half=half: e.scalar_tensor_tensor(
                                    out=ktm[:, 512 * half:512 * (half + 1)],
                                    in0=tmpa[:], scalar=0.0, in1=tmpb[:],
                                    op0=ALU.max, op1=ALU.add), inc="v")
                                bank_free[b] = ev_k
                            p.wait_ev("tensor", ev_wv)
                            for half in range(2):
                                b = 2 + half
                                p.wait_ev("tensor", bank_free[b])
                                for k in range(DC):
                                    evg = p.do("tensor", lambda e, k=k, b=b, t4=t4, half=half: e.matmul(
                                        pbank[b][:],
                                        xn2[:, ST * k + 128 * t4:ST * k + 128 * (t4 + 1)],
                                        wslotB[:, D * k + 512 * half:D * k + 512 * (half + 1)],
                                        start=(k == 0), stop=(k == DC - 1)),
                                        inc=("t" if k == DC - 1 else None))
                                p.wait_ev("vector", evg)
                                p.wait_ev("vector", v65_free)
                                ev_v = p.do("vector", lambda e, b=b, half=half: e.tensor_add(
                                    v65[:].rearrange("p (h c) -> p h c", c=65)
                                        [:, 8 * half:8 * (half + 1), 0:64],
                                    pbank[b][:].rearrange("p (h c) -> p h c", c=64),
                                    bcV[:, 512 * half:512 * (half + 1)]
                                        .rearrange("p (h c) -> p h c", c=64)),
                                    inc="v")
                                bank_free[b] = ev_v
                            p.wait_ev("tensor", ev_k)
                            p.wait_ev("tensor", ev_v)
                            for q in range(4, 8):
                                p.wait_ev("tensor", bank_free[q])
                            evkv = None
                            for h in range(NH):
                                bkn = 4 + h // 4
                                r0 = 64 * (h % 2)
                                c0 = 65 * ((h % 4) // 2)
                                evkv = p.do("tensor", lambda e, h=h, bkn=bkn, r0=r0, c0=c0: e.matmul(
                                    pbank[bkn][r0:r0 + 64, c0:c0 + 65],
                                    ktm[:, HD * h:HD * (h + 1)],
                                    v65[:, 65 * h:65 * (h + 1)],
                                    start=True, stop=True,
                                    tile_position=(0, r0)),
                                    inc=("t" if h == NH - 1 else None))
                            ktm_free = evkv
                            v65_free = evkv
                            p.wait_ev("vector", evkv)
                            for q in range(4):
                                if t_ == 0:
                                    ev_kvadd = p.do("vector", lambda e, q=q: e.tensor_copy(
                                        out=skv[:, 130 * q:130 * (q + 1)],
                                        in_=pbank[4 + q][:, 0:130]), inc="v")
                                else:
                                    ev_kvadd = p.do("vector", lambda e, q=q: e.tensor_add(
                                        skv[:, 130 * q:130 * (q + 1)],
                                        skv[:, 130 * q:130 * (q + 1)],
                                        pbank[4 + q][:, 0:130]), inc="v")
                            for q in range(4, 8):
                                bank_free[q] = ev_kvadd
                            xn2_free = evkv if t4 == 3 else xn2_free

                    p.wait("gpsimd", "t", p.cnt["t"])
                    wq_in = wqt.rearrange("(c p) f -> p c f", p=128)
                    p.do("gpsimd", lambda e: e.dma_start(
                        out=wslotA[:].rearrange("p (c f) -> p c f", c=DC),
                        in_=wq_in), inc="dw", k=16)
                    p.wait("gpsimd", "dw", p.cnt["dw"])
                    ev_wq = p.now("dw")

                    p.wait_ev("gpsimd", ev_kvadd)
                    p.do("gpsimd", lambda e: e.dma_start(
                        out=cc_in[:, :], in_=skv[:]), inc="dw", k=16)
                    p.wait("gpsimd", "dw", p.cnt["dw"])
                    p.do("gpsimd", lambda e: e.collective_compute(
                        "AllReduce", ALU.add,
                        replica_groups=[[0, 1], [2, 3], [4, 5], [6, 7]],
                        ins=[cc_in.ap().opt()], outs=[cc_out.ap().opt()]),
                        inc="cc")
                    p.wait("gpsimd", "cc", 1)
                    p.do("gpsimd", lambda e: e.dma_start(
                        out=skv[:], in_=cc_out[:, :]), inc="dw", k=16)
                    p.wait("gpsimd", "dw", p.cnt["dw"])
                    ev_kvtot = p.now("dw")

                es_q = contextlib.ExitStack()
                with es_q:
                    wolo = sbuf("wolo", [128, DC * D], F8, es_q)
                    wostage = sbuf("wostage", [128, 512], F32, es_q)
                    xn2q = sbuf("xn2q", [128, DC * QT], BF16, es_q)
                    qphi = sbuf("qphi", [128, DC * QT], BF16, es_q)
                    outpre = sbuf("outpre", [128, DC * QT], BF16, es_q)
                    outpre_lo = sbuf("outpre_lo", [128, DC * QT], BF16, es_q)
                    outpre_h8 = sbuf("outpre_h8", [128, DC * QT], F8, es_q)
                    kvbd = sbuf("kvbd", [128, DC * 128], BF16, es_q)
                    zbd = sbuf("zbd", [128, DC * 16], BF16, es_q)
                    tmpn = sbuf("tmpn", [16, QT], F32, es_q)
                    xbfQ = sbuf("xbfQ", [128, 2 * QT], BF16, es_q)
                    sqQ = sbuf("sqQ", [128, 2 * QT], BF16, es_q)
                    lnauxQ = (sbuf("xlQ", [128, 2 * QT], BF16, es_q),
                              sbuf("s32Q", [128, QT], F32, es_q),
                              sbuf("sqlQ", [128, 2 * QT], BF16, es_q))
                    tmpaQ = sbuf("tmpaQ", [128, QT], F32, es_q)
                    tmpeQ = sbuf("tmpeQ", [128, QT], F32, es_q)
                    tmpoQ = sbuf("tmpoQ", [128, QT], F32, es_q)

                    p.wait_ev("vector", ev_kvtot)
                    p.do("vector", lambda e: e.memset(kvbd[:], 0.0))
                    p.do("vector", lambda e: e.memset(zbd[:], 0.0))
                    ev_bd = None
                    for h in range(NH):
                        r0 = 64 * (h % 2)
                        c0 = 130 * (h // 4) + 65 * ((h % 4) // 2)
                        blk = h // 2
                        p.do("vector", lambda e, h=h, r0=r0, c0=c0, blk=blk: e.tensor_copy(
                            out=kvbd[r0:r0 + 64,
                                     128 * blk + 64 * (h % 2):128 * blk + 64 * (h % 2) + 64],
                            in_=skv[r0:r0 + 64, c0:c0 + 64]))
                        ev_bd = p.do("vector", lambda e, h=h, r0=r0, c0=c0, blk=blk: e.tensor_copy(
                            out=zbd[r0:r0 + 64, 16 * blk + h:16 * blk + h + 1],
                            in_=skv[r0:r0 + 64, c0 + 64:c0 + 65]), inc="v")

                    ev_res_attn = None
                    qphi_free = None
                    outpre_free = None
                    for s in range(NQT):
                        ev_xq = emit_ln(QT * s, QT, xbfQ, sqQ, lnauxQ, tmpaQ,
                                        xn2q, 0, slotw=QT)
                        p.wait_ev("tensor", ev_wq)
                        ev_qphi = None
                        for n in range(DC):
                            b = 4 + n % 2
                            p.wait_ev("tensor", bank_free[b])
                            p.wait_ev("tensor", ev_xq)
                            for k in range(DC):
                                evg = p.do("tensor", lambda e, n=n, k=k, b=b: e.matmul(
                                    pbank[b][:, 0:QT],
                                    wslotA[:, D * k + 128 * n:D * k + 128 * (n + 1)],
                                    xn2q[:, QT * k:QT * (k + 1)],
                                    start=(k == 0), stop=(k == DC - 1)),
                                    inc=("t" if k == DC - 1 else None))
                            p.wait_ev("vector", evg)
                            p.wait_ev("vector", qphi_free)
                            ev_mn = p.do("vector", lambda e, n=n, b=b: e.tensor_scalar(
                                out=tmpaQ[:], in0=pbank[b][:, 0:QT],
                                scalar1=bq_pp[:, n:n + 1], scalar2=0.0,
                                op0=ALU.add, op1=ALU.min), inc="v")
                            p.wait_ev("scalar", ev_mn)
                            p.wait_ev("scalar", ev_bq)
                            ev_ex = p.do("scalar", lambda e: e.activation(
                                out=tmpeQ[:], in_=tmpaQ[:], func=AF.Exp), inc="s")
                            p.wait_ev("vector", ev_ex)
                            p.do("vector", lambda e, n=n, b=b: e.tensor_scalar(
                                out=tmpaQ[:], in0=pbank[b][:, 0:QT],
                                scalar1=bq_pp[:, n:n + 1], scalar2=0.0,
                                op0=ALU.add, op1=ALU.max))
                            ev_qphi = p.do("vector", lambda e, n=n: e.tensor_add(
                                qphi[:, QT * n:QT * (n + 1)], tmpaQ[:], tmpeQ[:]),
                                inc="v")
                            bank_free[b] = ev_qphi
                        p.wait_ev("tensor", bank_free[6])
                        p.wait_ev("tensor", ev_bd)
                        p.wait_ev("tensor", ev_qphi)
                        for k in range(DC):
                            evg = p.do("tensor", lambda e, k=k: e.matmul(
                                pbank[6][0:16, 0:QT], zbd[:, 16 * k:16 * (k + 1)],
                                qphi[:, QT * k:QT * (k + 1)],
                                start=(k == 0), stop=(k == DC - 1)),
                                inc=("t" if k == DC - 1 else None))
                        p.wait_ev("vector", evg)
                        p.do("vector", lambda e: e.tensor_single_scalar(
                            out=tmpn[:], in_=pbank[6][0:16, 0:QT],
                            scalar=EPS_ATTN, op=ALU.add))
                        ev_rec = p.do("vector", lambda e: e.reciprocal(
                            out=tmpn[:], in_=tmpn[:]), inc="v")
                        bank_free[6] = ev_rec
                        for k in range(DC):
                            b = 4 + k % 2
                            br = 6 + k % 2
                            p.wait_ev("tensor", bank_free[b])
                            p.wait_ev("tensor", bank_free[br])
                            p.wait_ev("tensor", ev_rec)
                            evg = p.do("tensor", lambda e, k=k, b=b: e.matmul(
                                pbank[b][:, 0:QT], kvbd[:, 128 * k:128 * (k + 1)],
                                qphi[:, QT * k:QT * (k + 1)],
                                start=True, stop=True))
                            evg = p.do("tensor", lambda e, k=k, br=br: e.matmul(
                                pbank[br][:, 0:QT], ebc[:, 128 * k:128 * (k + 1)],
                                tmpn[:], start=True, stop=True), inc="t")
                            p.wait_ev("vector", evg)
                            p.wait_ev("vector", outpre_free)
                            p.do("vector", lambda e, k=k, b=b: e.tensor_copy(
                                out=tmpoQ[:], in_=pbank[b][:, 0:QT]))
                            p.do("vector", lambda e, k=k, br=br: e.tensor_mul(
                                tmpoQ[:], tmpoQ[:], pbank[br][:, 0:QT]))
                            p.do("vector", lambda e, k=k: e.tensor_copy(
                                out=outpre[:, QT * k:QT * (k + 1)], in_=tmpoQ[:]))
                            p.do("vector", lambda e, k=k: e.tensor_sub(
                                outpre_lo[:, QT * k:QT * (k + 1)], tmpoQ[:],
                                outpre[:, QT * k:QT * (k + 1)]))
                            ev_op = p.do("vector", lambda e, k=k: e.tensor_copy(
                                out=outpre_h8[:, QT * k:QT * (k + 1)],
                                in_=outpre[:, QT * k:QT * (k + 1)]), inc="v")
                            bank_free[b] = ev_op
                            bank_free[br] = ev_op
                        qphi_free = ev_op
                        if s == 0:
                            p.wait("gpsimd", "v", p.cnt["v"])
                            wo_in = wot.rearrange("(c p) f -> p c f", p=128)
                            p.do("gpsimd", lambda e: e.dma_start(
                                out=wslotB[:].rearrange("p (c f) -> p c f", c=DC),
                                in_=wo_in), inc="dw", k=16)
                            p.wait("gpsimd", "dw", p.cnt["dw"])
                            ev_wo = p.now("dw")
                            ev_wl = None
                            for k in range(DC):
                                for hf in range(2):
                                    p.wait_ev("gpsimd", ev_wl)
                                    evst = p.do("gpsimd", lambda e, k=k, hf=hf: e.dma_start(
                                        out=wostage[:],
                                        in_=wot[128 * k:128 * (k + 1),
                                                512 * hf:512 * (hf + 1)]),
                                        inc="dw", k=16)
                                    p.wait("gpsimd", "dw", p.cnt["dw"])
                                    p.wait_ev("vector", evst)
                                    ev_wl = p.do("vector", lambda e, k=k, hf=hf: e.tensor_sub(
                                        wolo[:, D * k + 512 * hf:D * k + 512 * (hf + 1)],
                                        wostage[:],
                                        wslotB[:, D * k + 512 * hf:D * k + 512 * (hf + 1)]),
                                        inc="v")
                            ev_wo_lo = ev_wl
                        p.wait_ev("tensor", ev_wo)
                        p.wait_ev("tensor", ev_wo_lo)
                        for n in range(DC):
                            b = n % 2
                            p.wait_ev("tensor", bank_free[b])
                            p.wait_ev("tensor", ev_op)
                            for k in range(DC):
                                p.do("tensor", lambda e, n=n, k=k, b=b: e.matmul(
                                    pbank[b][:, 0:QT],
                                    wslotB[:, D * k + 128 * n:D * k + 128 * (n + 1)],
                                    outpre[:, QT * k:QT * (k + 1)],
                                    start=(k == 0), stop=False))
                                p.do("tensor", lambda e, n=n, k=k, b=b: e.matmul(
                                    pbank[b][:, 0:QT],
                                    wslotB[:, D * k + 128 * n:D * k + 128 * (n + 1)],
                                    outpre_lo[:, QT * k:QT * (k + 1)],
                                    start=False, stop=False))
                                evg = p.do("tensor", lambda e, n=n, k=k, b=b: e.matmul(
                                    pbank[b][:, 0:QT],
                                    wolo[:, D * k + 128 * n:D * k + 128 * (n + 1)],
                                    outpre_h8[:, QT * k:QT * (k + 1)],
                                    start=False, stop=(k == DC - 1)),
                                    inc=("t" if k == DC - 1 else None))
                            p.wait_ev("vector", evg)
                            p.wait_ev("vector", ev_bo)
                            ev_res_attn = p.do("vector", lambda e, n=n, s=s, b=b: e.scalar_tensor_tensor(
                                out=sX[:, T * n + QT * s:T * n + QT * (s + 1)],
                                in0=pbank[b][:, 0:QT], scalar=bo_pp[:, n:n + 1],
                                in1=sX[:, T * n + QT * s:T * n + QT * (s + 1)],
                                op0=ALU.add, op1=ALU.add), inc="v")
                            bank_free[b] = ev_res_attn
                        outpre_free = evg

        ev_ffn2 = emit_ffn(first=False) if "ffn2" in phases else None

        p.wait_ev("sync", ev_ffn2)
        p.wait("sync", "v", p.cnt["v"])
        ndo = 0
        for k in range(DC):
            p.do("sync", lambda e, k=k: e.dma_start(
                out=outT[128 * k:128 * (k + 1), :],
                in_=sX[:, T * k:T * (k + 1)]), inc="do", k=16)
            ndo += 16
        p.wait("sync", "do", ndo)

        with nc.Block() as block:
            for eng_name, deco in (("sync", block.sync), ("vector", block.vector),
                                   ("scalar", block.scalar), ("tensor", block.tensor),
                                   ("gpsimd", block.gpsimd)):
                ops = p.ops[eng_name]
                def run(e, ops=ops):
                    for f in ops:
                        f(e)
                deco(run)

    return nc


def _threefry2x32(k0, k1, x0, x1):
    x0 = x0.astype(np.uint32).copy()
    x1 = x1.astype(np.uint32).copy()
    ks0, ks1 = np.uint32(k0), np.uint32(k1)
    ks2 = np.uint32(ks0 ^ ks1 ^ np.uint32(0x1BD11BDA))
    rots = [(13, 15, 26, 6), (17, 29, 16, 24)]
    with np.errstate(over="ignore"):
        x0 = (x0 + ks0).astype(np.uint32)
        x1 = (x1 + ks1).astype(np.uint32)
        ks = [ks1, ks2, ks0]
        for r in range(5):
            for d in rots[r % 2]:
                x0 = (x0 + x1).astype(np.uint32)
                x1 = ((x1 << np.uint32(d)) | (x1 >> np.uint32(32 - d))).astype(np.uint32)
                x1 = (x1 ^ x0).astype(np.uint32)
            x0 = (x0 + ks[r % 3]).astype(np.uint32)
            x1 = (x1 + ks[(r + 1) % 3] + np.uint32(r + 1)).astype(np.uint32)
    return x0, x1


def _host_constants():
    n = 4 * 4096 * 1024
    i = np.arange(n, dtype=np.uint64)
    b1, b2 = _threefry2x32(0, 42, (i >> np.uint64(32)).astype(np.uint32),
                           (i & np.uint64(0xFFFFFFFF)).astype(np.uint32))
    bits = (b1 ^ b2).astype(np.uint32)
    u = ((bits >> np.uint32(9)) | np.uint32(0x3F800000)).view(np.float32)         - np.float32(1.0)
    keep = (u < np.float32(0.8)).reshape(4, 4096, 1024)
    mask = np.where(keep, np.float32(0.625), np.float32(0.0))
    ebc = np.zeros((16, 1024), np.float32)
    for h in range(16):
        for k in range(8):
            if h // 2 == k:
                ebc[h, 128 * k + 64 * (h % 2):128 * k + 64 * (h % 2) + 64] = 1.0
    return mask, ebc


def kernel(L0_attn_x, L0_attn_Wq, L0_attn_bq, L0_attn_Wk, L0_attn_bk,
           L0_attn_Wv, L0_attn_bv, L0_attn_Wo, L0_attn_bo,
           L0_ffn_W1, L0_ffn_b1, L0_ffn_W2, L0_ffn_b2):
    x = np.asarray(L0_attn_x, np.float32)
    f8 = ml_dtypes.float8_e5m2

    mask, ebc = _host_constants()

    w1t = np.ascontiguousarray(np.asarray(L0_ffn_W1, np.float32).T)
    w2t = np.ascontiguousarray(np.asarray(L0_ffn_W2, np.float32).T)
    wqt = np.ascontiguousarray(np.asarray(L0_attn_Wq, np.float32).T)
    wkt = np.ascontiguousarray(np.asarray(L0_attn_Wk, np.float32).T)
    wvt = np.ascontiguousarray(np.asarray(L0_attn_Wv, np.float32).T)
    wot = np.ascontiguousarray(np.asarray(L0_attn_Wo, np.float32).T)
    b1_pp = np.ascontiguousarray(np.asarray(L0_ffn_b1, np.float32).reshape(HC, 128).T)
    b2_pp = np.ascontiguousarray(np.asarray(L0_ffn_b2, np.float32).reshape(DC, 128).T)
    bq_pp = np.ascontiguousarray(np.asarray(L0_attn_bq, np.float32).reshape(DC, 128).T)
    bo_pp = np.ascontiguousarray(np.asarray(L0_attn_bo, np.float32).reshape(DC, 128).T)
    rb = np.concatenate([np.asarray(L0_attn_bk, np.float32),
                         np.asarray(L0_attn_bv, np.float32)]).reshape(1, 2 * D)
    rb = np.ascontiguousarray(rb)

    nc = build_nc()
    in_maps = []
    for c in range(8):
        b, h = c // 2, c % 2
        xc = np.ascontiguousarray(x[b, T * h:T * (h + 1), :].T)
        mc = np.ascontiguousarray(mask[b, T * h:T * (h + 1), :].T).astype(f8)
        in_maps.append({
            "xT": xc, "w1t": w1t, "w2t": w2t, "wqt": wqt, "wkt": wkt,
            "wvt": wvt, "wot": wot, "b1_pp": b1_pp, "b2_pp": b2_pp,
            "bq_d": bq_pp, "bo_d": bo_pp, "rb_d": rb, "ebc_d": ebc,
            "maskT": mc,
        })
    res = run_bass_kernel_spmd(nc, in_maps, core_ids=list(range(8)))
    out = np.empty((4, 4096, 1024), np.float32)
    for c in range(8):
        b, h = c // 2, c % 2
        out[b, T * h:T * (h + 1), :] = np.asarray(res.results[c]["outT"]).T
    return out


# version 06
# speedup vs baseline: 1.0637x; 1.0637x over previous
import contextlib

import numpy as np
import ml_dtypes

import concourse.bass as bass
import concourse.mybir as mybir
from concourse.bass_utils import run_bass_kernel_spmd

F32 = mybir.dt.float32
BF16 = mybir.dt.bfloat16
F8 = mybir.dt.float8e5
AF = mybir.ActivationFunctionType
ALU = mybir.AluOpType

D = 1024
DC = 8
HID = 4096
HC = 32
T = 2048
ST = 512
NST = T // ST
QT = 256
NQT = T // QT
NH = 16
HD = 64
EPS_LN = 1e-5
EPS_ATTN = 1e-6
INV_D = 1.0 / D
DRPM = mybir.MatmulPerfMode.DoubleRow
DR_MM1 = False
DR_MM2 = True

ENGS = ("sync", "vector", "scalar", "tensor", "gpsimd")


class Prog:

    def __init__(self):
        self.ops = {e: [] for e in ENGS}
        self.sems = {}
        self.cnt = {}

    def add_sem(self, name, handle):
        self.sems[name] = handle
        self.cnt[name] = 0

    def wait(self, eng, sname, val):
        if val <= 0:
            return
        h = self.sems[sname]
        self.ops[eng].append(lambda e, h=h, v=val: e.wait_ge(h, v))

    def wait_ev(self, eng, ev):
        if ev is not None:
            self.wait(eng, ev[0], ev[1])

    def do(self, eng, fn, inc=None, k=1):
        if inc is None:
            self.ops[eng].append(fn)
            return None
        h = self.sems[inc]
        self.cnt[inc] += k
        def g(e, fn=fn, h=h, k=k):
            fn(e).then_inc(h, k)
        self.ops[eng].append(g)
        return (inc, self.cnt[inc])

    def now(self, sname):
        return (sname, self.cnt[sname])


def build_nc(phases=("ffn1", "attn", "ffn2")):
    nc = bass.Bass("TRN2", target_bir_lowering=False, debug=False,
                   detect_race_conditions=False)

    xT = nc.dram_tensor("xT", [D, T], F32, kind="ExternalInput").ap()
    w1t = nc.dram_tensor("w1t", [D, HID], F32, kind="ExternalInput").ap()
    w2t = nc.dram_tensor("w2t", [HID, D], F32, kind="ExternalInput").ap()
    wqt = nc.dram_tensor("wqt", [D, D], F32, kind="ExternalInput").ap()
    wkt = nc.dram_tensor("wkt", [D, D], F32, kind="ExternalInput").ap()
    wvt = nc.dram_tensor("wvt", [D, D], F32, kind="ExternalInput").ap()
    wot = nc.dram_tensor("wot", [D, D], F32, kind="ExternalInput").ap()
    b1_pp = nc.dram_tensor("b1_pp", [128, HC], F32, kind="ExternalInput").ap()
    b2_pp = nc.dram_tensor("b2_pp", [128, DC], F32, kind="ExternalInput").ap()
    bq_d = nc.dram_tensor("bq_d", [128, DC], F32, kind="ExternalInput").ap()
    bo_d = nc.dram_tensor("bo_d", [128, DC], F32, kind="ExternalInput").ap()
    rb_d = nc.dram_tensor("rb_d", [1, 2 * D], F32, kind="ExternalInput").ap()
    ebc_d = nc.dram_tensor("ebc_d", [16, D], F32, kind="ExternalInput").ap()
    maskT = nc.dram_tensor("maskT", [D, T], F8, kind="ExternalInput").ap()
    outT = nc.dram_tensor("outT", [D, T], F32, kind="ExternalOutput").ap()
    cc_in = nc.dram_tensor("cc_in", [128, 520], F32)
    cc_out = nc.dram_tensor("cc_out", [128, 520], F32)

    p = Prog()
    es0 = contextlib.ExitStack()
    with es0:
        def sbuf(name, shape, dt, stack=es0):
            return stack.enter_context(nc.sbuf_tensor(name, shape, dt))

        def sem(name):
            h = es0.enter_context(nc.semaphore(name))
            p.add_sem(name, h)
            return h

        sX = sbuf("sX", [128, DC * T], F32)
        sW1 = sbuf("sW1", [128, DC * HID], F8)
        wslotA = sbuf("wslotA", [128, DC * D], BF16)
        b1q8 = sbuf("b1q8", [128, HC], F8)
        b2q8 = sbuf("b2q8", [128, DC], F8)
        b1qf = sbuf("b1qf", [128, HC], F32)
        b2qf = sbuf("b2qf", [128, DC], F32)
        bq_pp = sbuf("bq_pp", [128, DC], F32)
        bo_pp = sbuf("bo_pp", [128, DC], F32)
        ebc = sbuf("ebc", [16, D], F32)
        onec_bf = sbuf("onec_bf", [128, 1], BF16)
        oner_f = sbuf("oner_f", [1, 128], F32)
        oner_bf = sbuf("oner_bf", [1, 128], BF16)
        skv = sbuf("skv", [128, 520], F32)
        statrow = sbuf("statrow", [1, 4 * 512], F32)

        for s in ("din", "dw", "v", "s", "t", "g", "do", "cc"):
            sem(s)

        pbank = [es0.enter_context(nc.psum_tensor(f"pb{i}", [128, 512], F32))
                 for i in range(8)]
        bank_free = [None] * 8

        x_ready = []
        for s in range(NST):
            for k in range(DC):
                p.do("sync", lambda e, k=k, s=s: e.dma_start(
                    out=sX[:, T * k + ST * s: T * k + ST * (s + 1)],
                    in_=xT[128 * k:128 * (k + 1), ST * s:ST * (s + 1)]),
                    inc="din", k=16)
            x_ready.append(p.cnt["din"])
            p.wait("sync", "din", p.cnt["din"])

        p.do("sync", lambda e: e.dma_start(out=bq_pp[:], in_=bq_d),
             inc="din", k=16)
        p.do("sync", lambda e: e.dma_start(out=bo_pp[:], in_=bo_d),
             inc="din", k=16)
        p.wait("sync", "din", p.cnt["din"])
        ev_bq = ev_bo = p.now("din")

        for k in range(DC):
            p.do("gpsimd", lambda e, k=k: e.dma_start(
                out=sW1[:, HID * k:HID * (k + 1)],
                in_=w1t[128 * k:128 * (k + 1), :]), inc="dw", k=16)
        p.wait("gpsimd", "dw", p.cnt["dw"])
        ev_w1 = p.now("dw")
        w2_in = w2t.rearrange("(c p) f -> p c f", p=128)
        p.do("gpsimd", lambda e: e.dma_start(out=b1q8[:], in_=b1_pp),
             inc="dw", k=16)
        p.do("gpsimd", lambda e: e.dma_start(out=b2q8[:], in_=b2_pp),
             inc="dw", k=16)
        p.do("gpsimd", lambda e: e.dma_start(out=ebc[:], in_=ebc_d),
             inc="dw", k=16)
        p.wait("gpsimd", "dw", p.cnt["dw"])
        ev_b1q = ev_b2q = ev_rb = ev_ebc = p.now("dw")

        p.wait_ev("vector", ev_b1q)
        p.wait_ev("vector", ev_b2q)
        p.do("vector", lambda e: e.memset(onec_bf[:], 1.0))
        p.do("vector", lambda e: e.memset(oner_f[:], 1.0))
        p.do("vector", lambda e: e.memset(oner_bf[:], 1.0))
        p.do("vector", lambda e: e.tensor_copy(out=b1qf[:], in_=b1q8[:]))
        ev_binit = p.do("vector", lambda e: e.tensor_copy(
            out=b2qf[:], in_=b2q8[:]), inc="v")

        ln_state = {"statrow_free": None}

        def emit_ln(tok0, ntok, xbfc, sqc, lnaux, tmpf, out_tile, out_col0,
                    x_wait_din=None, slotw=512):
            m_row = statrow[0:1, 0:ntok]
            veA = statrow[0:1, 512:512 + ntok]
            Brow = statrow[0:1, 1024:1024 + ntok]
            msq = Brow
            nrow = statrow[0:1, 1536:1536 + ntok]
            n2row = Brow

            for b in (4, 5):
                p.wait_ev("tensor", bank_free[b])
            xlc, s32, sqlc = lnaux
            ev_mm_list = [None] * DC
            for k in range(DC):
                sl = k % 2
                if x_wait_din is not None and k == 0:
                    p.wait("vector", "din", x_wait_din)
                if k >= 2:
                    p.wait_ev("vector", ev_mm_list[k - 2])
                xh = xbfc[:, slotw * sl:slotw * sl + ntok]
                xl = xlc[:, slotw * sl:slotw * sl + ntok]
                s3 = s32[:, 0:ntok]
                sqh = sqc[:, slotw * sl:slotw * sl + ntok]
                sql = sqlc[:, slotw * sl:slotw * sl + ntok]
                xs = sX[:, T * k + tok0:T * k + tok0 + ntok]
                p.do("vector", lambda e, xh=xh, xs=xs: e.tensor_copy(
                    out=xh, in_=xs))
                p.do("vector", lambda e, xl=xl, xs=xs, xh=xh: e.tensor_sub(
                    xl, xs, xh))
                p.do("vector", lambda e, s3=s3, xs=xs: e.tensor_mul(
                    s3, xs, xs))
                p.do("vector", lambda e, sqh=sqh, s3=s3: e.tensor_copy(
                    out=sqh, in_=s3))
                ev_cast = p.do("vector", lambda e, sql=sql, s3=s3, sqh=sqh:
                    e.tensor_sub(sql, s3, sqh), inc="v")
                p.wait_ev("tensor", ev_cast)
                p.do("tensor", lambda e, k=k, xh=xh: e.matmul(
                    pbank[4][0:1, 0:ntok], onec_bf[:], xh,
                    start=(k == 0), stop=False))
                p.do("tensor", lambda e, k=k, xl=xl: e.matmul(
                    pbank[4][0:1, 0:ntok], onec_bf[:], xl,
                    start=False, stop=(k == DC - 1)))
                p.do("tensor", lambda e, k=k, sqh=sqh: e.matmul(
                    pbank[5][0:1, 0:ntok], onec_bf[:], sqh,
                    start=(k == 0), stop=False))
                ev_mm_list[k] = p.do("tensor", lambda e, k=k, sql=sql: e.matmul(
                    pbank[5][0:1, 0:ntok], onec_bf[:], sql,
                    start=False, stop=(k == DC - 1)), inc="t")
            ev_mm = ev_mm_list[DC - 1]
            p.wait_ev("vector", ev_mm)
            p.wait_ev("vector", ln_state["statrow_free"])
            p.do("vector", lambda e: e.tensor_single_scalar(
                out=m_row, in_=pbank[4][0:1, 0:ntok], scalar=INV_D,
                op=ALU.mult))
            p.do("vector", lambda e: e.tensor_scalar(
                out=veA, in0=pbank[5][0:1, 0:ntok], scalar1=INV_D,
                scalar2=EPS_LN, op0=ALU.mult, op1=ALU.add))
            p.do("vector", lambda e: e.tensor_mul(msq, m_row, m_row))
            ev_ve = p.do("vector", lambda e: e.tensor_sub(veA, veA, msq),
                         inc="v")
            bank_free[4] = ev_ve
            bank_free[5] = ev_ve
            p.wait_ev("scalar", ev_ve)
            p.do("scalar", lambda e: e.activation(out=nrow, in_=veA, func=AF.Ln))
            ev_A = p.do("scalar", lambda e: e.activation(
                out=nrow, in_=nrow, func=AF.Exp, scale=-0.5), inc="s")
            p.wait_ev("vector", ev_A)
            p.do("vector", lambda e: e.tensor_mul(n2row, nrow, nrow))
            p.do("vector", lambda e: e.tensor_mul(n2row, n2row, veA))
            p.do("vector", lambda e: e.tensor_scalar(
                out=n2row, in0=n2row, scalar1=-0.5, scalar2=1.5,
                op0=ALU.mult, op1=ALU.add))
            ev_A = p.do("vector", lambda e: e.tensor_mul(veA, nrow, n2row),
                        inc="v")
            ev_B = p.do("vector", lambda e: e.scalar_tensor_tensor(
                out=Brow, in0=veA, scalar=-1.0, in1=m_row,
                op0=ALU.mult, op1=ALU.mult), inc="v")
            p.wait_ev("tensor", bank_free[6])
            p.wait_ev("tensor", bank_free[7])
            p.wait_ev("tensor", ev_A)
            p.wait_ev("tensor", ev_B)
            p.do("tensor", lambda e: e.matmul(
                pbank[6][:, 0:ntok], oner_f[:], veA, start=True, stop=True))
            ev_bc = p.do("tensor", lambda e: e.matmul(
                pbank[7][:, 0:ntok], oner_f[:], Brow, start=True, stop=True),
                inc="t")
            p.wait_ev("vector", ev_bc)
            ev_ap = None
            for k in range(DC):
                p.do("vector", lambda e, k=k: e.tensor_mul(
                    tmpf[:, 0:ntok],
                    sX[:, T * k + tok0:T * k + tok0 + ntok],
                    pbank[6][:, 0:ntok]))
                ev_ap = p.do("vector", lambda e, k=k: e.tensor_add(
                    out_tile[:, out_col0 + ntok * k:out_col0 + ntok * (k + 1)],
                    tmpf[:, 0:ntok],
                    pbank[7][:, 0:ntok]), inc="v")
            bank_free[6] = ev_ap
            bank_free[7] = ev_ap
            ln_state["statrow_free"] = ev_bc
            return ev_ap

        def emit_ffn(first):
            if not first:
                p.wait("sync", "v", p.cnt["v"])
                p.wait("sync", "t", p.cnt["t"])
            esf = contextlib.ExitStack()
            with esf:
                sW2 = sbuf("sW2F" + str(first), [128, HC * D], F8, esf)
                mask2 = sbuf("maskF" + str(first), [128, DC * ST], F8, esf)
                xnq = sbuf("xnqF" + str(first), [128, 2 * DC * ST], F8, esf)
                hbf = sbuf("hbfF" + str(first), [128, 2 * 512], BF16, esf)
                hq8 = sbuf("hq8F" + str(first), [128, HC * ST], F8, esf)
                y2b = sbuf("y2bF" + str(first), [128, 2 * 512], BF16, esf)
                tmpf = sbuf("tmpfF" + str(first), [128, 512], F32, esf)
                xbfc = sbuf("xbfF" + str(first), [128, 2 * 512], BF16, esf)
                sqc = sbuf("sqcF" + str(first), [128, 2 * 512], BF16, esf)
                lnaux = (sbuf("xlF" + str(first), [128, 2 * 512], BF16, esf),
                         sbuf("s32F" + str(first), [128, 512], F32, esf),
                         sbuf("sqlF" + str(first), [128, 2 * 512], BF16, esf))

                if not first:
                    p.wait("gpsimd", "v", p.cnt["v"])
                    p.wait("gpsimd", "t", p.cnt["t"])
                p.do("gpsimd", lambda e: e.dma_start(
                    out=sW2[:].rearrange("p (c f) -> p c f", c=HC), in_=w2_in),
                    inc="dw", k=16)
                p.wait("gpsimd", "dw", p.cnt["dw"])
                ev_w2 = p.now("dw")

                mask_free = [None]
                xnq_free = [None, None]
                hq8_free = None
                hbf_free = [None, None]
                y2_free = [None, None]

                ev_res = None
                for s in range(NST):
                    msl = 0
                    p.wait_ev("sync", mask_free[msl])
                    p.do("sync", lambda e, s=s, msl=msl: e.dma_start(
                        out=mask2[:, DC * ST * msl:DC * ST * (msl + 1)]
                            .rearrange("p (c f) -> p c f", c=DC),
                        in_=maskT.rearrange("(c p) f -> p c f", p=128)
                            [:, :, ST * s:ST * (s + 1)]), inc="din", k=16)
                    p.wait("sync", "din", p.cnt["din"])
                    ev_mask = p.now("din")

                    p.wait_ev("vector", xnq_free[msl])
                    ev_xnq = emit_ln(
                        ST * s, ST, xbfc, sqc, lnaux, tmpf,
                        xnq, DC * ST * msl,
                        x_wait_din=(x_ready[s] if first else None))

                    p.wait_ev("tensor", ev_w1)
                    sW1_3 = sW1[:].rearrange("p (c f) -> p c f", c=DC)
                    xnq_3 = xnq[:, DC * ST * msl:DC * ST * (msl + 1)].rearrange(
                        "p (c f) -> p c f", c=DC)
                    ev_cast32 = None
                    for m in range(HC):
                        b = m % 2
                        p.wait_ev("tensor", bank_free[b])
                        p.wait_ev("tensor", ev_xnq)
                        if DR_MM1:
                            for k in range(DC // 2):
                                evg = p.do("tensor", lambda e, m=m, k=k, b=b: e.matmul(
                                    pbank[b][:],
                                    sW1_3[:, 2 * k:2 * k + 2, 128 * m:128 * (m + 1)],
                                    xnq_3[:, 2 * k:2 * k + 2, :],
                                    start=(k == 0), stop=(k == DC // 2 - 1),
                                    perf_mode=DRPM),
                                    inc=("t" if k == DC // 2 - 1 else None))
                        else:
                            for k in range(DC):
                                evg = p.do("tensor", lambda e, m=m, k=k, b=b, msl=msl: e.matmul(
                                    pbank[b][:],
                                    sW1[:, HID * k + 128 * m:HID * k + 128 * (m + 1)],
                                    xnq[:, DC * ST * msl + ST * k:DC * ST * msl + ST * (k + 1)],
                                    start=(k == 0), stop=(k == DC - 1)),
                                    inc=("t" if k == DC - 1 else None))
                        hsl = m % 2
                        p.wait_ev("scalar", evg)
                        p.wait_ev("scalar", hbf_free[hsl])
                        if first and m == 0 and s == 0:
                            p.wait_ev("scalar", ev_binit)
                        ev_ev = p.do("scalar", lambda e, m=m, b=b, hsl=hsl: e.activation(
                            out=hbf[:, 512 * hsl:512 * (hsl + 1)],
                            in_=pbank[b][:], func=AF.Relu,
                            bias=b1qf[:, m:m + 1]), inc="s")
                        bank_free[b] = ev_ev
                        p.wait_ev("vector", ev_ev)
                        if hq8_free is not None and m == 0:
                            p.wait_ev("vector", hq8_free)
                        ev_cast32 = p.do("vector", lambda e, m=m, hsl=hsl: e.tensor_copy(
                            out=hq8[:, ST * m:ST * (m + 1)],
                            in_=hbf[:, 512 * hsl:512 * (hsl + 1)]), inc="v")
                        hbf_free[hsl] = ev_cast32

                    p.wait_ev("tensor", ev_w2)
                    sW2_3 = sW2[:].rearrange("p (c f) -> p c f", c=HC)
                    hq8_3 = hq8[:].rearrange("p (c f) -> p c f", c=HC)
                    for n in range(DC):
                        b = 2 + n % 2
                        p.wait_ev("tensor", bank_free[b])
                        p.wait_ev("tensor", ev_cast32)
                        if DR_MM2 and not first:
                            for j in range(HC // 2):
                                evg = p.do("tensor", lambda e, n=n, j=j, b=b: e.matmul(
                                    pbank[b][:],
                                    sW2_3[:, 2 * j:2 * j + 2, 128 * n:128 * (n + 1)],
                                    hq8_3[:, 2 * j:2 * j + 2, :],
                                    start=(j == 0), stop=(j == HC // 2 - 1),
                                    perf_mode=DRPM),
                                    inc=("t" if j == HC // 2 - 1 else None))
                        else:
                            for j in range(HC):
                                evg = p.do("tensor", lambda e, n=n, j=j, b=b: e.matmul(
                                    pbank[b][:],
                                    sW2[:, D * j + 128 * n:D * j + 128 * (n + 1)],
                                    hq8[:, ST * j:ST * (j + 1)],
                                    start=(j == 0), stop=(j == HC - 1)),
                                    inc=("t" if j == HC - 1 else None))
                        if n == DC - 1:
                            hq8_free = evg
                        ysl = n % 2
                        p.wait_ev("scalar", evg)
                        p.wait_ev("scalar", y2_free[ysl])
                        ev_y = p.do("scalar", lambda e, n=n, b=b, ysl=ysl: e.activation(
                            out=y2b[:, 512 * ysl:512 * (ysl + 1)],
                            in_=pbank[b][:], func=AF.Identity,
                            bias=b2qf[:, n:n + 1]), inc="s")
                        bank_free[b] = ev_y
                        p.wait_ev("vector", ev_y)
                        p.wait_ev("vector", ev_mask)
                        p.do("vector", lambda e, n=n, ysl=ysl, msl=msl: e.tensor_mul(
                            tmpf[:, 0:512],
                            y2b[:, 512 * ysl:512 * (ysl + 1)],
                            mask2[:, DC * ST * msl + ST * n:DC * ST * msl + ST * (n + 1)]))
                        ev_res = p.do("vector", lambda e, n=n, s=s: e.tensor_add(
                            sX[:, T * n + ST * s:T * n + ST * (s + 1)],
                            sX[:, T * n + ST * s:T * n + ST * (s + 1)],
                            tmpf[:, 0:512]), inc="v")
                        y2_free[ysl] = ev_res
                        if n >= DC - 2:
                            mask_free[msl] = ev_res
                        xnq_free[msl] = evg
            return ev_res

        ev_ffn1 = emit_ffn(first=True) if "ffn1" in phases else None

        wk_in = wkt.rearrange("(c p) f -> p c f", p=128)
        p.do("gpsimd", lambda e: e.dma_start(
            out=wslotA[:].rearrange("p (c f) -> p c f", c=DC), in_=wk_in),
            inc="dw", k=16)
        p.wait("gpsimd", "dw", p.cnt["dw"])
        ev_wk = p.now("dw")

        if "attn" in phases:
            es_a = contextlib.ExitStack()
            with es_a:
                wslotB = sbuf("wslotB", [128, DC * D], BF16, es_a)

                es_kv = contextlib.ExitStack()
                with es_kv:
                    bcK = sbuf("bcK", [128, D], F32, es_kv)
                    bcV = sbuf("bcV", [128, D], F32, es_kv)
                    es_rb = contextlib.ExitStack()
                    rbf32 = sbuf("rbf32", [1, 2 * D], F32, es_rb)

                    p.wait_ev("gpsimd", ev_ffn1)
                    p.wait("gpsimd", "t", p.cnt["t"])
                    wv_in = wvt.rearrange("(c p) f -> p c f", p=128)
                    p.do("gpsimd", lambda e: e.dma_start(out=rbf32[:], in_=rb_d),
                         inc="dw", k=16)
                    p.do("gpsimd", lambda e: e.dma_start(
                        out=wslotB[:].rearrange("p (c f) -> p c f", c=DC),
                        in_=wv_in), inc="dw", k=16)
                    p.wait("gpsimd", "dw", p.cnt["dw"])
                    ev_wv = p.now("dw")
                    p.wait_ev("tensor", ev_wv)
                    for b in (0, 1, 2, 3):
                        p.wait_ev("tensor", bank_free[b])
                    for half in range(2):
                        p.do("tensor", lambda e, half=half: e.matmul(
                            pbank[half][:], oner_f[:],
                            rbf32[0:1, 512 * half:512 * (half + 1)],
                            start=True, stop=True))
                        ev_bc0 = p.do("tensor", lambda e, half=half: e.matmul(
                            pbank[2 + half][:], oner_f[:],
                            rbf32[0:1, D + 512 * half:D + 512 * (half + 1)],
                            start=True, stop=True), inc="t")
                    p.wait_ev("vector", ev_bc0)
                    for half in range(2):
                        p.do("vector", lambda e, half=half: e.tensor_copy(
                            out=bcK[:, 512 * half:512 * (half + 1)],
                            in_=pbank[half][:]))
                        ev_bcv = p.do("vector", lambda e, half=half: e.tensor_copy(
                            out=bcV[:, 512 * half:512 * (half + 1)],
                            in_=pbank[2 + half][:]), inc="v")
                    for b in (0, 1, 2, 3):
                        bank_free[b] = ev_bcv
                    es_rb.close()
                    xn2 = sbuf("xn2", [128, DC * ST], BF16, es_kv)
                    ktm = sbuf("ktm", [128, D], BF16, es_kv)
                    v65 = sbuf("v65", [128, NH * 65], BF16, es_kv)
                    xbfA = sbuf("xbfA", [128, 2 * 512], BF16, es_kv)
                    sqA = sbuf("sqA", [128, 2 * 512], BF16, es_kv)
                    lnauxA = (sbuf("xlA", [128, 2 * 512], BF16, es_kv),
                              sbuf("s32A", [128, 512], F32, es_kv),
                              sbuf("sqlA", [128, 2 * 512], BF16, es_kv))
                    tmpa = sbuf("tmpa", [128, 512], F32, es_kv)
                    tmpb = lnauxA[1]

                    p.do("vector", lambda e: e.memset(
                        v65[:].rearrange("p (h c) -> p h c", c=65)[:, :, 64:65], 1.0))

                    xn2_free = None
                    ktm_free = None
                    v65_free = None
                    ev_kvadd = None
                    for s in range(NST):
                        p.wait_ev("vector", xn2_free)
                        p.wait_ev("vector", ev_ffn1)
                        ev_xn2 = emit_ln(ST * s, ST, xbfA, sqA, lnauxA, tmpa, xn2, 0)
                        for t4 in range(4):
                            t_ = 4 * s + t4
                            p.wait_ev("tensor", ev_wk)
                            p.wait_ev("tensor", ev_xn2)
                            for half in range(2):
                                b = half
                                p.wait_ev("tensor", bank_free[b])
                                for k in range(DC):
                                    evg = p.do("tensor", lambda e, k=k, b=b, t4=t4, half=half: e.matmul(
                                        pbank[b][:],
                                        xn2[:, ST * k + 128 * t4:ST * k + 128 * (t4 + 1)],
                                        wslotA[:, D * k + 512 * half:D * k + 512 * (half + 1)],
                                        start=(k == 0), stop=(k == DC - 1)),
                                        inc=("t" if k == DC - 1 else None))
                                p.wait_ev("vector", evg)
                                p.wait_ev("vector", ktm_free)
                                p.do("vector", lambda e, b=b, half=half: e.tensor_add(
                                    tmpa[:], pbank[b][:],
                                    bcK[:, 512 * half:512 * (half + 1)]))
                                ev_mn = p.do("vector", lambda e: e.tensor_scalar_min(
                                    out=tmpb[:], in0=tmpa[:], scalar1=0.0), inc="v")
                                p.wait_ev("scalar", ev_mn)
                                ev_ex = p.do("scalar", lambda e: e.activation(
                                    out=tmpb[:], in_=tmpb[:], func=AF.Exp), inc="s")
                                p.wait_ev("vector", ev_ex)
                                ev_k = p.do("vector", lambda e, half=half: e.scalar_tensor_tensor(
                                    out=ktm[:, 512 * half:512 * (half + 1)],
                                    in0=tmpa[:], scalar=0.0, in1=tmpb[:],
                                    op0=ALU.max, op1=ALU.add), inc="v")
                                bank_free[b] = ev_k
                            p.wait_ev("tensor", ev_wv)
                            for half in range(2):
                                b = 2 + half
                                p.wait_ev("tensor", bank_free[b])
                                for k in range(DC):
                                    evg = p.do("tensor", lambda e, k=k, b=b, t4=t4, half=half: e.matmul(
                                        pbank[b][:],
                                        xn2[:, ST * k + 128 * t4:ST * k + 128 * (t4 + 1)],
                                        wslotB[:, D * k + 512 * half:D * k + 512 * (half + 1)],
                                        start=(k == 0), stop=(k == DC - 1)),
                                        inc=("t" if k == DC - 1 else None))
                                p.wait_ev("vector", evg)
                                p.wait_ev("vector", v65_free)
                                ev_v = p.do("vector", lambda e, b=b, half=half: e.tensor_add(
                                    v65[:].rearrange("p (h c) -> p h c", c=65)
                                        [:, 8 * half:8 * (half + 1), 0:64],
                                    pbank[b][:].rearrange("p (h c) -> p h c", c=64),
                                    bcV[:, 512 * half:512 * (half + 1)]
                                        .rearrange("p (h c) -> p h c", c=64)),
                                    inc="v")
                                bank_free[b] = ev_v
                            p.wait_ev("tensor", ev_k)
                            p.wait_ev("tensor", ev_v)
                            for q in range(4, 8):
                                p.wait_ev("tensor", bank_free[q])
                            evkv = None
                            for h in range(NH):
                                bkn = 4 + h // 4
                                r0 = 64 * (h % 2)
                                c0 = 65 * ((h % 4) // 2)
                                evkv = p.do("tensor", lambda e, h=h, bkn=bkn, r0=r0, c0=c0: e.matmul(
                                    pbank[bkn][r0:r0 + 64, c0:c0 + 65],
                                    ktm[:, HD * h:HD * (h + 1)],
                                    v65[:, 65 * h:65 * (h + 1)],
                                    start=True, stop=True,
                                    tile_position=(0, r0)),
                                    inc=("t" if h == NH - 1 else None))
                            ktm_free = evkv
                            v65_free = evkv
                            p.wait_ev("vector", evkv)
                            for q in range(4):
                                if t_ == 0:
                                    ev_kvadd = p.do("vector", lambda e, q=q: e.tensor_copy(
                                        out=skv[:, 130 * q:130 * (q + 1)],
                                        in_=pbank[4 + q][:, 0:130]), inc="v")
                                else:
                                    ev_kvadd = p.do("vector", lambda e, q=q: e.tensor_add(
                                        skv[:, 130 * q:130 * (q + 1)],
                                        skv[:, 130 * q:130 * (q + 1)],
                                        pbank[4 + q][:, 0:130]), inc="v")
                            for q in range(4, 8):
                                bank_free[q] = ev_kvadd
                            xn2_free = evkv if t4 == 3 else xn2_free

                    p.wait("gpsimd", "t", p.cnt["t"])
                    wq_in = wqt.rearrange("(c p) f -> p c f", p=128)
                    p.do("gpsimd", lambda e: e.dma_start(
                        out=wslotA[:].rearrange("p (c f) -> p c f", c=DC),
                        in_=wq_in), inc="dw", k=16)
                    p.wait("gpsimd", "dw", p.cnt["dw"])
                    ev_wq = p.now("dw")

                    p.wait_ev("gpsimd", ev_kvadd)
                    p.do("gpsimd", lambda e: e.dma_start(
                        out=cc_in[:, :], in_=skv[:]), inc="dw", k=16)
                    p.wait("gpsimd", "dw", p.cnt["dw"])
                    p.do("gpsimd", lambda e: e.collective_compute(
                        "AllReduce", ALU.add,
                        replica_groups=[[0, 1], [2, 3], [4, 5], [6, 7]],
                        ins=[cc_in.ap().opt()], outs=[cc_out.ap().opt()]),
                        inc="cc")
                    p.wait("gpsimd", "cc", 1)
                    p.do("gpsimd", lambda e: e.dma_start(
                        out=skv[:], in_=cc_out[:, :]), inc="dw", k=16)
                    p.wait("gpsimd", "dw", p.cnt["dw"])
                    ev_kvtot = p.now("dw")

                es_q = contextlib.ExitStack()
                with es_q:
                    wolo = sbuf("wolo", [128, DC * D], F8, es_q)
                    wostage = sbuf("wostage", [128, 512], F32, es_q)
                    xn2q = sbuf("xn2q", [128, DC * QT], BF16, es_q)
                    qphi = sbuf("qphi", [128, DC * QT], BF16, es_q)
                    outpre = sbuf("outpre", [128, DC * QT], BF16, es_q)
                    outpre_lo = sbuf("outpre_lo", [128, DC * QT], BF16, es_q)
                    outpre_h8 = sbuf("outpre_h8", [128, DC * QT], F8, es_q)
                    kvbd = sbuf("kvbd", [128, DC * 128], BF16, es_q)
                    zbd = sbuf("zbd", [128, DC * 16], BF16, es_q)
                    tmpn = sbuf("tmpn", [16, QT], F32, es_q)
                    xbfQ = sbuf("xbfQ", [128, 2 * QT], BF16, es_q)
                    sqQ = sbuf("sqQ", [128, 2 * QT], BF16, es_q)
                    lnauxQ = (sbuf("xlQ", [128, 2 * QT], BF16, es_q),
                              sbuf("s32Q", [128, QT], F32, es_q),
                              sbuf("sqlQ", [128, 2 * QT], BF16, es_q))
                    tmpaQ = sbuf("tmpaQ", [128, QT], F32, es_q)
                    tmpeQ = sbuf("tmpeQ", [128, QT], F32, es_q)
                    tmpoQ = sbuf("tmpoQ", [128, QT], F32, es_q)

                    p.wait_ev("vector", ev_kvtot)
                    p.do("vector", lambda e: e.memset(kvbd[:], 0.0))
                    p.do("vector", lambda e: e.memset(zbd[:], 0.0))
                    ev_bd = None
                    for h in range(NH):
                        r0 = 64 * (h % 2)
                        c0 = 130 * (h // 4) + 65 * ((h % 4) // 2)
                        blk = h // 2
                        p.do("vector", lambda e, h=h, r0=r0, c0=c0, blk=blk: e.tensor_copy(
                            out=kvbd[r0:r0 + 64,
                                     128 * blk + 64 * (h % 2):128 * blk + 64 * (h % 2) + 64],
                            in_=skv[r0:r0 + 64, c0:c0 + 64]))
                        ev_bd = p.do("vector", lambda e, h=h, r0=r0, c0=c0, blk=blk: e.tensor_copy(
                            out=zbd[r0:r0 + 64, 16 * blk + h:16 * blk + h + 1],
                            in_=skv[r0:r0 + 64, c0 + 64:c0 + 65]), inc="v")

                    ev_res_attn = None
                    qphi_free = None
                    outpre_free = None
                    for s in range(NQT):
                        ev_xq = emit_ln(QT * s, QT, xbfQ, sqQ, lnauxQ, tmpaQ,
                                        xn2q, 0, slotw=QT)
                        p.wait_ev("tensor", ev_wq)
                        ev_qphi = None
                        for n in range(DC):
                            b = 4 + n % 2
                            p.wait_ev("tensor", bank_free[b])
                            p.wait_ev("tensor", ev_xq)
                            for k in range(DC):
                                evg = p.do("tensor", lambda e, n=n, k=k, b=b: e.matmul(
                                    pbank[b][:, 0:QT],
                                    wslotA[:, D * k + 128 * n:D * k + 128 * (n + 1)],
                                    xn2q[:, QT * k:QT * (k + 1)],
                                    start=(k == 0), stop=(k == DC - 1)),
                                    inc=("t" if k == DC - 1 else None))
                            p.wait_ev("vector", evg)
                            p.wait_ev("vector", qphi_free)
                            ev_mn = p.do("vector", lambda e, n=n, b=b: e.tensor_scalar(
                                out=tmpaQ[:], in0=pbank[b][:, 0:QT],
                                scalar1=bq_pp[:, n:n + 1], scalar2=0.0,
                                op0=ALU.add, op1=ALU.min), inc="v")
                            p.wait_ev("scalar", ev_mn)
                            p.wait_ev("scalar", ev_bq)
                            ev_ex = p.do("scalar", lambda e: e.activation(
                                out=tmpeQ[:], in_=tmpaQ[:], func=AF.Exp), inc="s")
                            p.wait_ev("vector", ev_ex)
                            p.do("vector", lambda e, n=n, b=b: e.tensor_scalar(
                                out=tmpaQ[:], in0=pbank[b][:, 0:QT],
                                scalar1=bq_pp[:, n:n + 1], scalar2=0.0,
                                op0=ALU.add, op1=ALU.max))
                            ev_qphi = p.do("vector", lambda e, n=n: e.tensor_add(
                                qphi[:, QT * n:QT * (n + 1)], tmpaQ[:], tmpeQ[:]),
                                inc="v")
                            bank_free[b] = ev_qphi
                        p.wait_ev("tensor", bank_free[6])
                        p.wait_ev("tensor", ev_bd)
                        p.wait_ev("tensor", ev_qphi)
                        for k in range(DC):
                            evg = p.do("tensor", lambda e, k=k: e.matmul(
                                pbank[6][0:16, 0:QT], zbd[:, 16 * k:16 * (k + 1)],
                                qphi[:, QT * k:QT * (k + 1)],
                                start=(k == 0), stop=(k == DC - 1)),
                                inc=("t" if k == DC - 1 else None))
                        p.wait_ev("vector", evg)
                        p.do("vector", lambda e: e.tensor_single_scalar(
                            out=tmpn[:], in_=pbank[6][0:16, 0:QT],
                            scalar=EPS_ATTN, op=ALU.add))
                        ev_rec = p.do("vector", lambda e: e.reciprocal(
                            out=tmpn[:], in_=tmpn[:]), inc="v")
                        bank_free[6] = ev_rec
                        for k in range(DC):
                            b = 4 + k % 2
                            br = 6 + k % 2
                            p.wait_ev("tensor", bank_free[b])
                            p.wait_ev("tensor", bank_free[br])
                            p.wait_ev("tensor", ev_rec)
                            evg = p.do("tensor", lambda e, k=k, b=b: e.matmul(
                                pbank[b][:, 0:QT], kvbd[:, 128 * k:128 * (k + 1)],
                                qphi[:, QT * k:QT * (k + 1)],
                                start=True, stop=True))
                            evg = p.do("tensor", lambda e, k=k, br=br: e.matmul(
                                pbank[br][:, 0:QT], ebc[:, 128 * k:128 * (k + 1)],
                                tmpn[:], start=True, stop=True), inc="t")
                            p.wait_ev("vector", evg)
                            p.wait_ev("vector", outpre_free)
                            p.do("vector", lambda e, k=k, b=b: e.tensor_copy(
                                out=tmpoQ[:], in_=pbank[b][:, 0:QT]))
                            p.do("vector", lambda e, k=k, br=br: e.tensor_mul(
                                tmpoQ[:], tmpoQ[:], pbank[br][:, 0:QT]))
                            p.do("vector", lambda e, k=k: e.tensor_copy(
                                out=outpre[:, QT * k:QT * (k + 1)], in_=tmpoQ[:]))
                            p.do("vector", lambda e, k=k: e.tensor_sub(
                                outpre_lo[:, QT * k:QT * (k + 1)], tmpoQ[:],
                                outpre[:, QT * k:QT * (k + 1)]))
                            ev_op = p.do("vector", lambda e, k=k: e.tensor_copy(
                                out=outpre_h8[:, QT * k:QT * (k + 1)],
                                in_=outpre[:, QT * k:QT * (k + 1)]), inc="v")
                            bank_free[b] = ev_op
                            bank_free[br] = ev_op
                        qphi_free = ev_op
                        if s == 0:
                            p.wait("gpsimd", "v", p.cnt["v"])
                            wo_in = wot.rearrange("(c p) f -> p c f", p=128)
                            p.do("gpsimd", lambda e: e.dma_start(
                                out=wslotB[:].rearrange("p (c f) -> p c f", c=DC),
                                in_=wo_in), inc="dw", k=16)
                            p.wait("gpsimd", "dw", p.cnt["dw"])
                            ev_wo = p.now("dw")
                            ev_wl = None
                            for k in range(DC):
                                for hf in range(2):
                                    p.wait_ev("gpsimd", ev_wl)
                                    evst = p.do("gpsimd", lambda e, k=k, hf=hf: e.dma_start(
                                        out=wostage[:],
                                        in_=wot[128 * k:128 * (k + 1),
                                                512 * hf:512 * (hf + 1)]),
                                        inc="dw", k=16)
                                    p.wait("gpsimd", "dw", p.cnt["dw"])
                                    p.wait_ev("vector", evst)
                                    ev_wl = p.do("vector", lambda e, k=k, hf=hf: e.tensor_sub(
                                        wolo[:, D * k + 512 * hf:D * k + 512 * (hf + 1)],
                                        wostage[:],
                                        wslotB[:, D * k + 512 * hf:D * k + 512 * (hf + 1)]),
                                        inc="v")
                            ev_wo_lo = ev_wl
                        p.wait_ev("tensor", ev_wo)
                        p.wait_ev("tensor", ev_wo_lo)
                        for n in range(DC):
                            b = n % 2
                            p.wait_ev("tensor", bank_free[b])
                            p.wait_ev("tensor", ev_op)
                            for k in range(DC):
                                p.do("tensor", lambda e, n=n, k=k, b=b: e.matmul(
                                    pbank[b][:, 0:QT],
                                    wslotB[:, D * k + 128 * n:D * k + 128 * (n + 1)],
                                    outpre[:, QT * k:QT * (k + 1)],
                                    start=(k == 0), stop=False))
                                p.do("tensor", lambda e, n=n, k=k, b=b: e.matmul(
                                    pbank[b][:, 0:QT],
                                    wslotB[:, D * k + 128 * n:D * k + 128 * (n + 1)],
                                    outpre_lo[:, QT * k:QT * (k + 1)],
                                    start=False, stop=False))
                                evg = p.do("tensor", lambda e, n=n, k=k, b=b: e.matmul(
                                    pbank[b][:, 0:QT],
                                    wolo[:, D * k + 128 * n:D * k + 128 * (n + 1)],
                                    outpre_h8[:, QT * k:QT * (k + 1)],
                                    start=False, stop=(k == DC - 1)),
                                    inc=("t" if k == DC - 1 else None))
                            p.wait_ev("vector", evg)
                            p.wait_ev("vector", ev_bo)
                            ev_res_attn = p.do("vector", lambda e, n=n, s=s, b=b: e.scalar_tensor_tensor(
                                out=sX[:, T * n + QT * s:T * n + QT * (s + 1)],
                                in0=pbank[b][:, 0:QT], scalar=bo_pp[:, n:n + 1],
                                in1=sX[:, T * n + QT * s:T * n + QT * (s + 1)],
                                op0=ALU.add, op1=ALU.add), inc="v")
                            bank_free[b] = ev_res_attn
                        outpre_free = evg

        ev_ffn2 = emit_ffn(first=False) if "ffn2" in phases else None

        p.wait_ev("sync", ev_ffn2)
        p.wait("sync", "v", p.cnt["v"])
        ndo = 0
        for k in range(DC):
            p.do("sync", lambda e, k=k: e.dma_start(
                out=outT[128 * k:128 * (k + 1), :],
                in_=sX[:, T * k:T * (k + 1)]), inc="do", k=16)
            ndo += 16
        p.wait("sync", "do", ndo)

        with nc.Block() as block:
            for eng_name, deco in (("sync", block.sync), ("vector", block.vector),
                                   ("scalar", block.scalar), ("tensor", block.tensor),
                                   ("gpsimd", block.gpsimd)):
                ops = p.ops[eng_name]
                def run(e, ops=ops):
                    for f in ops:
                        f(e)
                deco(run)

    return nc


def _threefry2x32(k0, k1, x0, x1):
    x0 = x0.astype(np.uint32).copy()
    x1 = x1.astype(np.uint32).copy()
    ks0, ks1 = np.uint32(k0), np.uint32(k1)
    ks2 = np.uint32(ks0 ^ ks1 ^ np.uint32(0x1BD11BDA))
    rots = [(13, 15, 26, 6), (17, 29, 16, 24)]
    with np.errstate(over="ignore"):
        x0 = (x0 + ks0).astype(np.uint32)
        x1 = (x1 + ks1).astype(np.uint32)
        ks = [ks1, ks2, ks0]
        for r in range(5):
            for d in rots[r % 2]:
                x0 = (x0 + x1).astype(np.uint32)
                x1 = ((x1 << np.uint32(d)) | (x1 >> np.uint32(32 - d))).astype(np.uint32)
                x1 = (x1 ^ x0).astype(np.uint32)
            x0 = (x0 + ks[r % 3]).astype(np.uint32)
            x1 = (x1 + ks[(r + 1) % 3] + np.uint32(r + 1)).astype(np.uint32)
    return x0, x1


def _host_constants():
    n = 4 * 4096 * 1024
    i = np.arange(n, dtype=np.uint64)
    b1, b2 = _threefry2x32(0, 42, (i >> np.uint64(32)).astype(np.uint32),
                           (i & np.uint64(0xFFFFFFFF)).astype(np.uint32))
    bits = (b1 ^ b2).astype(np.uint32)
    u = ((bits >> np.uint32(9)) | np.uint32(0x3F800000)).view(np.float32)         - np.float32(1.0)
    keep = (u < np.float32(0.8)).reshape(4, 4096, 1024)
    mask = np.where(keep, np.float32(0.625), np.float32(0.0))
    ebc = np.zeros((16, 1024), np.float32)
    for h in range(16):
        for k in range(8):
            if h // 2 == k:
                ebc[h, 128 * k + 64 * (h % 2):128 * k + 64 * (h % 2) + 64] = 1.0
    return mask, ebc


def kernel(L0_attn_x, L0_attn_Wq, L0_attn_bq, L0_attn_Wk, L0_attn_bk,
           L0_attn_Wv, L0_attn_bv, L0_attn_Wo, L0_attn_bo,
           L0_ffn_W1, L0_ffn_b1, L0_ffn_W2, L0_ffn_b2):
    x = np.asarray(L0_attn_x, np.float32)
    f8 = ml_dtypes.float8_e5m2

    mask, ebc = _host_constants()

    w1t = np.ascontiguousarray(np.asarray(L0_ffn_W1, np.float32).T)
    w2t = np.ascontiguousarray(np.asarray(L0_ffn_W2, np.float32).T)
    wqt = np.ascontiguousarray(np.asarray(L0_attn_Wq, np.float32).T)
    wkt = np.ascontiguousarray(np.asarray(L0_attn_Wk, np.float32).T)
    wvt = np.ascontiguousarray(np.asarray(L0_attn_Wv, np.float32).T)
    wot = np.ascontiguousarray(np.asarray(L0_attn_Wo, np.float32).T)
    b1_pp = np.ascontiguousarray(np.asarray(L0_ffn_b1, np.float32).reshape(HC, 128).T)
    b2_pp = np.ascontiguousarray(np.asarray(L0_ffn_b2, np.float32).reshape(DC, 128).T)
    bq_pp = np.ascontiguousarray(np.asarray(L0_attn_bq, np.float32).reshape(DC, 128).T)
    bo_pp = np.ascontiguousarray(np.asarray(L0_attn_bo, np.float32).reshape(DC, 128).T)
    rb = np.concatenate([np.asarray(L0_attn_bk, np.float32),
                         np.asarray(L0_attn_bv, np.float32)]).reshape(1, 2 * D)
    rb = np.ascontiguousarray(rb)

    nc = build_nc()
    in_maps = []
    for c in range(8):
        b, h = c // 2, c % 2
        xc = np.ascontiguousarray(x[b, T * h:T * (h + 1), :].T)
        mc = np.ascontiguousarray(mask[b, T * h:T * (h + 1), :].T).astype(f8)
        in_maps.append({
            "xT": xc, "w1t": w1t, "w2t": w2t, "wqt": wqt, "wkt": wkt,
            "wvt": wvt, "wot": wot, "b1_pp": b1_pp, "b2_pp": b2_pp,
            "bq_d": bq_pp, "bo_d": bo_pp, "rb_d": rb, "ebc_d": ebc,
            "maskT": mc,
        })
    res = run_bass_kernel_spmd(nc, in_maps, core_ids=list(range(8)))
    out = np.empty((4, 4096, 1024), np.float32)
    for c in range(8):
        b, h = c // 2, c % 2
        out[b, T * h:T * (h + 1), :] = np.asarray(res.results[c]["outT"]).T
    return out


# version 07
# speedup vs baseline: 1.1314x; 1.0636x over previous
import contextlib

import numpy as np
import ml_dtypes

import concourse.bass as bass
import concourse.mybir as mybir
from concourse.bass_utils import run_bass_kernel_spmd

F32 = mybir.dt.float32
BF16 = mybir.dt.bfloat16
F8 = mybir.dt.float8e5
AF = mybir.ActivationFunctionType
ALU = mybir.AluOpType

D = 1024
DC = 8
HID = 4096
HC = 32
T = 2048
ST = 512
NST = T // ST
QT = 256
NQT = T // QT
NH = 16
HD = 64
EPS_LN = 1e-5
EPS_ATTN = 1e-6
INV_D = 1.0 / D
DRPM = mybir.MatmulPerfMode.DoubleRow
DR_MM1 = False
DR_MM2 = True

ENGS = ("sync", "vector", "scalar", "tensor", "gpsimd")


class Prog:

    def __init__(self):
        self.ops = {e: [] for e in ENGS}
        self.sems = {}
        self.cnt = {}

    def add_sem(self, name, handle):
        self.sems[name] = handle
        self.cnt[name] = 0

    def wait(self, eng, sname, val):
        if val <= 0:
            return
        h = self.sems[sname]
        self.ops[eng].append(lambda e, h=h, v=val: e.wait_ge(h, v))

    def wait_ev(self, eng, ev):
        if ev is not None:
            self.wait(eng, ev[0], ev[1])

    def do(self, eng, fn, inc=None, k=1):
        if inc is None:
            self.ops[eng].append(fn)
            return None
        h = self.sems[inc]
        self.cnt[inc] += k
        def g(e, fn=fn, h=h, k=k):
            fn(e).then_inc(h, k)
        self.ops[eng].append(g)
        return (inc, self.cnt[inc])

    def now(self, sname):
        return (sname, self.cnt[sname])


def build_nc(phases=("ffn1", "attn", "ffn2")):
    nc = bass.Bass("TRN2", target_bir_lowering=False, debug=False,
                   detect_race_conditions=False)

    xT = nc.dram_tensor("xT", [D, T], F32, kind="ExternalInput").ap()
    w1t = nc.dram_tensor("w1t", [D, HID], F32, kind="ExternalInput").ap()
    w2t = nc.dram_tensor("w2t", [HID, D], F32, kind="ExternalInput").ap()
    wqt = nc.dram_tensor("wqt", [D, D], F32, kind="ExternalInput").ap()
    wkt = nc.dram_tensor("wkt", [D, D], F32, kind="ExternalInput").ap()
    wvt = nc.dram_tensor("wvt", [D, D], F32, kind="ExternalInput").ap()
    wot = nc.dram_tensor("wot", [D, D], F32, kind="ExternalInput").ap()
    b1_pp = nc.dram_tensor("b1_pp", [128, HC], F32, kind="ExternalInput").ap()
    b2_pp = nc.dram_tensor("b2_pp", [128, DC], F32, kind="ExternalInput").ap()
    bq_d = nc.dram_tensor("bq_d", [128, DC], F32, kind="ExternalInput").ap()
    bo_d = nc.dram_tensor("bo_d", [128, DC], F32, kind="ExternalInput").ap()
    rb_d = nc.dram_tensor("rb_d", [1, 2 * D], F32, kind="ExternalInput").ap()
    ebc_d = nc.dram_tensor("ebc_d", [16, D], F32, kind="ExternalInput").ap()
    maskT = nc.dram_tensor("maskT", [D, T], F8, kind="ExternalInput").ap()
    outT = nc.dram_tensor("outT", [D, T], F32, kind="ExternalOutput").ap()
    cc_in = nc.dram_tensor("cc_in", [128, 520], F32)
    cc_out = nc.dram_tensor("cc_out", [128, 520], F32)

    p = Prog()
    es0 = contextlib.ExitStack()
    with es0:
        def sbuf(name, shape, dt, stack=es0):
            return stack.enter_context(nc.sbuf_tensor(name, shape, dt))

        def sem(name):
            h = es0.enter_context(nc.semaphore(name))
            p.add_sem(name, h)
            return h

        sX = sbuf("sX", [128, DC * T], F32)
        sW1 = sbuf("sW1", [128, DC * HID], F8)
        wslotA = sbuf("wslotA", [128, DC * D], BF16)
        b1q8 = sbuf("b1q8", [128, HC], F8)
        b2q8 = sbuf("b2q8", [128, DC], F8)
        b1qf = sbuf("b1qf", [128, HC], F32)
        b2qf = sbuf("b2qf", [128, DC], F32)
        bq_pp = sbuf("bq_pp", [128, DC], F32)
        bo_pp = sbuf("bo_pp", [128, DC], F32)
        ebc = sbuf("ebc", [16, D], F32)
        onec_bf = sbuf("onec_bf", [128, 1], BF16)
        onec_f = sbuf("onec_f", [128, 1], F32)
        oner_f = sbuf("oner_f", [1, 128], F32)
        oner_bf = sbuf("oner_bf", [1, 128], BF16)
        skv = sbuf("skv", [128, 520], F32)
        statrow = sbuf("statrow", [1, 4 * 512], F32)
        Abc = sbuf("Abc", [128, 512], F32)
        Bbc = sbuf("Bbc", [128, 512], F32)

        for s in ("din", "dw", "v", "s", "t", "g", "do", "cc"):
            sem(s)

        pbank = [es0.enter_context(nc.psum_tensor(f"pb{i}", [128, 512], F32))
                 for i in range(8)]
        bank_free = [None] * 8

        x_ready = []
        for s in range(NST):
            for k in range(DC):
                p.do("sync", lambda e, k=k, s=s: e.dma_start(
                    out=sX[:, T * k + ST * s: T * k + ST * (s + 1)],
                    in_=xT[128 * k:128 * (k + 1), ST * s:ST * (s + 1)]),
                    inc="din", k=16)
            x_ready.append(p.cnt["din"])
            p.wait("sync", "din", p.cnt["din"])

        p.do("sync", lambda e: e.dma_start(out=bq_pp[:], in_=bq_d),
             inc="din", k=16)
        p.do("sync", lambda e: e.dma_start(out=bo_pp[:], in_=bo_d),
             inc="din", k=16)
        p.wait("sync", "din", p.cnt["din"])
        ev_bq = ev_bo = p.now("din")

        ev_w1a = None
        for half in range(2):
            for k in range(DC):
                p.do("gpsimd", lambda e, k=k, half=half: e.dma_start(
                    out=sW1[:, HID * k + 2048 * half:HID * k + 2048 * (half + 1)],
                    in_=w1t[128 * k:128 * (k + 1),
                            2048 * half:2048 * (half + 1)]), inc="dw", k=16)
            p.wait("gpsimd", "dw", p.cnt["dw"])
            if half == 0:
                ev_w1a = p.now("dw")
        ev_w1 = p.now("dw")
        w2_in = w2t.rearrange("(c p) f -> p c f", p=128)
        p.do("gpsimd", lambda e: e.dma_start(out=b1q8[:], in_=b1_pp),
             inc="dw", k=16)
        p.do("gpsimd", lambda e: e.dma_start(out=b2q8[:], in_=b2_pp),
             inc="dw", k=16)
        p.do("gpsimd", lambda e: e.dma_start(out=ebc[:], in_=ebc_d),
             inc="dw", k=16)
        p.wait("gpsimd", "dw", p.cnt["dw"])
        ev_b1q = ev_b2q = ev_rb = ev_ebc = p.now("dw")

        p.wait_ev("vector", ev_b1q)
        p.wait_ev("vector", ev_b2q)
        p.do("vector", lambda e: e.memset(onec_bf[:], 1.0))
        p.do("vector", lambda e: e.memset(onec_f[:], 1.0))
        p.do("vector", lambda e: e.memset(oner_f[:], 1.0))
        p.do("vector", lambda e: e.memset(oner_bf[:], 1.0))
        p.do("vector", lambda e: e.tensor_copy(out=b1qf[:], in_=b1q8[:]))
        ev_binit = p.do("vector", lambda e: e.tensor_copy(
            out=b2qf[:], in_=b2q8[:]), inc="v")

        ln_state = {"statrow_free": None, "ab_free": None}

        def emit_ln_stats(tok0, ntok, s3c, sb=(4, 5), x_wait_din=None):
            m_row = statrow[0:1, 0:ntok]
            veA = statrow[0:1, 512:512 + ntok]
            Brow = statrow[0:1, 1024:1024 + ntok]
            msq = Brow
            nrow = statrow[0:1, 1536:1536 + ntok]
            n2row = Brow
            sb0, sb1 = sb
            p.wait_ev("tensor", ev_binit)
            p.wait_ev("tensor", bank_free[sb0])
            p.wait_ev("tensor", bank_free[sb1])
            if x_wait_din is not None:
                p.wait("tensor", "din", x_wait_din)
                p.wait("vector", "din", x_wait_din)
            ev_sum = None
            for k in range(DC):
                ev_sum = p.do("tensor", lambda e, k=k: e.matmul(
                    pbank[sb0][0:1, 0:ntok], onec_f[:],
                    sX[:, T * k + tok0:T * k + tok0 + ntok],
                    start=(k == 0), stop=(k == DC - 1)),
                    inc=("t" if k == DC - 1 else None))
            ev_sq = [None] * DC
            for k in range(DC):
                s3 = s3c[:, ntok * (k % 2):ntok * (k % 2 + 1)]
                if k >= 2:
                    p.wait_ev("vector", ev_sq[k - 2])
                ev_v = p.do("vector", lambda e, s3=s3, k=k: e.tensor_mul(
                    s3, sX[:, T * k + tok0:T * k + tok0 + ntok],
                    sX[:, T * k + tok0:T * k + tok0 + ntok]), inc="v")
                p.wait_ev("tensor", ev_v)
                ev_sq[k] = p.do("tensor", lambda e, s3=s3, k=k: e.matmul(
                    pbank[sb1][0:1, 0:ntok], onec_f[:], s3,
                    start=(k == 0), stop=(k == DC - 1)), inc="t")
            p.wait_ev("vector", ev_sum)
            p.wait_ev("vector", ev_sq[DC - 1])
            p.wait_ev("vector", ln_state["statrow_free"])
            p.do("vector", lambda e: e.tensor_single_scalar(
                out=m_row, in_=pbank[sb0][0:1, 0:ntok], scalar=INV_D,
                op=ALU.mult))
            p.do("vector", lambda e: e.tensor_scalar(
                out=veA, in0=pbank[sb1][0:1, 0:ntok], scalar1=INV_D,
                scalar2=EPS_LN, op0=ALU.mult, op1=ALU.add))
            p.do("vector", lambda e: e.tensor_mul(msq, m_row, m_row))
            ev_ve = p.do("vector", lambda e: e.tensor_sub(veA, veA, msq),
                         inc="v")
            bank_free[sb0] = ev_ve
            bank_free[sb1] = ev_ve
            p.wait_ev("scalar", ev_ve)
            p.do("scalar", lambda e: e.activation(out=nrow, in_=veA, func=AF.Ln))
            ev_A = p.do("scalar", lambda e: e.activation(
                out=nrow, in_=nrow, func=AF.Exp, scale=-0.5), inc="s")
            p.wait_ev("vector", ev_A)
            p.do("vector", lambda e: e.tensor_mul(n2row, nrow, nrow))
            p.do("vector", lambda e: e.tensor_mul(n2row, n2row, veA))
            p.do("vector", lambda e: e.tensor_scalar(
                out=n2row, in0=n2row, scalar1=-0.5, scalar2=1.5,
                op0=ALU.mult, op1=ALU.add))
            ev_A = p.do("vector", lambda e: e.tensor_mul(veA, nrow, n2row),
                        inc="v")
            ev_B = p.do("vector", lambda e: e.scalar_tensor_tensor(
                out=Brow, in0=veA, scalar=-1.0, in1=m_row,
                op0=ALU.mult, op1=ALU.mult), inc="v")
            return (ev_A, ev_B)

        def emit_ln_apply(tok0, ntok, out_tile, out_col0, tmpf, evAB,
                          bb=(6, 7)):
            veA = statrow[0:1, 512:512 + ntok]
            Brow = statrow[0:1, 1024:1024 + ntok]
            b0, b1 = bb
            ev_A, ev_B = evAB
            p.wait_ev("tensor", bank_free[b0])
            p.wait_ev("tensor", bank_free[b1])
            p.wait_ev("tensor", ev_A)
            p.wait_ev("tensor", ev_B)
            p.do("tensor", lambda e: e.matmul(
                pbank[b0][:, 0:ntok], oner_f[:], veA, start=True, stop=True))
            ev_bc = p.do("tensor", lambda e: e.matmul(
                pbank[b1][:, 0:ntok], oner_f[:], Brow, start=True, stop=True),
                inc="t")
            ln_state["statrow_free"] = ev_bc
            p.wait_ev("scalar", ev_bc)
            p.wait_ev("scalar", ln_state["ab_free"])
            p.do("scalar", lambda e: e.activation(
                out=Abc[:, 0:ntok], in_=pbank[b0][:, 0:ntok], func=AF.Copy))
            ev_ab = p.do("scalar", lambda e: e.activation(
                out=Bbc[:, 0:ntok], in_=pbank[b1][:, 0:ntok], func=AF.Copy),
                inc="s")
            bank_free[b0] = ev_ab
            bank_free[b1] = ev_ab
            p.wait_ev("vector", ev_ab)
            evs = []
            for k in range(DC):
                p.do("vector", lambda e, k=k: e.tensor_mul(
                    tmpf[:, 0:ntok],
                    sX[:, T * k + tok0:T * k + tok0 + ntok],
                    Abc[:, 0:ntok]))
                evs.append(p.do("vector", lambda e, k=k: e.tensor_add(
                    out_tile[:, out_col0 + ntok * k:out_col0 + ntok * (k + 1)],
                    tmpf[:, 0:ntok],
                    Bbc[:, 0:ntok]), inc="v"))
            ln_state["ab_free"] = evs[-1]
            return evs

        def emit_ln_full(tok0, ntok, s3c, tmpf, out_tile, out_col0,
                         sb=(4, 5), bb=(6, 7), x_wait_din=None):
            evAB = emit_ln_stats(tok0, ntok, s3c, sb, x_wait_din)
            return emit_ln_apply(tok0, ntok, out_tile, out_col0, tmpf,
                                 evAB, bb)[-1]

        def emit_ffn(first):
            if not first:
                p.wait("sync", "v", p.cnt["v"])
                p.wait("sync", "t", p.cnt["t"])
            esf = contextlib.ExitStack()
            with esf:
                sW2 = sbuf("sW2F" + str(first), [128, HC * D], F8, esf)
                mask2 = sbuf("maskF" + str(first), [128, DC * ST], F8, esf)
                xnq = sbuf("xnqF" + str(first), [128, 2 * DC * ST], F8, esf)
                hbf = sbuf("hbfF" + str(first), [128, 2 * 512], BF16, esf)
                hq8 = sbuf("hq8F" + str(first), [128, HC * ST], F8, esf)
                y2b = sbuf("y2bF" + str(first), [128, DC * 512], BF16, esf)
                s3c = sbuf("s3F" + str(first), [128, 2 * 512], F32, esf)
                tmpf = s3c

                if not first:
                    p.wait("gpsimd", "v", p.cnt["v"])
                    p.wait("gpsimd", "t", p.cnt["t"])
                    p.wait("tensor", "v", p.cnt["v"])
                ev_w2q = []
                for q in range(4):
                    p.do("gpsimd", lambda e, q=q: e.dma_start(
                        out=sW2[:, D * 8 * q:D * 8 * (q + 1)]
                            .rearrange("p (c f) -> p c f", c=8),
                        in_=w2_in[:, 8 * q:8 * (q + 1), :]), inc="dw", k=16)
                    p.wait("gpsimd", "dw", p.cnt["dw"])
                    ev_w2q.append(p.now("dw"))

                mask_free = [None, None]
                mm1_last = [None, None]
                hq8_free = None
                hbf_free = [None, None]
                y2_free = [None] * DC

                evAB = emit_ln_stats(0, ST, s3c,
                                     x_wait_din=(x_ready[0] if first else None))
                apply_evs = emit_ln_apply(0, ST, xnq, 0, tmpf, evAB)

                sW1_3 = sW1[:].rearrange("p (c f) -> p c f", c=DC)
                sW2_3 = sW2[:].rearrange("p (c f) -> p c f", c=HC)
                hq8_3 = hq8[:].rearrange("p (c f) -> p c f", c=HC)
                ev_res = None
                for s in range(NST):
                    msl = s % 2
                    p.wait_ev("sync", mask_free[0])
                    p.do("sync", lambda e, s=s: e.dma_start(
                        out=mask2[:].rearrange("p (c f) -> p c f", c=DC),
                        in_=maskT.rearrange("(c p) f -> p c f", p=128)
                            [:, :, ST * s:ST * (s + 1)]), inc="din", k=16)
                    p.wait("sync", "din", p.cnt["din"])
                    ev_mask = p.now("din")

                    xnq_3 = xnq[:, DC * ST * msl:DC * ST * (msl + 1)].rearrange(
                        "p (c f) -> p c f", c=DC)
                    ev_cast32 = None
                    for m in range(HC):
                        b = m % 2
                        p.wait_ev("tensor", bank_free[b])
                        p.wait_ev("tensor", ev_w1a if m < HC // 2 else ev_w1)
                        if DR_MM1:
                            for k in range(DC // 2):
                                if m == 0:
                                    p.wait_ev("tensor", apply_evs[2 * k + 1])
                                evg = p.do("tensor", lambda e, m=m, k=k, b=b, msl=msl: e.matmul(
                                    pbank[b][:],
                                    sW1_3[:, 2 * k:2 * k + 2, 128 * m:128 * (m + 1)],
                                    xnq_3[:, 2 * k:2 * k + 2, :],
                                    start=(k == 0), stop=(k == DC // 2 - 1),
                                    perf_mode=DRPM),
                                    inc=("t" if k == DC // 2 - 1 else None))
                        else:
                            for k in range(DC):
                                if m == 0:
                                    p.wait_ev("tensor", apply_evs[k])
                                evg = p.do("tensor", lambda e, m=m, k=k, b=b, msl=msl: e.matmul(
                                    pbank[b][:],
                                    sW1[:, HID * k + 128 * m:HID * k + 128 * (m + 1)],
                                    xnq[:, DC * ST * msl + ST * k:DC * ST * msl + ST * (k + 1)],
                                    start=(k == 0), stop=(k == DC - 1)),
                                    inc=("t" if k == DC - 1 else None))
                        hsl = m % 2
                        p.wait_ev("scalar", evg)
                        p.wait_ev("scalar", hbf_free[hsl])
                        if first and m == 0 and s == 0:
                            p.wait_ev("scalar", ev_binit)
                        ev_ev = p.do("scalar", lambda e, m=m, b=b, hsl=hsl: e.activation(
                            out=hbf[:, 512 * hsl:512 * (hsl + 1)],
                            in_=pbank[b][:], func=AF.Relu,
                            bias=b1qf[:, m:m + 1]), inc="s")
                        bank_free[b] = ev_ev
                        p.wait_ev("vector", ev_ev)
                        if hq8_free is not None and m == 0:
                            p.wait_ev("vector", hq8_free)
                        ev_cast32 = p.do("vector", lambda e, m=m, hsl=hsl: e.tensor_copy(
                            out=hq8[:, ST * m:ST * (m + 1)],
                            in_=hbf[:, 512 * hsl:512 * (hsl + 1)]), inc="v")
                        hbf_free[hsl] = ev_cast32
                    mm1_last[msl] = evg

                    if s + 1 < NST:
                        evAB = emit_ln_stats(
                            ST * (s + 1), ST, s3c,
                            x_wait_din=(x_ready[s + 1] if first else None))

                    y_evs = []
                    for n in range(DC):
                        b = 2 + n % 2
                        p.wait_ev("tensor", bank_free[b])
                        p.wait_ev("tensor", ev_cast32)
                        if DR_MM2 and not first:
                            for j in range(HC // 2):
                                if s == 0:
                                    p.wait_ev("tensor", ev_w2q[(2 * j + 1) // 8])
                                evg = p.do("tensor", lambda e, n=n, j=j, b=b: e.matmul(
                                    pbank[b][:],
                                    sW2_3[:, 2 * j:2 * j + 2, 128 * n:128 * (n + 1)],
                                    hq8_3[:, 2 * j:2 * j + 2, :],
                                    start=(j == 0), stop=(j == HC // 2 - 1),
                                    perf_mode=DRPM),
                                    inc=("t" if j == HC // 2 - 1 else None))
                        else:
                            for j in range(HC):
                                if s == 0:
                                    p.wait_ev("tensor", ev_w2q[j // 8])
                                evg = p.do("tensor", lambda e, n=n, j=j, b=b: e.matmul(
                                    pbank[b][:],
                                    sW2[:, D * j + 128 * n:D * j + 128 * (n + 1)],
                                    hq8[:, ST * j:ST * (j + 1)],
                                    start=(j == 0), stop=(j == HC - 1)),
                                    inc=("t" if j == HC - 1 else None))
                        if n == DC - 1:
                            hq8_free = evg
                        p.wait_ev("scalar", evg)
                        p.wait_ev("scalar", y2_free[n])
                        ev_y = p.do("scalar", lambda e, n=n, b=b: e.activation(
                            out=y2b[:, 512 * n:512 * (n + 1)],
                            in_=pbank[b][:], func=AF.Identity,
                            bias=b2qf[:, n:n + 1]), inc="s")
                        bank_free[b] = ev_y
                        y_evs.append(ev_y)

                    if s + 1 < NST:
                        p.wait_ev("vector", mm1_last[(s + 1) % 2])
                        apply_evs = emit_ln_apply(
                            ST * (s + 1), ST, xnq, DC * ST * ((s + 1) % 2),
                            tmpf, evAB)

                    for n in range(DC):
                        p.wait_ev("vector", y_evs[n])
                        if n == 0:
                            p.wait_ev("vector", ev_mask)
                        p.do("vector", lambda e, n=n: e.tensor_mul(
                            tmpf[:, 0:512],
                            y2b[:, 512 * n:512 * (n + 1)],
                            mask2[:, ST * n:ST * (n + 1)]))
                        ev_res = p.do("vector", lambda e, n=n, s=s: e.tensor_add(
                            sX[:, T * n + ST * s:T * n + ST * (s + 1)],
                            sX[:, T * n + ST * s:T * n + ST * (s + 1)],
                            tmpf[:, 0:512]), inc="v")
                        y2_free[n] = ev_res
                        if n == DC - 1:
                            mask_free[0] = ev_res
            return ev_res

        ev_ffn1 = emit_ffn(first=True) if "ffn1" in phases else None

        wk_in = wkt.rearrange("(c p) f -> p c f", p=128)
        p.do("gpsimd", lambda e: e.dma_start(
            out=wslotA[:].rearrange("p (c f) -> p c f", c=DC), in_=wk_in),
            inc="dw", k=16)
        p.wait("gpsimd", "dw", p.cnt["dw"])
        ev_wk = p.now("dw")

        if "attn" in phases:
            es_a = contextlib.ExitStack()
            with es_a:
                wslotB = sbuf("wslotB", [128, DC * D], BF16, es_a)

                es_kv = contextlib.ExitStack()
                with es_kv:
                    bcK = sbuf("bcK", [128, D], F32, es_kv)
                    bcV = sbuf("bcV", [128, D], F32, es_kv)
                    es_rb = contextlib.ExitStack()
                    rbf32 = sbuf("rbf32", [1, 2 * D], F32, es_rb)

                    p.wait_ev("gpsimd", ev_ffn1)
                    p.wait("gpsimd", "t", p.cnt["t"])
                    wv_in = wvt.rearrange("(c p) f -> p c f", p=128)
                    p.do("gpsimd", lambda e: e.dma_start(out=rbf32[:], in_=rb_d),
                         inc="dw", k=16)
                    p.do("gpsimd", lambda e: e.dma_start(
                        out=wslotB[:].rearrange("p (c f) -> p c f", c=DC),
                        in_=wv_in), inc="dw", k=16)
                    p.wait("gpsimd", "dw", p.cnt["dw"])
                    ev_wv = p.now("dw")
                    p.wait_ev("tensor", ev_wv)
                    for b in (0, 1, 2, 3):
                        p.wait_ev("tensor", bank_free[b])
                    for half in range(2):
                        p.do("tensor", lambda e, half=half: e.matmul(
                            pbank[half][:], oner_f[:],
                            rbf32[0:1, 512 * half:512 * (half + 1)],
                            start=True, stop=True))
                        ev_bc0 = p.do("tensor", lambda e, half=half: e.matmul(
                            pbank[2 + half][:], oner_f[:],
                            rbf32[0:1, D + 512 * half:D + 512 * (half + 1)],
                            start=True, stop=True), inc="t")
                    p.wait_ev("vector", ev_bc0)
                    for half in range(2):
                        p.do("vector", lambda e, half=half: e.tensor_copy(
                            out=bcK[:, 512 * half:512 * (half + 1)],
                            in_=pbank[half][:]))
                        ev_bcv = p.do("vector", lambda e, half=half: e.tensor_copy(
                            out=bcV[:, 512 * half:512 * (half + 1)],
                            in_=pbank[2 + half][:]), inc="v")
                    for b in (0, 1, 2, 3):
                        bank_free[b] = ev_bcv
                    es_rb.close()
                    xn2 = sbuf("xn2", [128, DC * ST], BF16, es_kv)
                    ktm = sbuf("ktm", [128, D], BF16, es_kv)
                    v65 = sbuf("v65", [128, NH * 65], BF16, es_kv)
                    s3A = sbuf("s3A", [128, 2 * 512], F32, es_kv)
                    tmpa = sbuf("tmpa", [128, 512], F32, es_kv)
                    tmpb = sbuf("tmpb", [128, 512], F32, es_kv)

                    p.do("vector", lambda e: e.memset(
                        v65[:].rearrange("p (h c) -> p h c", c=65)[:, :, 64:65], 1.0))

                    xn2_free = None
                    ktm_free = None
                    v65_free = None
                    ev_kvadd = None
                    p.wait("tensor", "v", p.cnt["v"])
                    for s in range(NST):
                        p.wait_ev("vector", xn2_free)
                        p.wait_ev("vector", ev_ffn1)
                        ev_xn2 = emit_ln_full(ST * s, ST, s3A, tmpa, xn2, 0)
                        for t4 in range(4):
                            t_ = 4 * s + t4
                            p.wait_ev("tensor", ev_wk)
                            p.wait_ev("tensor", ev_xn2)
                            for half in range(2):
                                b = half
                                p.wait_ev("tensor", bank_free[b])
                                for k in range(DC):
                                    evg = p.do("tensor", lambda e, k=k, b=b, t4=t4, half=half: e.matmul(
                                        pbank[b][:],
                                        xn2[:, ST * k + 128 * t4:ST * k + 128 * (t4 + 1)],
                                        wslotA[:, D * k + 512 * half:D * k + 512 * (half + 1)],
                                        start=(k == 0), stop=(k == DC - 1)),
                                        inc=("t" if k == DC - 1 else None))
                                p.wait_ev("vector", evg)
                                p.wait_ev("vector", ktm_free)
                                p.do("vector", lambda e, b=b, half=half: e.tensor_add(
                                    tmpa[:], pbank[b][:],
                                    bcK[:, 512 * half:512 * (half + 1)]))
                                ev_mn = p.do("vector", lambda e: e.tensor_scalar_min(
                                    out=tmpb[:], in0=tmpa[:], scalar1=0.0), inc="v")
                                p.wait_ev("scalar", ev_mn)
                                ev_ex = p.do("scalar", lambda e: e.activation(
                                    out=tmpb[:], in_=tmpb[:], func=AF.Exp), inc="s")
                                p.wait_ev("vector", ev_ex)
                                ev_k = p.do("vector", lambda e, half=half: e.scalar_tensor_tensor(
                                    out=ktm[:, 512 * half:512 * (half + 1)],
                                    in0=tmpa[:], scalar=0.0, in1=tmpb[:],
                                    op0=ALU.max, op1=ALU.add), inc="v")
                                bank_free[b] = ev_k
                            p.wait_ev("tensor", ev_wv)
                            for half in range(2):
                                b = 2 + half
                                p.wait_ev("tensor", bank_free[b])
                                for k in range(DC):
                                    evg = p.do("tensor", lambda e, k=k, b=b, t4=t4, half=half: e.matmul(
                                        pbank[b][:],
                                        xn2[:, ST * k + 128 * t4:ST * k + 128 * (t4 + 1)],
                                        wslotB[:, D * k + 512 * half:D * k + 512 * (half + 1)],
                                        start=(k == 0), stop=(k == DC - 1)),
                                        inc=("t" if k == DC - 1 else None))
                                p.wait_ev("vector", evg)
                                p.wait_ev("vector", v65_free)
                                ev_v = p.do("vector", lambda e, b=b, half=half: e.tensor_add(
                                    v65[:].rearrange("p (h c) -> p h c", c=65)
                                        [:, 8 * half:8 * (half + 1), 0:64],
                                    pbank[b][:].rearrange("p (h c) -> p h c", c=64),
                                    bcV[:, 512 * half:512 * (half + 1)]
                                        .rearrange("p (h c) -> p h c", c=64)),
                                    inc="v")
                                bank_free[b] = ev_v
                            p.wait_ev("tensor", ev_k)
                            p.wait_ev("tensor", ev_v)
                            for q in range(4, 8):
                                p.wait_ev("tensor", bank_free[q])
                            evkv = None
                            for h in range(NH):
                                bkn = 4 + h // 4
                                r0 = 64 * (h % 2)
                                c0 = 65 * ((h % 4) // 2)
                                evkv = p.do("tensor", lambda e, h=h, bkn=bkn, r0=r0, c0=c0: e.matmul(
                                    pbank[bkn][r0:r0 + 64, c0:c0 + 65],
                                    ktm[:, HD * h:HD * (h + 1)],
                                    v65[:, 65 * h:65 * (h + 1)],
                                    start=True, stop=True,
                                    tile_position=(0, r0)),
                                    inc=("t" if h == NH - 1 else None))
                            ktm_free = evkv
                            v65_free = evkv
                            p.wait_ev("vector", evkv)
                            for q in range(4):
                                if t_ == 0:
                                    ev_kvadd = p.do("vector", lambda e, q=q: e.tensor_copy(
                                        out=skv[:, 130 * q:130 * (q + 1)],
                                        in_=pbank[4 + q][:, 0:130]), inc="v")
                                else:
                                    ev_kvadd = p.do("vector", lambda e, q=q: e.tensor_add(
                                        skv[:, 130 * q:130 * (q + 1)],
                                        skv[:, 130 * q:130 * (q + 1)],
                                        pbank[4 + q][:, 0:130]), inc="v")
                            for q in range(4, 8):
                                bank_free[q] = ev_kvadd
                            xn2_free = evkv if t4 == 3 else xn2_free

                    p.wait_ev("gpsimd", ev_kvadd)
                    p.do("gpsimd", lambda e: e.dma_start(
                        out=cc_in[:, :], in_=skv[:]), inc="dw", k=16)
                    p.wait("gpsimd", "dw", p.cnt["dw"])
                    p.do("gpsimd", lambda e: e.collective_compute(
                        "AllReduce", ALU.add,
                        replica_groups=[[0, 1], [2, 3], [4, 5], [6, 7]],
                        ins=[cc_in.ap().opt()], outs=[cc_out.ap().opt()]),
                        inc="cc")
                    p.wait("gpsimd", "t", p.cnt["t"])
                    wq_in = wqt.rearrange("(c p) f -> p c f", p=128)
                    p.do("gpsimd", lambda e: e.dma_start(
                        out=wslotA[:].rearrange("p (c f) -> p c f", c=DC),
                        in_=wq_in), inc="dw", k=16)
                    p.wait("gpsimd", "dw", p.cnt["dw"])
                    ev_wq = p.now("dw")
                    p.wait("gpsimd", "cc", 1)
                    p.do("gpsimd", lambda e: e.dma_start(
                        out=skv[:], in_=cc_out[:, :]), inc="dw", k=16)
                    p.wait("gpsimd", "dw", p.cnt["dw"])
                    ev_kvtot = p.now("dw")

                es_q = contextlib.ExitStack()
                with es_q:
                    wolo = sbuf("wolo", [128, DC * D], F8, es_q)
                    wostage = sbuf("wostage", [128, 512], F32, es_q)
                    xn2q = sbuf("xn2q", [128, DC * QT], BF16, es_q)
                    qphi = sbuf("qphi", [128, DC * QT], BF16, es_q)
                    outpre = sbuf("outpre", [128, DC * QT], BF16, es_q)
                    outpre_lo = sbuf("outpre_lo", [128, DC * QT], BF16, es_q)
                    outpre_h8 = sbuf("outpre_h8", [128, DC * QT], F8, es_q)
                    kvbd = sbuf("kvbd", [128, DC * 128], BF16, es_q)
                    zbd = sbuf("zbd", [128, DC * 16], BF16, es_q)
                    tmpn = sbuf("tmpn", [16, QT], F32, es_q)
                    s3Q = sbuf("s3Q", [128, 2 * QT], F32, es_q)
                    tmpaQ = sbuf("tmpaQ", [128, QT], F32, es_q)
                    tmpeQ = sbuf("tmpeQ", [128, QT], F32, es_q)
                    tmpoQ = sbuf("tmpoQ", [128, QT], F32, es_q)

                    p.wait_ev("vector", ev_kvtot)
                    p.do("vector", lambda e: e.memset(kvbd[:], 0.0))
                    p.do("vector", lambda e: e.memset(zbd[:], 0.0))
                    ev_bd = None
                    for h in range(NH):
                        r0 = 64 * (h % 2)
                        c0 = 130 * (h // 4) + 65 * ((h % 4) // 2)
                        blk = h // 2
                        p.do("vector", lambda e, h=h, r0=r0, c0=c0, blk=blk: e.tensor_copy(
                            out=kvbd[r0:r0 + 64,
                                     128 * blk + 64 * (h % 2):128 * blk + 64 * (h % 2) + 64],
                            in_=skv[r0:r0 + 64, c0:c0 + 64]))
                        ev_bd = p.do("vector", lambda e, h=h, r0=r0, c0=c0, blk=blk: e.tensor_copy(
                            out=zbd[r0:r0 + 64, 16 * blk + h:16 * blk + h + 1],
                            in_=skv[r0:r0 + 64, c0 + 64:c0 + 65]), inc="v")

                    ev_res_attn = None
                    qphi_free = None
                    outpre_free = None
                    for s in range(NQT):
                        ev_xq = emit_ln_full(QT * s, QT, s3Q, tmpaQ, xn2q, 0)
                        p.wait_ev("tensor", ev_wq)
                        ev_qphi = None
                        for n in range(DC):
                            b = 4 + n % 2
                            p.wait_ev("tensor", bank_free[b])
                            p.wait_ev("tensor", ev_xq)
                            for k in range(DC):
                                evg = p.do("tensor", lambda e, n=n, k=k, b=b: e.matmul(
                                    pbank[b][:, 0:QT],
                                    wslotA[:, D * k + 128 * n:D * k + 128 * (n + 1)],
                                    xn2q[:, QT * k:QT * (k + 1)],
                                    start=(k == 0), stop=(k == DC - 1)),
                                    inc=("t" if k == DC - 1 else None))
                            p.wait_ev("vector", evg)
                            p.wait_ev("vector", qphi_free)
                            ev_mn = p.do("vector", lambda e, n=n, b=b: e.tensor_scalar(
                                out=tmpaQ[:], in0=pbank[b][:, 0:QT],
                                scalar1=bq_pp[:, n:n + 1], scalar2=0.0,
                                op0=ALU.add, op1=ALU.min), inc="v")
                            p.wait_ev("scalar", ev_mn)
                            p.wait_ev("scalar", ev_bq)
                            ev_ex = p.do("scalar", lambda e: e.activation(
                                out=tmpeQ[:], in_=tmpaQ[:], func=AF.Exp), inc="s")
                            p.wait_ev("vector", ev_ex)
                            p.do("vector", lambda e, n=n, b=b: e.tensor_scalar(
                                out=tmpaQ[:], in0=pbank[b][:, 0:QT],
                                scalar1=bq_pp[:, n:n + 1], scalar2=0.0,
                                op0=ALU.add, op1=ALU.max))
                            ev_qphi = p.do("vector", lambda e, n=n: e.tensor_add(
                                qphi[:, QT * n:QT * (n + 1)], tmpaQ[:], tmpeQ[:]),
                                inc="v")
                            bank_free[b] = ev_qphi
                        p.wait_ev("tensor", bank_free[6])
                        p.wait_ev("tensor", ev_bd)
                        p.wait_ev("tensor", ev_qphi)
                        for k in range(DC):
                            evg = p.do("tensor", lambda e, k=k: e.matmul(
                                pbank[6][0:16, 0:QT], zbd[:, 16 * k:16 * (k + 1)],
                                qphi[:, QT * k:QT * (k + 1)],
                                start=(k == 0), stop=(k == DC - 1)),
                                inc=("t" if k == DC - 1 else None))
                        p.wait_ev("vector", evg)
                        p.do("vector", lambda e: e.tensor_single_scalar(
                            out=tmpn[:], in_=pbank[6][0:16, 0:QT],
                            scalar=EPS_ATTN, op=ALU.add))
                        ev_rec = p.do("vector", lambda e: e.reciprocal(
                            out=tmpn[:], in_=tmpn[:]), inc="v")
                        bank_free[6] = ev_rec
                        for k in range(DC):
                            b = 4 + k % 2
                            br = 6 + k % 2
                            p.wait_ev("tensor", bank_free[b])
                            p.wait_ev("tensor", bank_free[br])
                            p.wait_ev("tensor", ev_rec)
                            evg = p.do("tensor", lambda e, k=k, b=b: e.matmul(
                                pbank[b][:, 0:QT], kvbd[:, 128 * k:128 * (k + 1)],
                                qphi[:, QT * k:QT * (k + 1)],
                                start=True, stop=True))
                            evg = p.do("tensor", lambda e, k=k, br=br: e.matmul(
                                pbank[br][:, 0:QT], ebc[:, 128 * k:128 * (k + 1)],
                                tmpn[:], start=True, stop=True), inc="t")
                            p.wait_ev("vector", evg)
                            p.wait_ev("vector", outpre_free)
                            p.do("vector", lambda e, k=k, b=b: e.tensor_copy(
                                out=tmpoQ[:], in_=pbank[b][:, 0:QT]))
                            p.do("vector", lambda e, k=k, br=br: e.tensor_mul(
                                tmpoQ[:], tmpoQ[:], pbank[br][:, 0:QT]))
                            p.do("vector", lambda e, k=k: e.tensor_copy(
                                out=outpre[:, QT * k:QT * (k + 1)], in_=tmpoQ[:]))
                            p.do("vector", lambda e, k=k: e.tensor_sub(
                                outpre_lo[:, QT * k:QT * (k + 1)], tmpoQ[:],
                                outpre[:, QT * k:QT * (k + 1)]))
                            ev_op = p.do("vector", lambda e, k=k: e.tensor_copy(
                                out=outpre_h8[:, QT * k:QT * (k + 1)],
                                in_=outpre[:, QT * k:QT * (k + 1)]), inc="v")
                            bank_free[b] = ev_op
                            bank_free[br] = ev_op
                        qphi_free = ev_op
                        if s == 0:
                            p.wait("gpsimd", "v", p.cnt["v"])
                            wo_in = wot.rearrange("(c p) f -> p c f", p=128)
                            p.do("gpsimd", lambda e: e.dma_start(
                                out=wslotB[:].rearrange("p (c f) -> p c f", c=DC),
                                in_=wo_in), inc="dw", k=16)
                            p.wait("gpsimd", "dw", p.cnt["dw"])
                            ev_wo = p.now("dw")
                            ev_wl = None
                            for k in range(DC):
                                for hf in range(2):
                                    p.wait_ev("gpsimd", ev_wl)
                                    evst = p.do("gpsimd", lambda e, k=k, hf=hf: e.dma_start(
                                        out=wostage[:],
                                        in_=wot[128 * k:128 * (k + 1),
                                                512 * hf:512 * (hf + 1)]),
                                        inc="dw", k=16)
                                    p.wait("gpsimd", "dw", p.cnt["dw"])
                                    p.wait_ev("vector", evst)
                                    ev_wl = p.do("vector", lambda e, k=k, hf=hf: e.tensor_sub(
                                        wolo[:, D * k + 512 * hf:D * k + 512 * (hf + 1)],
                                        wostage[:],
                                        wslotB[:, D * k + 512 * hf:D * k + 512 * (hf + 1)]),
                                        inc="v")
                            ev_wo_lo = ev_wl
                        p.wait_ev("tensor", ev_wo)
                        p.wait_ev("tensor", ev_wo_lo)
                        for n in range(DC):
                            b = n % 2
                            p.wait_ev("tensor", bank_free[b])
                            p.wait_ev("tensor", ev_op)
                            for k in range(DC):
                                p.do("tensor", lambda e, n=n, k=k, b=b: e.matmul(
                                    pbank[b][:, 0:QT],
                                    wslotB[:, D * k + 128 * n:D * k + 128 * (n + 1)],
                                    outpre[:, QT * k:QT * (k + 1)],
                                    start=(k == 0), stop=False))
                                p.do("tensor", lambda e, n=n, k=k, b=b: e.matmul(
                                    pbank[b][:, 0:QT],
                                    wslotB[:, D * k + 128 * n:D * k + 128 * (n + 1)],
                                    outpre_lo[:, QT * k:QT * (k + 1)],
                                    start=False, stop=False))
                                evg = p.do("tensor", lambda e, n=n, k=k, b=b: e.matmul(
                                    pbank[b][:, 0:QT],
                                    wolo[:, D * k + 128 * n:D * k + 128 * (n + 1)],
                                    outpre_h8[:, QT * k:QT * (k + 1)],
                                    start=False, stop=(k == DC - 1)),
                                    inc=("t" if k == DC - 1 else None))
                            p.wait_ev("vector", evg)
                            p.wait_ev("vector", ev_bo)
                            ev_res_attn = p.do("vector", lambda e, n=n, s=s, b=b: e.scalar_tensor_tensor(
                                out=sX[:, T * n + QT * s:T * n + QT * (s + 1)],
                                in0=pbank[b][:, 0:QT], scalar=bo_pp[:, n:n + 1],
                                in1=sX[:, T * n + QT * s:T * n + QT * (s + 1)],
                                op0=ALU.add, op1=ALU.add), inc="v")
                            bank_free[b] = ev_res_attn
                        outpre_free = evg

        ev_ffn2 = emit_ffn(first=False) if "ffn2" in phases else None

        p.wait_ev("sync", ev_ffn2)
        p.wait("sync", "v", p.cnt["v"])
        ndo = 0
        for k in range(DC):
            p.do("sync", lambda e, k=k: e.dma_start(
                out=outT[128 * k:128 * (k + 1), :],
                in_=sX[:, T * k:T * (k + 1)]), inc="do", k=16)
            ndo += 16
        p.wait("sync", "do", ndo)

        with nc.Block() as block:
            for eng_name, deco in (("sync", block.sync), ("vector", block.vector),
                                   ("scalar", block.scalar), ("tensor", block.tensor),
                                   ("gpsimd", block.gpsimd)):
                ops = p.ops[eng_name]
                def run(e, ops=ops):
                    for f in ops:
                        f(e)
                deco(run)

    return nc


def _threefry2x32(k0, k1, x0, x1):
    x0 = x0.astype(np.uint32).copy()
    x1 = x1.astype(np.uint32).copy()
    ks0, ks1 = np.uint32(k0), np.uint32(k1)
    ks2 = np.uint32(ks0 ^ ks1 ^ np.uint32(0x1BD11BDA))
    rots = [(13, 15, 26, 6), (17, 29, 16, 24)]
    with np.errstate(over="ignore"):
        x0 = (x0 + ks0).astype(np.uint32)
        x1 = (x1 + ks1).astype(np.uint32)
        ks = [ks1, ks2, ks0]
        for r in range(5):
            for d in rots[r % 2]:
                x0 = (x0 + x1).astype(np.uint32)
                x1 = ((x1 << np.uint32(d)) | (x1 >> np.uint32(32 - d))).astype(np.uint32)
                x1 = (x1 ^ x0).astype(np.uint32)
            x0 = (x0 + ks[r % 3]).astype(np.uint32)
            x1 = (x1 + ks[(r + 1) % 3] + np.uint32(r + 1)).astype(np.uint32)
    return x0, x1


def _host_constants():
    n = 4 * 4096 * 1024
    i = np.arange(n, dtype=np.uint64)
    b1, b2 = _threefry2x32(0, 42, (i >> np.uint64(32)).astype(np.uint32),
                           (i & np.uint64(0xFFFFFFFF)).astype(np.uint32))
    bits = (b1 ^ b2).astype(np.uint32)
    u = ((bits >> np.uint32(9)) | np.uint32(0x3F800000)).view(np.float32)         - np.float32(1.0)
    keep = (u < np.float32(0.8)).reshape(4, 4096, 1024)
    mask = np.where(keep, np.float32(0.625), np.float32(0.0))
    ebc = np.zeros((16, 1024), np.float32)
    for h in range(16):
        for k in range(8):
            if h // 2 == k:
                ebc[h, 128 * k + 64 * (h % 2):128 * k + 64 * (h % 2) + 64] = 1.0
    return mask, ebc


def kernel(L0_attn_x, L0_attn_Wq, L0_attn_bq, L0_attn_Wk, L0_attn_bk,
           L0_attn_Wv, L0_attn_bv, L0_attn_Wo, L0_attn_bo,
           L0_ffn_W1, L0_ffn_b1, L0_ffn_W2, L0_ffn_b2):
    x = np.asarray(L0_attn_x, np.float32)
    f8 = ml_dtypes.float8_e5m2

    mask, ebc = _host_constants()

    w1t = np.ascontiguousarray(np.asarray(L0_ffn_W1, np.float32).T)
    w2t = np.ascontiguousarray(np.asarray(L0_ffn_W2, np.float32).T)
    wqt = np.ascontiguousarray(np.asarray(L0_attn_Wq, np.float32).T)
    wkt = np.ascontiguousarray(np.asarray(L0_attn_Wk, np.float32).T)
    wvt = np.ascontiguousarray(np.asarray(L0_attn_Wv, np.float32).T)
    wot = np.ascontiguousarray(np.asarray(L0_attn_Wo, np.float32).T)
    b1_pp = np.ascontiguousarray(np.asarray(L0_ffn_b1, np.float32).reshape(HC, 128).T)
    b2_pp = np.ascontiguousarray(np.asarray(L0_ffn_b2, np.float32).reshape(DC, 128).T)
    bq_pp = np.ascontiguousarray(np.asarray(L0_attn_bq, np.float32).reshape(DC, 128).T)
    bo_pp = np.ascontiguousarray(np.asarray(L0_attn_bo, np.float32).reshape(DC, 128).T)
    rb = np.concatenate([np.asarray(L0_attn_bk, np.float32),
                         np.asarray(L0_attn_bv, np.float32)]).reshape(1, 2 * D)
    rb = np.ascontiguousarray(rb)

    nc = build_nc()
    in_maps = []
    for c in range(8):
        b, h = c // 2, c % 2
        xc = np.ascontiguousarray(x[b, T * h:T * (h + 1), :].T)
        mc = np.ascontiguousarray(mask[b, T * h:T * (h + 1), :].T).astype(f8)
        in_maps.append({
            "xT": xc, "w1t": w1t, "w2t": w2t, "wqt": wqt, "wkt": wkt,
            "wvt": wvt, "wot": wot, "b1_pp": b1_pp, "b2_pp": b2_pp,
            "bq_d": bq_pp, "bo_d": bo_pp, "rb_d": rb, "ebc_d": ebc,
            "maskT": mc,
        })
    res = run_bass_kernel_spmd(nc, in_maps, core_ids=list(range(8)))
    out = np.empty((4, 4096, 1024), np.float32)
    for c in range(8):
        b, h = c // 2, c % 2
        out[b, T * h:T * (h + 1), :] = np.asarray(res.results[c]["outT"]).T
    return out


# version 08
# speedup vs baseline: 1.1650x; 1.0297x over previous
import contextlib

import numpy as np
import ml_dtypes

import concourse.bass as bass
import concourse.mybir as mybir
from concourse.bass_utils import run_bass_kernel_spmd

F32 = mybir.dt.float32
BF16 = mybir.dt.bfloat16
F8 = mybir.dt.float8e5
AF = mybir.ActivationFunctionType
ALU = mybir.AluOpType

D = 1024
DC = 8
HID = 4096
HC = 32
T = 2048
ST = 512
NST = T // ST
QT = 256
NQT = T // QT
NH = 16
HD = 64
EPS_LN = 1e-5
EPS_ATTN = 1e-6
INV_D = 1.0 / D
DRPM = mybir.MatmulPerfMode.DoubleRow
DR_MM1 = False
DR_MM2 = True

ENGS = ("sync", "vector", "scalar", "tensor", "gpsimd")


class Prog:

    def __init__(self):
        self.ops = {e: [] for e in ENGS}
        self.sems = {}
        self.cnt = {}

    def add_sem(self, name, handle):
        self.sems[name] = handle
        self.cnt[name] = 0

    def wait(self, eng, sname, val):
        if val <= 0:
            return
        h = self.sems[sname]
        self.ops[eng].append(lambda e, h=h, v=val: e.wait_ge(h, v))

    def wait_ev(self, eng, ev):
        if ev is not None:
            self.wait(eng, ev[0], ev[1])

    def do(self, eng, fn, inc=None, k=1):
        if inc is None:
            self.ops[eng].append(fn)
            return None
        h = self.sems[inc]
        self.cnt[inc] += k
        def g(e, fn=fn, h=h, k=k):
            fn(e).then_inc(h, k)
        self.ops[eng].append(g)
        return (inc, self.cnt[inc])

    def now(self, sname):
        return (sname, self.cnt[sname])


def build_nc(phases=("ffn1", "attn", "ffn2")):
    nc = bass.Bass("TRN2", target_bir_lowering=False, debug=False,
                   detect_race_conditions=False)

    xT = nc.dram_tensor("xT", [D, T], F32, kind="ExternalInput").ap()
    w1t = nc.dram_tensor("w1t", [D, HID], F32, kind="ExternalInput").ap()
    w2t = nc.dram_tensor("w2t", [HID, D], F32, kind="ExternalInput").ap()
    wqt = nc.dram_tensor("wqt", [D, D], F32, kind="ExternalInput").ap()
    wkt = nc.dram_tensor("wkt", [D, D], F32, kind="ExternalInput").ap()
    wvt = nc.dram_tensor("wvt", [D, D], F32, kind="ExternalInput").ap()
    wot = nc.dram_tensor("wot", [D, D], F32, kind="ExternalInput").ap()
    b1_pp = nc.dram_tensor("b1_pp", [128, HC], F32, kind="ExternalInput").ap()
    b2_pp = nc.dram_tensor("b2_pp", [128, DC], F32, kind="ExternalInput").ap()
    bq_d = nc.dram_tensor("bq_d", [128, DC], F32, kind="ExternalInput").ap()
    bo_d = nc.dram_tensor("bo_d", [128, DC], F32, kind="ExternalInput").ap()
    rb_d = nc.dram_tensor("rb_d", [1, 2 * D], F32, kind="ExternalInput").ap()
    ebc_d = nc.dram_tensor("ebc_d", [16, D], F32, kind="ExternalInput").ap()
    maskT = nc.dram_tensor("maskT", [D, T], F8, kind="ExternalInput").ap()
    outT = nc.dram_tensor("outT", [D, T], F32, kind="ExternalOutput").ap()
    cc_in = nc.dram_tensor("cc_in", [128, 520], F32)
    cc_out = nc.dram_tensor("cc_out", [128, 520], F32)

    p = Prog()
    es0 = contextlib.ExitStack()
    with es0:
        def sbuf(name, shape, dt, stack=es0):
            return stack.enter_context(nc.sbuf_tensor(name, shape, dt))

        def sem(name):
            h = es0.enter_context(nc.semaphore(name))
            p.add_sem(name, h)
            return h

        sX = sbuf("sX", [128, DC * T], F32)
        sW1 = sbuf("sW1", [128, DC * HID], F8)
        wslotA = sbuf("wslotA", [128, DC * D], BF16)
        b1q8 = sbuf("b1q8", [128, HC], F8)
        b2q8 = sbuf("b2q8", [128, DC], F8)
        b1qf = sbuf("b1qf", [128, HC], F32)
        b2qf = sbuf("b2qf", [128, DC], F32)
        bq_pp = sbuf("bq_pp", [128, DC], F32)
        bo_pp = sbuf("bo_pp", [128, DC], F32)
        ebc = sbuf("ebc", [16, D], F32)
        onec_bf = sbuf("onec_bf", [128, 1], BF16)
        onec_f = sbuf("onec_f", [128, 1], F32)
        oner_f = sbuf("oner_f", [1, 128], F32)
        oner_bf = sbuf("oner_bf", [1, 128], BF16)
        skv = sbuf("skv", [128, 520], F32)
        statrow = sbuf("statrow", [1, 4 * 512], F32)
        Abc = sbuf("Abc", [128, 512], F32)
        Bbc = sbuf("Bbc", [128, 512], F32)

        for s in ("din", "dw", "v", "s", "t", "g", "do", "cc"):
            sem(s)

        pbank = [es0.enter_context(nc.psum_tensor(f"pb{i}", [128, 512], F32))
                 for i in range(8)]
        bank_free = [None] * 8

        x_ready = []
        for s in range(NST):
            for k in range(DC):
                p.do("sync", lambda e, k=k, s=s: e.dma_start(
                    out=sX[:, T * k + ST * s: T * k + ST * (s + 1)],
                    in_=xT[128 * k:128 * (k + 1), ST * s:ST * (s + 1)]),
                    inc="din", k=16)
            x_ready.append(p.cnt["din"])
            p.wait("sync", "din", p.cnt["din"])

        p.do("sync", lambda e: e.dma_start(out=bq_pp[:], in_=bq_d),
             inc="din", k=16)
        p.do("sync", lambda e: e.dma_start(out=bo_pp[:], in_=bo_d),
             inc="din", k=16)
        p.wait("sync", "din", p.cnt["din"])
        ev_bq = ev_bo = p.now("din")

        ev_w1a = None
        for half in range(2):
            for k in range(DC):
                p.do("gpsimd", lambda e, k=k, half=half: e.dma_start(
                    out=sW1[:, HID * k + 2048 * half:HID * k + 2048 * (half + 1)],
                    in_=w1t[128 * k:128 * (k + 1),
                            2048 * half:2048 * (half + 1)]), inc="dw", k=16)
            p.wait("gpsimd", "dw", p.cnt["dw"])
            if half == 0:
                ev_w1a = p.now("dw")
        ev_w1 = p.now("dw")
        w2_in = w2t.rearrange("(c p) f -> p c f", p=128)
        p.do("gpsimd", lambda e: e.dma_start(out=b1q8[:], in_=b1_pp),
             inc="dw", k=16)
        p.do("gpsimd", lambda e: e.dma_start(out=b2q8[:], in_=b2_pp),
             inc="dw", k=16)
        p.do("gpsimd", lambda e: e.dma_start(out=ebc[:], in_=ebc_d),
             inc="dw", k=16)
        p.wait("gpsimd", "dw", p.cnt["dw"])
        ev_b1q = ev_b2q = ev_rb = ev_ebc = p.now("dw")

        p.wait_ev("vector", ev_b1q)
        p.wait_ev("vector", ev_b2q)
        p.do("vector", lambda e: e.memset(onec_bf[:], 1.0))
        p.do("vector", lambda e: e.memset(onec_f[:], 1.0))
        p.do("vector", lambda e: e.memset(oner_f[:], 1.0))
        p.do("vector", lambda e: e.memset(oner_bf[:], 1.0))
        p.do("vector", lambda e: e.tensor_copy(out=b1qf[:], in_=b1q8[:]))
        ev_binit = p.do("vector", lambda e: e.tensor_copy(
            out=b2qf[:], in_=b2q8[:]), inc="v")

        ln_state = {"statrow_free": None, "ab_free": None}

        def emit_ln_stats(tok0, ntok, s3c, sb=(4, 5), x_wait_din=None):
            m_row = statrow[0:1, 0:ntok]
            veA = statrow[0:1, 512:512 + ntok]
            Brow = statrow[0:1, 1024:1024 + ntok]
            msq = Brow
            nrow = statrow[0:1, 1536:1536 + ntok]
            n2row = Brow
            sb0, sb1 = sb
            p.wait_ev("tensor", ev_binit)
            p.wait_ev("tensor", bank_free[sb0])
            p.wait_ev("tensor", bank_free[sb1])
            if x_wait_din is not None:
                p.wait("tensor", "din", x_wait_din)
                p.wait("vector", "din", x_wait_din)
            ev_sum = None
            for k in range(DC):
                ev_sum = p.do("tensor", lambda e, k=k: e.matmul(
                    pbank[sb0][0:1, 0:ntok], onec_f[:],
                    sX[:, T * k + tok0:T * k + tok0 + ntok],
                    start=(k == 0), stop=(k == DC - 1)),
                    inc=("t" if k == DC - 1 else None))
            ev_sq = [None] * DC
            for k in range(DC):
                s3 = s3c[:, ntok * (k % 2):ntok * (k % 2 + 1)]
                if k >= 2:
                    p.wait_ev("vector", ev_sq[k - 2])
                ev_v = p.do("vector", lambda e, s3=s3, k=k: e.tensor_mul(
                    s3, sX[:, T * k + tok0:T * k + tok0 + ntok],
                    sX[:, T * k + tok0:T * k + tok0 + ntok]), inc="v")
                p.wait_ev("tensor", ev_v)
                ev_sq[k] = p.do("tensor", lambda e, s3=s3, k=k: e.matmul(
                    pbank[sb1][0:1, 0:ntok], onec_f[:], s3,
                    start=(k == 0), stop=(k == DC - 1)), inc="t")
            p.wait_ev("vector", ev_sum)
            p.wait_ev("vector", ev_sq[DC - 1])
            p.wait_ev("vector", ln_state["statrow_free"])
            p.do("vector", lambda e: e.tensor_single_scalar(
                out=m_row, in_=pbank[sb0][0:1, 0:ntok], scalar=INV_D,
                op=ALU.mult))
            p.do("vector", lambda e: e.tensor_scalar(
                out=veA, in0=pbank[sb1][0:1, 0:ntok], scalar1=INV_D,
                scalar2=EPS_LN, op0=ALU.mult, op1=ALU.add))
            p.do("vector", lambda e: e.tensor_mul(msq, m_row, m_row))
            ev_ve = p.do("vector", lambda e: e.tensor_sub(veA, veA, msq),
                         inc="v")
            bank_free[sb0] = ev_ve
            bank_free[sb1] = ev_ve
            p.wait_ev("scalar", ev_ve)
            p.do("scalar", lambda e: e.activation(out=nrow, in_=veA, func=AF.Ln))
            ev_A = p.do("scalar", lambda e: e.activation(
                out=nrow, in_=nrow, func=AF.Exp, scale=-0.5), inc="s")
            p.wait_ev("vector", ev_A)
            p.do("vector", lambda e: e.tensor_mul(n2row, nrow, nrow))
            p.do("vector", lambda e: e.tensor_mul(n2row, n2row, veA))
            p.do("vector", lambda e: e.tensor_scalar(
                out=n2row, in0=n2row, scalar1=-0.5, scalar2=1.5,
                op0=ALU.mult, op1=ALU.add))
            ev_A = p.do("vector", lambda e: e.tensor_mul(veA, nrow, n2row),
                        inc="v")
            ev_B = p.do("vector", lambda e: e.scalar_tensor_tensor(
                out=Brow, in0=veA, scalar=-1.0, in1=m_row,
                op0=ALU.mult, op1=ALU.mult), inc="v")
            return (ev_A, ev_B)

        def emit_ln_apply(tok0, ntok, out_tile, out_col0, tmpf, evAB,
                          bb=(6, 7)):
            veA = statrow[0:1, 512:512 + ntok]
            Brow = statrow[0:1, 1024:1024 + ntok]
            b0, b1 = bb
            ev_A, ev_B = evAB
            p.wait_ev("tensor", bank_free[b0])
            p.wait_ev("tensor", bank_free[b1])
            p.wait_ev("tensor", ev_A)
            p.wait_ev("tensor", ev_B)
            p.do("tensor", lambda e: e.matmul(
                pbank[b0][:, 0:ntok], oner_f[:], veA, start=True, stop=True))
            ev_bc = p.do("tensor", lambda e: e.matmul(
                pbank[b1][:, 0:ntok], oner_f[:], Brow, start=True, stop=True),
                inc="t")
            ln_state["statrow_free"] = ev_bc
            p.wait_ev("scalar", ev_bc)
            p.wait_ev("scalar", ln_state["ab_free"])
            p.do("scalar", lambda e: e.activation(
                out=Abc[:, 0:ntok], in_=pbank[b0][:, 0:ntok], func=AF.Copy))
            ev_ab = p.do("scalar", lambda e: e.activation(
                out=Bbc[:, 0:ntok], in_=pbank[b1][:, 0:ntok], func=AF.Copy),
                inc="s")
            bank_free[b0] = ev_ab
            bank_free[b1] = ev_ab
            p.wait_ev("vector", ev_ab)
            evs = []
            for k in range(DC):
                p.do("vector", lambda e, k=k: e.tensor_mul(
                    tmpf[:, 0:ntok],
                    sX[:, T * k + tok0:T * k + tok0 + ntok],
                    Abc[:, 0:ntok]))
                evs.append(p.do("vector", lambda e, k=k: e.tensor_add(
                    out_tile[:, out_col0 + ntok * k:out_col0 + ntok * (k + 1)],
                    tmpf[:, 0:ntok],
                    Bbc[:, 0:ntok]), inc="v"))
            ln_state["ab_free"] = evs[-1]
            return evs

        def emit_ln_full(tok0, ntok, s3c, tmpf, out_tile, out_col0,
                         sb=(4, 5), bb=(6, 7), x_wait_din=None):
            evAB = emit_ln_stats(tok0, ntok, s3c, sb, x_wait_din)
            return emit_ln_apply(tok0, ntok, out_tile, out_col0, tmpf,
                                 evAB, bb)[-1]

        def emit_ffn(first):
            if not first:
                p.wait("sync", "v", p.cnt["v"])
                p.wait("sync", "t", p.cnt["t"])
            esf = contextlib.ExitStack()
            with esf:
                sW2 = sbuf("sW2F" + str(first), [128, HC * D], F8, esf)
                mask2 = sbuf("maskF" + str(first), [128, DC * ST], F8, esf)
                xnq = sbuf("xnqF" + str(first), [128, 2 * DC * ST], F8, esf)
                hbf = sbuf("hbfF" + str(first), [128, 2 * 512], BF16, esf)
                hq8 = sbuf("hq8F" + str(first), [128, HC * ST], F8, esf)
                y2b = sbuf("y2bF" + str(first), [128, DC * 512], BF16, esf)
                s3c = sbuf("s3F" + str(first), [128, 2 * 512], F32, esf)
                tmpf = s3c

                if not first:
                    p.wait("gpsimd", "v", p.cnt["v"])
                    p.wait("gpsimd", "t", p.cnt["t"])
                    p.wait("tensor", "v", p.cnt["v"])
                ev_w2q = []
                for q in range(4):
                    p.do("gpsimd", lambda e, q=q: e.dma_start(
                        out=sW2[:, D * 8 * q:D * 8 * (q + 1)]
                            .rearrange("p (c f) -> p c f", c=8),
                        in_=w2_in[:, 8 * q:8 * (q + 1), :]), inc="dw", k=16)
                    p.wait("gpsimd", "dw", p.cnt["dw"])
                    ev_w2q.append(p.now("dw"))

                mask_free = [None, None]
                mm1_last = [None, None]
                hq8_free = None
                hbf_free = [None, None]
                y2_free = [None] * DC

                evAB = emit_ln_stats(0, ST, s3c,
                                     x_wait_din=(x_ready[0] if first else None))
                apply_evs = emit_ln_apply(0, ST, xnq, 0, tmpf, evAB)

                sW1_3 = sW1[:].rearrange("p (c f) -> p c f", c=DC)
                sW2_3 = sW2[:].rearrange("p (c f) -> p c f", c=HC)
                hq8_3 = hq8[:].rearrange("p (c f) -> p c f", c=HC)
                ev_res = None
                for s in range(NST):
                    msl = s % 2
                    p.wait_ev("sync", mask_free[0])
                    p.do("sync", lambda e, s=s: e.dma_start(
                        out=mask2[:].rearrange("p (c f) -> p c f", c=DC),
                        in_=maskT.rearrange("(c p) f -> p c f", p=128)
                            [:, :, ST * s:ST * (s + 1)]), inc="din", k=16)
                    p.wait("sync", "din", p.cnt["din"])
                    ev_mask = p.now("din")

                    xnq_3 = xnq[:, DC * ST * msl:DC * ST * (msl + 1)].rearrange(
                        "p (c f) -> p c f", c=DC)
                    ev_cast32 = None
                    for m in range(HC):
                        b = m % 2
                        p.wait_ev("tensor", bank_free[b])
                        p.wait_ev("tensor", ev_w1a if m < HC // 2 else ev_w1)
                        if DR_MM1:
                            for k in range(DC // 2):
                                if m == 0:
                                    p.wait_ev("tensor", apply_evs[2 * k + 1])
                                evg = p.do("tensor", lambda e, m=m, k=k, b=b, msl=msl: e.matmul(
                                    pbank[b][:],
                                    sW1_3[:, 2 * k:2 * k + 2, 128 * m:128 * (m + 1)],
                                    xnq_3[:, 2 * k:2 * k + 2, :],
                                    start=(k == 0), stop=(k == DC // 2 - 1),
                                    perf_mode=DRPM),
                                    inc=("t" if k == DC // 2 - 1 else None))
                        else:
                            for k in range(DC):
                                if m == 0:
                                    p.wait_ev("tensor", apply_evs[k])
                                evg = p.do("tensor", lambda e, m=m, k=k, b=b, msl=msl: e.matmul(
                                    pbank[b][:],
                                    sW1[:, HID * k + 128 * m:HID * k + 128 * (m + 1)],
                                    xnq[:, DC * ST * msl + ST * k:DC * ST * msl + ST * (k + 1)],
                                    start=(k == 0), stop=(k == DC - 1)),
                                    inc=("t" if k == DC - 1 else None))
                        hsl = m % 2
                        p.wait_ev("scalar", evg)
                        p.wait_ev("scalar", hbf_free[hsl])
                        if first and m == 0 and s == 0:
                            p.wait_ev("scalar", ev_binit)
                        ev_ev = p.do("scalar", lambda e, m=m, b=b, hsl=hsl: e.activation(
                            out=hbf[:, 512 * hsl:512 * (hsl + 1)],
                            in_=pbank[b][:], func=AF.Relu,
                            bias=b1qf[:, m:m + 1]), inc="s")
                        bank_free[b] = ev_ev
                        p.wait_ev("vector", ev_ev)
                        if hq8_free is not None and m == 0:
                            p.wait_ev("vector", hq8_free)
                        ev_cast32 = p.do("vector", lambda e, m=m, hsl=hsl: e.tensor_copy(
                            out=hq8[:, ST * m:ST * (m + 1)],
                            in_=hbf[:, 512 * hsl:512 * (hsl + 1)]), inc="v")
                        hbf_free[hsl] = ev_cast32
                    mm1_last[msl] = evg

                    if s + 1 < NST:
                        evAB = emit_ln_stats(
                            ST * (s + 1), ST, s3c,
                            x_wait_din=(x_ready[s + 1] if first else None))

                    y_evs = []
                    for n in range(DC):
                        b = 2 + n % 2
                        p.wait_ev("tensor", bank_free[b])
                        p.wait_ev("tensor", ev_cast32)
                        if DR_MM2 and not first:
                            for j in range(HC // 2):
                                if s == 0:
                                    p.wait_ev("tensor", ev_w2q[(2 * j + 1) // 8])
                                evg = p.do("tensor", lambda e, n=n, j=j, b=b: e.matmul(
                                    pbank[b][:],
                                    sW2_3[:, 2 * j:2 * j + 2, 128 * n:128 * (n + 1)],
                                    hq8_3[:, 2 * j:2 * j + 2, :],
                                    start=(j == 0), stop=(j == HC // 2 - 1),
                                    perf_mode=DRPM),
                                    inc=("t" if j == HC // 2 - 1 else None))
                        else:
                            for j in range(HC):
                                if s == 0:
                                    p.wait_ev("tensor", ev_w2q[j // 8])
                                evg = p.do("tensor", lambda e, n=n, j=j, b=b: e.matmul(
                                    pbank[b][:],
                                    sW2[:, D * j + 128 * n:D * j + 128 * (n + 1)],
                                    hq8[:, ST * j:ST * (j + 1)],
                                    start=(j == 0), stop=(j == HC - 1)),
                                    inc=("t" if j == HC - 1 else None))
                        if n == DC - 1:
                            hq8_free = evg
                        p.wait_ev("scalar", evg)
                        p.wait_ev("scalar", y2_free[n])
                        ev_y = p.do("scalar", lambda e, n=n, b=b: e.activation(
                            out=y2b[:, 512 * n:512 * (n + 1)],
                            in_=pbank[b][:], func=AF.Identity,
                            bias=b2qf[:, n:n + 1]), inc="s")
                        bank_free[b] = ev_y
                        y_evs.append(ev_y)

                    if s + 1 < NST:
                        p.wait_ev("vector", mm1_last[(s + 1) % 2])
                        apply_evs = emit_ln_apply(
                            ST * (s + 1), ST, xnq, DC * ST * ((s + 1) % 2),
                            tmpf, evAB)

                    for n in range(DC):
                        p.wait_ev("vector", y_evs[n])
                        if n == 0:
                            p.wait_ev("vector", ev_mask)
                        p.do("vector", lambda e, n=n: e.tensor_mul(
                            tmpf[:, 0:512],
                            y2b[:, 512 * n:512 * (n + 1)],
                            mask2[:, ST * n:ST * (n + 1)]))
                        ev_res = p.do("vector", lambda e, n=n, s=s: e.tensor_add(
                            sX[:, T * n + ST * s:T * n + ST * (s + 1)],
                            sX[:, T * n + ST * s:T * n + ST * (s + 1)],
                            tmpf[:, 0:512]), inc="v")
                        y2_free[n] = ev_res
                        if n == DC - 1:
                            mask_free[0] = ev_res
            return ev_res

        ev_ffn1 = emit_ffn(first=True) if "ffn1" in phases else None

        wk_in = wkt.rearrange("(c p) f -> p c f", p=128)
        p.do("gpsimd", lambda e: e.dma_start(
            out=wslotA[:].rearrange("p (c f) -> p c f", c=DC), in_=wk_in),
            inc="dw", k=16)
        p.wait("gpsimd", "dw", p.cnt["dw"])
        ev_wk = p.now("dw")

        if "attn" in phases:
            es_a = contextlib.ExitStack()
            with es_a:
                es_kv = contextlib.ExitStack()
                with es_kv:
                    wslotB = sbuf("wslotB", [128, DC * D], BF16, es_kv)
                    rbrow = sbuf("rbrow", [1, 2 * D], F32, es_kv)
                    xn2 = sbuf("xn2", [128, 2 * DC * ST], BF16, es_kv)
                    ktm = sbuf("ktm", [128, D], BF16, es_kv)
                    v65 = sbuf("v65", [128, NH * 65], BF16, es_kv)
                    s3A = sbuf("s3A", [128, 2 * 512], F32, es_kv)
                    tmpa = sbuf("tmpa", [128, 512], F32, es_kv)
                    tmpb = sbuf("tmpb", [128, 512], F32, es_kv)

                    p.wait_ev("sync", ev_ffn1)
                    p.wait("sync", "t", p.cnt["t"])
                    p.do("sync", lambda e: e.dma_start(out=rbrow[:], in_=rb_d),
                         inc="din", k=16)
                    p.wait("sync", "din", p.cnt["din"])
                    ev_rbl = p.now("din")
                    p.wait_ev("gpsimd", ev_ffn1)
                    p.wait("gpsimd", "t", p.cnt["t"])
                    wv_in = wvt.rearrange("(c p) f -> p c f", p=128)
                    p.do("gpsimd", lambda e: e.dma_start(
                        out=wslotB[:].rearrange("p (c f) -> p c f", c=DC),
                        in_=wv_in), inc="dw", k=16)
                    p.wait("gpsimd", "dw", p.cnt["dw"])
                    ev_wv = p.now("dw")

                    p.do("vector", lambda e: e.memset(
                        v65[:].rearrange("p (h c) -> p h c", c=65)[:, :, 64:65], 1.0))

                    ktm_free = None
                    v65_free = None
                    ev_kvadd = None
                    xn2_last = [None, None]
                    p.wait("tensor", "v", p.cnt["v"])
                    evAB = emit_ln_stats(0, ST, s3A)
                    apply_evs = emit_ln_apply(0, ST, xn2, 0, s3A, evAB)
                    for s in range(NST):
                        msl = s % 2
                        xcol = DC * ST * msl
                        for t4 in range(4):
                            t_ = 4 * s + t4
                            ev_kgrp = [None, None]
                            ev_vgrp = [None, None]
                            for proj in range(2):
                                for half in range(2):
                                    b = 2 * proj + half
                                    p.wait_ev("tensor", bank_free[b])
                                    if t_ == 0 and half == 0:
                                        p.wait_ev("tensor",
                                                  ev_wk if proj == 0 else ev_wv)
                                        p.wait_ev("tensor", ev_rbl)
                                    wsl = wslotA if proj == 0 else wslotB
                                    for k in range(DC):
                                        if t4 == 0 and proj == 0:
                                            p.wait_ev("tensor", apply_evs[k])
                                        p.do("tensor", lambda e, k=k, b=b, t4=t4, half=half, wsl=wsl, xcol=xcol: e.matmul(
                                            pbank[b][:],
                                            xn2[:, xcol + ST * k + 128 * t4:xcol + ST * k + 128 * (t4 + 1)],
                                            wsl[:, D * k + 512 * half:D * k + 512 * (half + 1)],
                                            start=(k == 0), stop=False))
                                    evg = p.do("tensor", lambda e, b=b, half=half, proj=proj: e.matmul(
                                        pbank[b][:], oner_f[:],
                                        rbrow[0:1, D * proj + 512 * half:D * proj + 512 * (half + 1)],
                                        start=False, stop=True), inc="t")
                                    if proj == 0:
                                        ev_kgrp[half] = evg
                                    else:
                                        ev_vgrp[half] = evg
                            ev_k = None
                            for half in range(2):
                                b = half
                                p.wait_ev("vector", ev_kgrp[half])
                                p.wait_ev("vector", ktm_free)
                                p.do("vector", lambda e, b=b: e.tensor_scalar_max(
                                    out=tmpa[:], in0=pbank[b][:], scalar1=0.0))
                                ev_mn = p.do("vector", lambda e, b=b: e.tensor_scalar_min(
                                    out=tmpb[:], in0=pbank[b][:], scalar1=0.0),
                                    inc="v")
                                p.wait_ev("scalar", ev_mn)
                                ev_ex = p.do("scalar", lambda e: e.activation(
                                    out=tmpb[:], in_=tmpb[:], func=AF.Exp),
                                    inc="s")
                                p.wait_ev("vector", ev_ex)
                                ev_k = p.do("vector", lambda e, half=half: e.tensor_add(
                                    ktm[:, 512 * half:512 * (half + 1)],
                                    tmpa[:], tmpb[:]), inc="v")
                                bank_free[b] = ev_k
                            ev_v = None
                            for half in range(2):
                                b = 2 + half
                                p.wait_ev("vector", ev_vgrp[half])
                                p.wait_ev("vector", v65_free)
                                ev_v = p.do("vector", lambda e, b=b, half=half: e.tensor_copy(
                                    out=v65[:].rearrange("p (h c) -> p h c", c=65)
                                        [:, 8 * half:8 * (half + 1), 0:64],
                                    in_=pbank[b][:].rearrange("p (h c) -> p h c", c=64)),
                                    inc="v")
                                bank_free[b] = ev_v
                            p.wait_ev("tensor", ev_k)
                            p.wait_ev("tensor", ev_v)
                            for q in range(4, 8):
                                p.wait_ev("tensor", bank_free[q])
                            evkv = None
                            for h in range(NH):
                                bkn = 4 + h // 4
                                r0 = 64 * (h % 2)
                                c0 = 65 * ((h % 4) // 2)
                                evkv = p.do("tensor", lambda e, h=h, bkn=bkn, r0=r0, c0=c0: e.matmul(
                                    pbank[bkn][r0:r0 + 64, c0:c0 + 65],
                                    ktm[:, HD * h:HD * (h + 1)],
                                    v65[:, 65 * h:65 * (h + 1)],
                                    start=True, stop=True,
                                    tile_position=(0, r0)),
                                    inc=("t" if h == NH - 1 else None))
                            ktm_free = evkv
                            v65_free = evkv
                            if t4 == 3:
                                xn2_last[msl] = evkv
                            p.wait_ev("vector", evkv)
                            for q in range(4):
                                if t_ == 0:
                                    ev_kvadd = p.do("vector", lambda e, q=q: e.tensor_copy(
                                        out=skv[:, 130 * q:130 * (q + 1)],
                                        in_=pbank[4 + q][:, 0:130]), inc="v")
                                else:
                                    ev_kvadd = p.do("vector", lambda e, q=q: e.tensor_add(
                                        skv[:, 130 * q:130 * (q + 1)],
                                        skv[:, 130 * q:130 * (q + 1)],
                                        pbank[4 + q][:, 0:130]), inc="v")
                            for q in range(4, 8):
                                bank_free[q] = ev_kvadd
                        if s + 1 < NST:
                            evAB = emit_ln_stats(ST * (s + 1), ST, s3A)
                            p.wait_ev("vector", xn2_last[(s + 1) % 2])
                            apply_evs = emit_ln_apply(
                                ST * (s + 1), ST, xn2, DC * ST * ((s + 1) % 2),
                                s3A, evAB)

                    p.wait_ev("gpsimd", ev_kvadd)
                    p.do("gpsimd", lambda e: e.dma_start(
                        out=cc_in[:, :], in_=skv[:]), inc="dw", k=16)
                    p.wait("gpsimd", "dw", p.cnt["dw"])
                    p.do("gpsimd", lambda e: e.collective_compute(
                        "AllReduce", ALU.add,
                        replica_groups=[[0, 1], [2, 3], [4, 5], [6, 7]],
                        ins=[cc_in.ap().opt()], outs=[cc_out.ap().opt()]),
                        inc="cc")
                    p.wait("gpsimd", "t", p.cnt["t"])
                    wq_in = wqt.rearrange("(c p) f -> p c f", p=128)
                    p.do("gpsimd", lambda e: e.dma_start(
                        out=wslotA[:].rearrange("p (c f) -> p c f", c=DC),
                        in_=wq_in), inc="dw", k=16)
                    p.wait("gpsimd", "dw", p.cnt["dw"])
                    ev_wq = p.now("dw")
                    p.wait("gpsimd", "cc", 1)
                    p.do("gpsimd", lambda e: e.dma_start(
                        out=skv[:], in_=cc_out[:, :]), inc="dw", k=16)
                    p.wait("gpsimd", "dw", p.cnt["dw"])
                    ev_kvtot = p.now("dw")

                es_q = contextlib.ExitStack()
                with es_q:
                    wslotO = sbuf("wslotO", [128, DC * D], F32, es_q)
                    xn2q = sbuf("xn2q", [128, DC * QT], BF16, es_q)
                    qphi = sbuf("qphi", [128, DC * QT], BF16, es_q)
                    outpre = sbuf("outpre", [128, DC * QT], F32, es_q)
                    kvbd = sbuf("kvbd", [128, DC * 128], BF16, es_q)
                    zbd = sbuf("zbd", [128, DC * 16], BF16, es_q)
                    tmpn = sbuf("tmpn", [16, QT], F32, es_q)
                    s3Q = sbuf("s3Q", [128, 2 * QT], F32, es_q)
                    tmpaQ = sbuf("tmpaQ", [128, QT], F32, es_q)
                    tmpeQ = sbuf("tmpeQ", [128, QT], F32, es_q)

                    p.wait("sync", "t", p.cnt["t"])
                    p.wait("sync", "v", p.cnt["v"])
                    p.wait_ev("sync", ev_wq)
                    for k in range(DC):
                        p.do("sync", lambda e, k=k: e.dma_start(
                            out=wslotO[:, D * k:D * (k + 1)],
                            in_=wot[128 * k:128 * (k + 1), :]), inc="din", k=16)
                    p.wait("sync", "din", p.cnt["din"])
                    ev_wo = p.now("din")

                    p.wait_ev("vector", ev_kvtot)
                    p.do("vector", lambda e: e.memset(kvbd[:], 0.0))
                    p.do("vector", lambda e: e.memset(zbd[:], 0.0))
                    ev_bd = None
                    for h in range(NH):
                        r0 = 64 * (h % 2)
                        c0 = 130 * (h // 4) + 65 * ((h % 4) // 2)
                        blk = h // 2
                        p.do("vector", lambda e, h=h, r0=r0, c0=c0, blk=blk: e.tensor_copy(
                            out=kvbd[r0:r0 + 64,
                                     128 * blk + 64 * (h % 2):128 * blk + 64 * (h % 2) + 64],
                            in_=skv[r0:r0 + 64, c0:c0 + 64]))
                        ev_bd = p.do("vector", lambda e, h=h, r0=r0, c0=c0, blk=blk: e.tensor_copy(
                            out=zbd[r0:r0 + 64, 16 * blk + h:16 * blk + h + 1],
                            in_=skv[r0:r0 + 64, c0 + 64:c0 + 65]), inc="v")

                    ev_res_attn = None
                    qphi_free = None
                    outpre_free = None
                    qproj_last = None
                    evABq = emit_ln_stats(0, QT, s3Q)
                    apply_evq = emit_ln_apply(0, QT, xn2q, 0, tmpaQ, evABq)
                    for s in range(NQT):
                        ev_qphi = None
                        for n in range(DC):
                            b = 4 + n % 2
                            p.wait_ev("tensor", bank_free[b])
                            if s == 0 and n == 0:
                                p.wait_ev("tensor", ev_wq)
                            for k in range(DC):
                                if n == 0:
                                    p.wait_ev("tensor", apply_evq[k])
                                evg = p.do("tensor", lambda e, n=n, k=k, b=b: e.matmul(
                                    pbank[b][:, 0:QT],
                                    wslotA[:, D * k + 128 * n:D * k + 128 * (n + 1)],
                                    xn2q[:, QT * k:QT * (k + 1)],
                                    start=(k == 0), stop=(k == DC - 1)),
                                    inc=("t" if k == DC - 1 else None))
                            if n == DC - 1:
                                qproj_last = evg
                            p.wait_ev("vector", evg)
                            p.wait_ev("vector", qphi_free)
                            ev_mn = p.do("vector", lambda e, n=n, b=b: e.tensor_scalar(
                                out=tmpaQ[:], in0=pbank[b][:, 0:QT],
                                scalar1=bq_pp[:, n:n + 1], scalar2=0.0,
                                op0=ALU.add, op1=ALU.min), inc="v")
                            p.wait_ev("scalar", ev_mn)
                            p.wait_ev("scalar", ev_bq)
                            ev_ex = p.do("scalar", lambda e: e.activation(
                                out=tmpeQ[:], in_=tmpaQ[:], func=AF.Exp), inc="s")
                            p.wait_ev("vector", ev_ex)
                            p.do("vector", lambda e, n=n, b=b: e.tensor_scalar(
                                out=tmpaQ[:], in0=pbank[b][:, 0:QT],
                                scalar1=bq_pp[:, n:n + 1], scalar2=0.0,
                                op0=ALU.add, op1=ALU.max))
                            ev_qphi = p.do("vector", lambda e, n=n: e.tensor_add(
                                qphi[:, QT * n:QT * (n + 1)], tmpaQ[:], tmpeQ[:]),
                                inc="v")
                            bank_free[b] = ev_qphi
                        p.wait_ev("tensor", bank_free[6])
                        p.wait_ev("tensor", ev_bd)
                        p.wait_ev("tensor", ev_qphi)
                        for k in range(DC):
                            evg = p.do("tensor", lambda e, k=k: e.matmul(
                                pbank[6][0:16, 0:QT], zbd[:, 16 * k:16 * (k + 1)],
                                qphi[:, QT * k:QT * (k + 1)],
                                start=(k == 0), stop=(k == DC - 1)),
                                inc=("t" if k == DC - 1 else None))
                        p.wait_ev("vector", evg)
                        p.do("vector", lambda e: e.tensor_single_scalar(
                            out=tmpn[:], in_=pbank[6][0:16, 0:QT],
                            scalar=EPS_ATTN, op=ALU.add))
                        ev_rec = p.do("vector", lambda e: e.reciprocal(
                            out=tmpn[:], in_=tmpn[:]), inc="v")
                        bank_free[6] = ev_rec
                        ev_op = None
                        for k in range(DC):
                            b = 4 + k % 2
                            br = 6 + k % 2
                            p.wait_ev("tensor", bank_free[b])
                            p.wait_ev("tensor", bank_free[br])
                            p.wait_ev("tensor", ev_rec)
                            evg = p.do("tensor", lambda e, k=k, b=b: e.matmul(
                                pbank[b][:, 0:QT], kvbd[:, 128 * k:128 * (k + 1)],
                                qphi[:, QT * k:QT * (k + 1)],
                                start=True, stop=True))
                            evg = p.do("tensor", lambda e, k=k, br=br: e.matmul(
                                pbank[br][:, 0:QT], ebc[:, 128 * k:128 * (k + 1)],
                                tmpn[:], start=True, stop=True), inc="t")
                            p.wait_ev("vector", evg)
                            p.wait_ev("vector", outpre_free)
                            p.do("vector", lambda e, k=k, b=b: e.tensor_copy(
                                out=outpre[:, QT * k:QT * (k + 1)],
                                in_=pbank[b][:, 0:QT]))
                            ev_op = p.do("vector", lambda e, k=k, br=br: e.tensor_mul(
                                outpre[:, QT * k:QT * (k + 1)],
                                outpre[:, QT * k:QT * (k + 1)],
                                pbank[br][:, 0:QT]), inc="v")
                            bank_free[b] = ev_op
                            bank_free[br] = ev_op
                        qphi_free = ev_op
                        if s + 1 < NQT:
                            evABq = emit_ln_stats(QT * (s + 1), QT, s3Q)
                            p.wait_ev("vector", qproj_last)
                            apply_evq = emit_ln_apply(
                                QT * (s + 1), QT, xn2q, 0, tmpaQ, evABq)
                        for n in range(DC):
                            b = n % 2
                            p.wait_ev("tensor", bank_free[b])
                            p.wait_ev("tensor", ev_op)
                            if s == 0 and n == 0:
                                p.wait_ev("tensor", ev_wo)
                            for k in range(DC):
                                evg = p.do("tensor", lambda e, n=n, k=k, b=b: e.matmul(
                                    pbank[b][:, 0:QT],
                                    wslotO[:, D * k + 128 * n:D * k + 128 * (n + 1)],
                                    outpre[:, QT * k:QT * (k + 1)],
                                    start=(k == 0), stop=(k == DC - 1)),
                                    inc=("t" if k == DC - 1 else None))
                            p.wait_ev("vector", evg)
                            p.wait_ev("vector", ev_bo)
                            ev_res_attn = p.do("vector", lambda e, n=n, s=s, b=b: e.scalar_tensor_tensor(
                                out=sX[:, T * n + QT * s:T * n + QT * (s + 1)],
                                in0=pbank[b][:, 0:QT], scalar=bo_pp[:, n:n + 1],
                                in1=sX[:, T * n + QT * s:T * n + QT * (s + 1)],
                                op0=ALU.add, op1=ALU.add), inc="v")
                            bank_free[b] = ev_res_attn
                        outpre_free = evg

        ev_ffn2 = emit_ffn(first=False) if "ffn2" in phases else None

        p.wait_ev("sync", ev_ffn2)
        p.wait("sync", "v", p.cnt["v"])
        ndo = 0
        for k in range(DC):
            p.do("sync", lambda e, k=k: e.dma_start(
                out=outT[128 * k:128 * (k + 1), :],
                in_=sX[:, T * k:T * (k + 1)]), inc="do", k=16)
            ndo += 16
        p.wait("sync", "do", ndo)

        with nc.Block() as block:
            for eng_name, deco in (("sync", block.sync), ("vector", block.vector),
                                   ("scalar", block.scalar), ("tensor", block.tensor),
                                   ("gpsimd", block.gpsimd)):
                ops = p.ops[eng_name]
                def run(e, ops=ops):
                    for f in ops:
                        f(e)
                deco(run)

    return nc


def _threefry2x32(k0, k1, x0, x1):
    x0 = x0.astype(np.uint32).copy()
    x1 = x1.astype(np.uint32).copy()
    ks0, ks1 = np.uint32(k0), np.uint32(k1)
    ks2 = np.uint32(ks0 ^ ks1 ^ np.uint32(0x1BD11BDA))
    rots = [(13, 15, 26, 6), (17, 29, 16, 24)]
    with np.errstate(over="ignore"):
        x0 = (x0 + ks0).astype(np.uint32)
        x1 = (x1 + ks1).astype(np.uint32)
        ks = [ks1, ks2, ks0]
        for r in range(5):
            for d in rots[r % 2]:
                x0 = (x0 + x1).astype(np.uint32)
                x1 = ((x1 << np.uint32(d)) | (x1 >> np.uint32(32 - d))).astype(np.uint32)
                x1 = (x1 ^ x0).astype(np.uint32)
            x0 = (x0 + ks[r % 3]).astype(np.uint32)
            x1 = (x1 + ks[(r + 1) % 3] + np.uint32(r + 1)).astype(np.uint32)
    return x0, x1


def _host_constants():
    n = 4 * 4096 * 1024
    i = np.arange(n, dtype=np.uint64)
    b1, b2 = _threefry2x32(0, 42, (i >> np.uint64(32)).astype(np.uint32),
                           (i & np.uint64(0xFFFFFFFF)).astype(np.uint32))
    bits = (b1 ^ b2).astype(np.uint32)
    u = ((bits >> np.uint32(9)) | np.uint32(0x3F800000)).view(np.float32)         - np.float32(1.0)
    keep = (u < np.float32(0.8)).reshape(4, 4096, 1024)
    mask = np.where(keep, np.float32(0.625), np.float32(0.0))
    ebc = np.zeros((16, 1024), np.float32)
    for h in range(16):
        for k in range(8):
            if h // 2 == k:
                ebc[h, 128 * k + 64 * (h % 2):128 * k + 64 * (h % 2) + 64] = 1.0
    return mask, ebc


def kernel(L0_attn_x, L0_attn_Wq, L0_attn_bq, L0_attn_Wk, L0_attn_bk,
           L0_attn_Wv, L0_attn_bv, L0_attn_Wo, L0_attn_bo,
           L0_ffn_W1, L0_ffn_b1, L0_ffn_W2, L0_ffn_b2):
    x = np.asarray(L0_attn_x, np.float32)
    f8 = ml_dtypes.float8_e5m2

    mask, ebc = _host_constants()

    w1t = np.ascontiguousarray(np.asarray(L0_ffn_W1, np.float32).T)
    w2t = np.ascontiguousarray(np.asarray(L0_ffn_W2, np.float32).T)
    wqt = np.ascontiguousarray(np.asarray(L0_attn_Wq, np.float32).T)
    wkt = np.ascontiguousarray(np.asarray(L0_attn_Wk, np.float32).T)
    wvt = np.ascontiguousarray(np.asarray(L0_attn_Wv, np.float32).T)
    wot = np.ascontiguousarray(np.asarray(L0_attn_Wo, np.float32).T)
    b1_pp = np.ascontiguousarray(np.asarray(L0_ffn_b1, np.float32).reshape(HC, 128).T)
    b2_pp = np.ascontiguousarray(np.asarray(L0_ffn_b2, np.float32).reshape(DC, 128).T)
    bq_pp = np.ascontiguousarray(np.asarray(L0_attn_bq, np.float32).reshape(DC, 128).T)
    bo_pp = np.ascontiguousarray(np.asarray(L0_attn_bo, np.float32).reshape(DC, 128).T)
    rb = np.concatenate([np.asarray(L0_attn_bk, np.float32),
                         np.asarray(L0_attn_bv, np.float32)]).reshape(1, 2 * D)
    rb = np.ascontiguousarray(rb)

    nc = build_nc()
    in_maps = []
    for c in range(8):
        b, h = c // 2, c % 2
        xc = np.ascontiguousarray(x[b, T * h:T * (h + 1), :].T)
        mc = np.ascontiguousarray(mask[b, T * h:T * (h + 1), :].T).astype(f8)
        in_maps.append({
            "xT": xc, "w1t": w1t, "w2t": w2t, "wqt": wqt, "wkt": wkt,
            "wvt": wvt, "wot": wot, "b1_pp": b1_pp, "b2_pp": b2_pp,
            "bq_d": bq_pp, "bo_d": bo_pp, "rb_d": rb, "ebc_d": ebc,
            "maskT": mc,
        })
    res = run_bass_kernel_spmd(nc, in_maps, core_ids=list(range(8)))
    out = np.empty((4, 4096, 1024), np.float32)
    for c in range(8):
        b, h = c // 2, c % 2
        out[b, T * h:T * (h + 1), :] = np.asarray(res.results[c]["outT"]).T
    return out


# version 09
# speedup vs baseline: 1.1701x; 1.0044x over previous
import contextlib

import numpy as np
import ml_dtypes

import concourse.bass as bass
import concourse.mybir as mybir
from concourse.bass_utils import run_bass_kernel_spmd

F32 = mybir.dt.float32
BF16 = mybir.dt.bfloat16
F8 = mybir.dt.float8e5
AF = mybir.ActivationFunctionType
ALU = mybir.AluOpType

D = 1024
DC = 8
HID = 4096
HC = 32
T = 2048
ST = 512
NST = T // ST
QT = 256
NQT = T // QT
NH = 16
HD = 64
EPS_LN = 1e-5
EPS_ATTN = 1e-6
INV_D = 1.0 / D
DRPM = mybir.MatmulPerfMode.DoubleRow
DR_MM1 = False
DR_MM2 = True

ENGS = ("sync", "vector", "scalar", "tensor", "gpsimd")


class Prog:

    def __init__(self):
        self.ops = {e: [] for e in ENGS}
        self.sems = {}
        self.cnt = {}

    def add_sem(self, name, handle):
        self.sems[name] = handle
        self.cnt[name] = 0

    def wait(self, eng, sname, val):
        if val <= 0:
            return
        h = self.sems[sname]
        self.ops[eng].append(lambda e, h=h, v=val: e.wait_ge(h, v))

    def wait_ev(self, eng, ev):
        if ev is not None:
            self.wait(eng, ev[0], ev[1])

    def do(self, eng, fn, inc=None, k=1):
        if inc is None:
            self.ops[eng].append(fn)
            return None
        h = self.sems[inc]
        self.cnt[inc] += k
        def g(e, fn=fn, h=h, k=k):
            fn(e).then_inc(h, k)
        self.ops[eng].append(g)
        return (inc, self.cnt[inc])

    def now(self, sname):
        return (sname, self.cnt[sname])


def build_nc(phases=("ffn1", "attn", "ffn2")):
    nc = bass.Bass("TRN2", target_bir_lowering=False, debug=False,
                   detect_race_conditions=False)

    xT = nc.dram_tensor("xT", [D, T], F32, kind="ExternalInput").ap()
    w1t = nc.dram_tensor("w1t", [D, HID], F32, kind="ExternalInput").ap()
    w2t = nc.dram_tensor("w2t", [HID, D], F32, kind="ExternalInput").ap()
    wqt = nc.dram_tensor("wqt", [D, D], F32, kind="ExternalInput").ap()
    wkt = nc.dram_tensor("wkt", [D, D], F32, kind="ExternalInput").ap()
    wvt = nc.dram_tensor("wvt", [D, D], F32, kind="ExternalInput").ap()
    wot = nc.dram_tensor("wot", [D, D], F32, kind="ExternalInput").ap()
    b1_pp = nc.dram_tensor("b1_pp", [128, HC], F32, kind="ExternalInput").ap()
    b2_pp = nc.dram_tensor("b2_pp", [128, DC], F32, kind="ExternalInput").ap()
    bq_d = nc.dram_tensor("bq_d", [128, DC], F32, kind="ExternalInput").ap()
    bo_d = nc.dram_tensor("bo_d", [128, DC], F32, kind="ExternalInput").ap()
    rb_d = nc.dram_tensor("rb_d", [1, 2 * D], F32, kind="ExternalInput").ap()
    ebc_d = nc.dram_tensor("ebc_d", [16, D], F32, kind="ExternalInput").ap()
    maskT = nc.dram_tensor("maskT", [D, T], F8, kind="ExternalInput").ap()
    outT = nc.dram_tensor("outT", [D, T], F32, kind="ExternalOutput").ap()
    cc_in = nc.dram_tensor("cc_in", [128, 520], F32)
    cc_out = nc.dram_tensor("cc_out", [128, 520], F32)

    p = Prog()
    es0 = contextlib.ExitStack()
    with es0:
        def sbuf(name, shape, dt, stack=es0):
            return stack.enter_context(nc.sbuf_tensor(name, shape, dt))

        def sem(name):
            h = es0.enter_context(nc.semaphore(name))
            p.add_sem(name, h)
            return h

        sX = sbuf("sX", [128, DC * T], F32)
        sW1 = sbuf("sW1", [128, DC * HID], F8)
        wslotA = sbuf("wslotA", [128, DC * D], BF16)
        b1q8 = sbuf("b1q8", [128, HC], F8)
        b2q8 = sbuf("b2q8", [128, DC], F8)
        b1qf = sbuf("b1qf", [128, HC], F32)
        b2qf = sbuf("b2qf", [128, DC], F32)
        bq_pp = sbuf("bq_pp", [128, DC], F32)
        bo_pp = sbuf("bo_pp", [128, DC], F32)
        ebc = sbuf("ebc", [16, D], F32)
        onec_bf = sbuf("onec_bf", [128, 1], BF16)
        onec_f = sbuf("onec_f", [128, 1], F32)
        oner_f = sbuf("oner_f", [1, 128], F32)
        oner_bf = sbuf("oner_bf", [1, 128], BF16)
        skv = sbuf("skv", [128, 520], F32)
        statrow = sbuf("statrow", [1, 4 * 512], F32)
        Abc = sbuf("Abc", [128, 512], F32)
        Bbc = sbuf("Bbc", [128, 512], F32)

        for s in ("din", "dw", "v", "s", "t", "g", "do", "cc"):
            sem(s)

        pbank = [es0.enter_context(nc.psum_tensor(f"pb{i}", [128, 512], F32))
                 for i in range(8)]
        bank_free = [None] * 8

        x_ready = []
        for s in range(NST):
            for k in range(DC):
                p.do("sync", lambda e, k=k, s=s: e.dma_start(
                    out=sX[:, T * k + ST * s: T * k + ST * (s + 1)],
                    in_=xT[128 * k:128 * (k + 1), ST * s:ST * (s + 1)]),
                    inc="din", k=16)
            x_ready.append(p.cnt["din"])
            p.wait("sync", "din", p.cnt["din"])

        p.do("sync", lambda e: e.dma_start(out=bq_pp[:], in_=bq_d),
             inc="din", k=16)
        p.do("sync", lambda e: e.dma_start(out=bo_pp[:], in_=bo_d),
             inc="din", k=16)
        p.wait("sync", "din", p.cnt["din"])
        ev_bq = ev_bo = p.now("din")

        ev_w1a = None
        for half in range(2):
            for k in range(DC):
                p.do("gpsimd", lambda e, k=k, half=half: e.dma_start(
                    out=sW1[:, HID * k + 2048 * half:HID * k + 2048 * (half + 1)],
                    in_=w1t[128 * k:128 * (k + 1),
                            2048 * half:2048 * (half + 1)]), inc="dw", k=16)
            p.wait("gpsimd", "dw", p.cnt["dw"])
            if half == 0:
                ev_w1a = p.now("dw")
        ev_w1 = p.now("dw")
        w2_in = w2t.rearrange("(c p) f -> p c f", p=128)
        p.do("gpsimd", lambda e: e.dma_start(out=b1q8[:], in_=b1_pp),
             inc="dw", k=16)
        p.do("gpsimd", lambda e: e.dma_start(out=b2q8[:], in_=b2_pp),
             inc="dw", k=16)
        p.do("gpsimd", lambda e: e.dma_start(out=ebc[:], in_=ebc_d),
             inc="dw", k=16)
        p.wait("gpsimd", "dw", p.cnt["dw"])
        ev_b1q = ev_b2q = ev_rb = ev_ebc = p.now("dw")

        p.wait_ev("vector", ev_b1q)
        p.wait_ev("vector", ev_b2q)
        p.do("vector", lambda e: e.memset(onec_bf[:], 1.0))
        p.do("vector", lambda e: e.memset(onec_f[:], 1.0))
        p.do("vector", lambda e: e.memset(oner_f[:], 1.0))
        p.do("vector", lambda e: e.memset(oner_bf[:], 1.0))
        p.do("vector", lambda e: e.tensor_copy(out=b1qf[:], in_=b1q8[:]))
        ev_binit = p.do("vector", lambda e: e.tensor_copy(
            out=b2qf[:], in_=b2q8[:]), inc="v")

        ln_state = {"statrow_free": None, "ab_free": None}

        def emit_ln_stats(tok0, ntok, s3c, sb=(4, 5), x_wait_din=None):
            m_row = statrow[0:1, 0:ntok]
            veA = statrow[0:1, 512:512 + ntok]
            Brow = statrow[0:1, 1024:1024 + ntok]
            msq = Brow
            nrow = statrow[0:1, 1536:1536 + ntok]
            n2row = Brow
            sb0, sb1 = sb
            p.wait_ev("tensor", ev_binit)
            p.wait_ev("tensor", bank_free[sb0])
            p.wait_ev("tensor", bank_free[sb1])
            if x_wait_din is not None:
                p.wait("tensor", "din", x_wait_din)
                p.wait("vector", "din", x_wait_din)
            ev_sum = None
            for k in range(DC):
                ev_sum = p.do("tensor", lambda e, k=k: e.matmul(
                    pbank[sb0][0:1, 0:ntok], onec_f[:],
                    sX[:, T * k + tok0:T * k + tok0 + ntok],
                    start=(k == 0), stop=(k == DC - 1)),
                    inc=("t" if k == DC - 1 else None))
            ev_sq = [None] * DC
            for k in range(DC):
                s3 = s3c[:, ntok * (k % 2):ntok * (k % 2 + 1)]
                if k >= 2:
                    p.wait_ev("vector", ev_sq[k - 2])
                ev_v = p.do("vector", lambda e, s3=s3, k=k: e.tensor_mul(
                    s3, sX[:, T * k + tok0:T * k + tok0 + ntok],
                    sX[:, T * k + tok0:T * k + tok0 + ntok]), inc="v")
                p.wait_ev("tensor", ev_v)
                ev_sq[k] = p.do("tensor", lambda e, s3=s3, k=k: e.matmul(
                    pbank[sb1][0:1, 0:ntok], onec_f[:], s3,
                    start=(k == 0), stop=(k == DC - 1)), inc="t")
            p.wait_ev("vector", ev_sum)
            p.wait_ev("vector", ev_sq[DC - 1])
            p.wait_ev("vector", ln_state["statrow_free"])
            p.do("vector", lambda e: e.tensor_single_scalar(
                out=m_row, in_=pbank[sb0][0:1, 0:ntok], scalar=INV_D,
                op=ALU.mult))
            p.do("vector", lambda e: e.tensor_scalar(
                out=veA, in0=pbank[sb1][0:1, 0:ntok], scalar1=INV_D,
                scalar2=EPS_LN, op0=ALU.mult, op1=ALU.add))
            p.do("vector", lambda e: e.tensor_mul(msq, m_row, m_row))
            ev_ve = p.do("vector", lambda e: e.tensor_sub(veA, veA, msq),
                         inc="v")
            bank_free[sb0] = ev_ve
            bank_free[sb1] = ev_ve
            p.wait_ev("scalar", ev_ve)
            p.do("scalar", lambda e: e.activation(out=nrow, in_=veA, func=AF.Ln))
            ev_A = p.do("scalar", lambda e: e.activation(
                out=nrow, in_=nrow, func=AF.Exp, scale=-0.5), inc="s")
            p.wait_ev("vector", ev_A)
            p.do("vector", lambda e: e.tensor_mul(n2row, nrow, nrow))
            p.do("vector", lambda e: e.tensor_mul(n2row, n2row, veA))
            p.do("vector", lambda e: e.tensor_scalar(
                out=n2row, in0=n2row, scalar1=-0.5, scalar2=1.5,
                op0=ALU.mult, op1=ALU.add))
            ev_A = p.do("vector", lambda e: e.tensor_mul(veA, nrow, n2row),
                        inc="v")
            ev_B = p.do("vector", lambda e: e.scalar_tensor_tensor(
                out=Brow, in0=veA, scalar=-1.0, in1=m_row,
                op0=ALU.mult, op1=ALU.mult), inc="v")
            return (ev_A, ev_B)

        def emit_ln_apply(tok0, ntok, out_tile, out_col0, tmpf, evAB,
                          bb=(6, 7)):
            veA = statrow[0:1, 512:512 + ntok]
            Brow = statrow[0:1, 1024:1024 + ntok]
            b0, b1 = bb
            ev_A, ev_B = evAB
            p.wait_ev("tensor", bank_free[b0])
            p.wait_ev("tensor", bank_free[b1])
            p.wait_ev("tensor", ev_A)
            p.wait_ev("tensor", ev_B)
            p.do("tensor", lambda e: e.matmul(
                pbank[b0][:, 0:ntok], oner_f[:], veA, start=True, stop=True))
            ev_bc = p.do("tensor", lambda e: e.matmul(
                pbank[b1][:, 0:ntok], oner_f[:], Brow, start=True, stop=True),
                inc="t")
            ln_state["statrow_free"] = ev_bc
            p.wait_ev("scalar", ev_bc)
            p.wait_ev("scalar", ln_state["ab_free"])
            p.do("scalar", lambda e: e.activation(
                out=Abc[:, 0:ntok], in_=pbank[b0][:, 0:ntok], func=AF.Copy))
            ev_ab = p.do("scalar", lambda e: e.activation(
                out=Bbc[:, 0:ntok], in_=pbank[b1][:, 0:ntok], func=AF.Copy),
                inc="s")
            bank_free[b0] = ev_ab
            bank_free[b1] = ev_ab
            p.wait_ev("vector", ev_ab)
            evs = []
            for k in range(DC):
                p.do("vector", lambda e, k=k: e.tensor_mul(
                    tmpf[:, 0:ntok],
                    sX[:, T * k + tok0:T * k + tok0 + ntok],
                    Abc[:, 0:ntok]))
                evs.append(p.do("vector", lambda e, k=k: e.tensor_add(
                    out_tile[:, out_col0 + ntok * k:out_col0 + ntok * (k + 1)],
                    tmpf[:, 0:ntok],
                    Bbc[:, 0:ntok]), inc="v"))
            ln_state["ab_free"] = evs[-1]
            return evs

        def emit_ln_full(tok0, ntok, s3c, tmpf, out_tile, out_col0,
                         sb=(4, 5), bb=(6, 7), x_wait_din=None):
            evAB = emit_ln_stats(tok0, ntok, s3c, sb, x_wait_din)
            return emit_ln_apply(tok0, ntok, out_tile, out_col0, tmpf,
                                 evAB, bb)[-1]

        def emit_ffn(first):
            if not first:
                p.wait("sync", "v", p.cnt["v"])
                p.wait("sync", "t", p.cnt["t"])
            esf = contextlib.ExitStack()
            with esf:
                sW2 = sbuf("sW2F" + str(first), [128, HC * D], F8, esf)
                mask2 = sbuf("maskF" + str(first), [128, DC * ST], F8, esf)
                xnq = sbuf("xnqF" + str(first), [128, 2 * DC * ST], F8, esf)
                hbf = sbuf("hbfF" + str(first), [128, 2 * 512], BF16, esf)
                hq8 = sbuf("hq8F" + str(first), [128, HC * ST], F8, esf)
                y2b = sbuf("y2bF" + str(first), [128, DC * 512], BF16, esf)
                s3c = sbuf("s3F" + str(first), [128, 2 * 512], F32, esf)
                tmpf = s3c

                if not first:
                    p.wait("gpsimd", "v", p.cnt["v"])
                    p.wait("gpsimd", "t", p.cnt["t"])
                    p.wait("tensor", "v", p.cnt["v"])
                ev_w2q = []
                for q in range(4):
                    p.do("gpsimd", lambda e, q=q: e.dma_start(
                        out=sW2[:, D * 8 * q:D * 8 * (q + 1)]
                            .rearrange("p (c f) -> p c f", c=8),
                        in_=w2_in[:, 8 * q:8 * (q + 1), :]), inc="dw", k=16)
                    p.wait("gpsimd", "dw", p.cnt["dw"])
                    ev_w2q.append(p.now("dw"))

                mask_free = [None, None]
                mm1_last = [None, None]
                hq8_free = None
                hbf_free = [None, None]
                y2_free = [None] * DC

                evAB = emit_ln_stats(0, ST, s3c,
                                     x_wait_din=(x_ready[0] if first else None))
                apply_evs = emit_ln_apply(0, ST, xnq, 0, tmpf, evAB)

                sW1_3 = sW1[:].rearrange("p (c f) -> p c f", c=DC)
                sW2_3 = sW2[:].rearrange("p (c f) -> p c f", c=HC)
                hq8_3 = hq8[:].rearrange("p (c f) -> p c f", c=HC)
                ev_res = None
                for s in range(NST):
                    msl = s % 2
                    p.wait_ev("sync", mask_free[0])
                    p.do("sync", lambda e, s=s: e.dma_start(
                        out=mask2[:].rearrange("p (c f) -> p c f", c=DC),
                        in_=maskT.rearrange("(c p) f -> p c f", p=128)
                            [:, :, ST * s:ST * (s + 1)]), inc="din", k=16)
                    p.wait("sync", "din", p.cnt["din"])
                    ev_mask = p.now("din")

                    xnq_3 = xnq[:, DC * ST * msl:DC * ST * (msl + 1)].rearrange(
                        "p (c f) -> p c f", c=DC)
                    ev_cast32 = None
                    for m in range(HC):
                        b = m % 2
                        p.wait_ev("tensor", bank_free[b])
                        p.wait_ev("tensor", ev_w1a if m < HC // 2 else ev_w1)
                        if DR_MM1:
                            for k in range(DC // 2):
                                if m == 0:
                                    p.wait_ev("tensor", apply_evs[2 * k + 1])
                                evg = p.do("tensor", lambda e, m=m, k=k, b=b, msl=msl: e.matmul(
                                    pbank[b][:],
                                    sW1_3[:, 2 * k:2 * k + 2, 128 * m:128 * (m + 1)],
                                    xnq_3[:, 2 * k:2 * k + 2, :],
                                    start=(k == 0), stop=(k == DC // 2 - 1),
                                    perf_mode=DRPM),
                                    inc=("t" if k == DC // 2 - 1 else None))
                        else:
                            for k in range(DC):
                                if m == 0:
                                    p.wait_ev("tensor", apply_evs[k])
                                evg = p.do("tensor", lambda e, m=m, k=k, b=b, msl=msl: e.matmul(
                                    pbank[b][:],
                                    sW1[:, HID * k + 128 * m:HID * k + 128 * (m + 1)],
                                    xnq[:, DC * ST * msl + ST * k:DC * ST * msl + ST * (k + 1)],
                                    start=(k == 0), stop=(k == DC - 1)),
                                    inc=("t" if k == DC - 1 else None))
                        hsl = m % 2
                        p.wait_ev("scalar", evg)
                        p.wait_ev("scalar", hbf_free[hsl])
                        if first and m == 0 and s == 0:
                            p.wait_ev("scalar", ev_binit)
                        ev_ev = p.do("scalar", lambda e, m=m, b=b, hsl=hsl: e.activation(
                            out=hbf[:, 512 * hsl:512 * (hsl + 1)],
                            in_=pbank[b][:], func=AF.Relu,
                            bias=b1qf[:, m:m + 1]), inc="s")
                        bank_free[b] = ev_ev
                        p.wait_ev("vector", ev_ev)
                        if hq8_free is not None and m == 0:
                            p.wait_ev("vector", hq8_free)
                        ev_cast32 = p.do("vector", lambda e, m=m, hsl=hsl: e.tensor_copy(
                            out=hq8[:, ST * m:ST * (m + 1)],
                            in_=hbf[:, 512 * hsl:512 * (hsl + 1)]), inc="v")
                        hbf_free[hsl] = ev_cast32
                    mm1_last[msl] = evg

                    if s + 1 < NST:
                        evAB = emit_ln_stats(
                            ST * (s + 1), ST, s3c,
                            x_wait_din=(x_ready[s + 1] if first else None))

                    y_evs = []
                    for n in range(DC):
                        b = 2 + n % 2
                        p.wait_ev("tensor", bank_free[b])
                        p.wait_ev("tensor", ev_cast32)
                        if DR_MM2 and not first:
                            for j in range(HC // 2):
                                if s == 0:
                                    p.wait_ev("tensor", ev_w2q[(2 * j + 1) // 8])
                                evg = p.do("tensor", lambda e, n=n, j=j, b=b: e.matmul(
                                    pbank[b][:],
                                    sW2_3[:, 2 * j:2 * j + 2, 128 * n:128 * (n + 1)],
                                    hq8_3[:, 2 * j:2 * j + 2, :],
                                    start=(j == 0), stop=(j == HC // 2 - 1),
                                    perf_mode=DRPM),
                                    inc=("t" if j == HC // 2 - 1 else None))
                        else:
                            for j in range(HC):
                                if s == 0:
                                    p.wait_ev("tensor", ev_w2q[j // 8])
                                evg = p.do("tensor", lambda e, n=n, j=j, b=b: e.matmul(
                                    pbank[b][:],
                                    sW2[:, D * j + 128 * n:D * j + 128 * (n + 1)],
                                    hq8[:, ST * j:ST * (j + 1)],
                                    start=(j == 0), stop=(j == HC - 1)),
                                    inc=("t" if j == HC - 1 else None))
                        if n == DC - 1:
                            hq8_free = evg
                        p.wait_ev("scalar", evg)
                        p.wait_ev("scalar", y2_free[n])
                        ev_y = p.do("scalar", lambda e, n=n, b=b: e.activation(
                            out=y2b[:, 512 * n:512 * (n + 1)],
                            in_=pbank[b][:], func=AF.Identity,
                            bias=b2qf[:, n:n + 1]), inc="s")
                        bank_free[b] = ev_y
                        y_evs.append(ev_y)

                    if s + 1 < NST:
                        p.wait_ev("vector", mm1_last[(s + 1) % 2])
                        apply_evs = emit_ln_apply(
                            ST * (s + 1), ST, xnq, DC * ST * ((s + 1) % 2),
                            tmpf, evAB)

                    for n in range(DC):
                        p.wait_ev("vector", y_evs[n])
                        if n == 0:
                            p.wait_ev("vector", ev_mask)
                        p.do("vector", lambda e, n=n: e.tensor_mul(
                            tmpf[:, 0:512],
                            y2b[:, 512 * n:512 * (n + 1)],
                            mask2[:, ST * n:ST * (n + 1)]))
                        ev_res = p.do("vector", lambda e, n=n, s=s: e.tensor_add(
                            sX[:, T * n + ST * s:T * n + ST * (s + 1)],
                            sX[:, T * n + ST * s:T * n + ST * (s + 1)],
                            tmpf[:, 0:512]), inc="v")
                        y2_free[n] = ev_res
                        if n == DC - 1:
                            mask_free[0] = ev_res
            return ev_res

        ev_ffn1 = emit_ffn(first=True) if "ffn1" in phases else None

        wk_in = wkt.rearrange("(c p) f -> p c f", p=128)
        p.do("gpsimd", lambda e: e.dma_start(
            out=wslotA[:].rearrange("p (c f) -> p c f", c=DC), in_=wk_in),
            inc="dw", k=16)
        p.wait("gpsimd", "dw", p.cnt["dw"])
        ev_wk = p.now("dw")

        if "attn" in phases:
            es_a = contextlib.ExitStack()
            with es_a:
                es_kv = contextlib.ExitStack()
                with es_kv:
                    wslotB = sbuf("wslotB", [128, DC * D], BF16, es_kv)
                    rbrow = sbuf("rbrow", [1, 2 * D], F32, es_kv)
                    xn2 = sbuf("xn2", [128, 2 * DC * ST], BF16, es_kv)
                    ktm = sbuf("ktm", [128, D], BF16, es_kv)
                    v65 = sbuf("v65", [128, NH * 65], BF16, es_kv)
                    s3A = sbuf("s3A", [128, 2 * 512], F32, es_kv)
                    tmpa = sbuf("tmpa", [128, 512], F32, es_kv)
                    tmpb = sbuf("tmpb", [128, 512], F32, es_kv)

                    p.wait_ev("sync", ev_ffn1)
                    p.wait("sync", "t", p.cnt["t"])
                    p.do("sync", lambda e: e.dma_start(out=rbrow[:], in_=rb_d),
                         inc="din", k=16)
                    p.wait("sync", "din", p.cnt["din"])
                    ev_rbl = p.now("din")
                    p.wait_ev("gpsimd", ev_ffn1)
                    p.wait("gpsimd", "t", p.cnt["t"])
                    wv_in = wvt.rearrange("(c p) f -> p c f", p=128)
                    p.do("gpsimd", lambda e: e.dma_start(
                        out=wslotB[:].rearrange("p (c f) -> p c f", c=DC),
                        in_=wv_in), inc="dw", k=16)
                    p.wait("gpsimd", "dw", p.cnt["dw"])
                    ev_wv = p.now("dw")

                    p.do("vector", lambda e: e.memset(
                        v65[:].rearrange("p (h c) -> p h c", c=65)[:, :, 64:65], 1.0))

                    ktm_free = None
                    v65_free = None
                    ev_kvadd = None
                    xn2_last = [None, None]
                    p.wait("tensor", "v", p.cnt["v"])
                    evAB = emit_ln_stats(0, ST, s3A)
                    apply_evs = emit_ln_apply(0, ST, xn2, 0, s3A, evAB)
                    for s in range(NST):
                        msl = s % 2
                        xcol = DC * ST * msl
                        for t4 in range(4):
                            t_ = 4 * s + t4
                            ev_kgrp = [None, None]
                            ev_vgrp = [None, None]
                            for proj in range(2):
                                for half in range(2):
                                    b = 2 * proj + half
                                    p.wait_ev("tensor", bank_free[b])
                                    if t_ == 0 and half == 0:
                                        p.wait_ev("tensor",
                                                  ev_wk if proj == 0 else ev_wv)
                                        p.wait_ev("tensor", ev_rbl)
                                    wsl = wslotA if proj == 0 else wslotB
                                    for k in range(DC):
                                        if t4 == 0 and proj == 0:
                                            p.wait_ev("tensor", apply_evs[k])
                                        p.do("tensor", lambda e, k=k, b=b, t4=t4, half=half, wsl=wsl, xcol=xcol: e.matmul(
                                            pbank[b][:],
                                            xn2[:, xcol + ST * k + 128 * t4:xcol + ST * k + 128 * (t4 + 1)],
                                            wsl[:, D * k + 512 * half:D * k + 512 * (half + 1)],
                                            start=(k == 0), stop=False))
                                    evg = p.do("tensor", lambda e, b=b, half=half, proj=proj: e.matmul(
                                        pbank[b][:], oner_f[:],
                                        rbrow[0:1, D * proj + 512 * half:D * proj + 512 * (half + 1)],
                                        start=False, stop=True), inc="t")
                                    if proj == 0:
                                        ev_kgrp[half] = evg
                                    else:
                                        ev_vgrp[half] = evg
                            ev_k = None
                            for half in range(2):
                                b = half
                                p.wait_ev("vector", ev_kgrp[half])
                                p.wait_ev("vector", ktm_free)
                                p.do("vector", lambda e, b=b: e.tensor_scalar_max(
                                    out=tmpa[:], in0=pbank[b][:], scalar1=0.0))
                                ev_mn = p.do("vector", lambda e, b=b: e.tensor_scalar_min(
                                    out=tmpb[:], in0=pbank[b][:], scalar1=0.0),
                                    inc="v")
                                p.wait_ev("scalar", ev_mn)
                                ev_ex = p.do("scalar", lambda e: e.activation(
                                    out=tmpb[:], in_=tmpb[:], func=AF.Exp),
                                    inc="s")
                                p.wait_ev("vector", ev_ex)
                                ev_k = p.do("vector", lambda e, half=half: e.tensor_add(
                                    ktm[:, 512 * half:512 * (half + 1)],
                                    tmpa[:], tmpb[:]), inc="v")
                                bank_free[b] = ev_k
                            ev_v = None
                            for half in range(2):
                                b = 2 + half
                                p.wait_ev("vector", ev_vgrp[half])
                                p.wait_ev("vector", v65_free)
                                ev_v = p.do("vector", lambda e, b=b, half=half: e.tensor_copy(
                                    out=v65[:].rearrange("p (h c) -> p h c", c=65)
                                        [:, 8 * half:8 * (half + 1), 0:64],
                                    in_=pbank[b][:].rearrange("p (h c) -> p h c", c=64)),
                                    inc="v")
                                bank_free[b] = ev_v
                            p.wait_ev("tensor", ev_k)
                            p.wait_ev("tensor", ev_v)
                            for q in range(4, 8):
                                p.wait_ev("tensor", bank_free[q])
                            evkv = None
                            for h in range(NH):
                                bkn = 4 + h // 4
                                r0 = 64 * (h % 2)
                                c0 = 65 * ((h % 4) // 2)
                                evkv = p.do("tensor", lambda e, h=h, bkn=bkn, r0=r0, c0=c0: e.matmul(
                                    pbank[bkn][r0:r0 + 64, c0:c0 + 65],
                                    ktm[:, HD * h:HD * (h + 1)],
                                    v65[:, 65 * h:65 * (h + 1)],
                                    start=True, stop=True,
                                    tile_position=(0, r0)),
                                    inc=("t" if h == NH - 1 else None))
                            ktm_free = evkv
                            v65_free = evkv
                            if t4 == 3:
                                xn2_last[msl] = evkv
                            p.wait_ev("vector", evkv)
                            for q in range(4):
                                if t_ == 0:
                                    ev_kvadd = p.do("vector", lambda e, q=q: e.tensor_copy(
                                        out=skv[:, 130 * q:130 * (q + 1)],
                                        in_=pbank[4 + q][:, 0:130]), inc="v")
                                else:
                                    ev_kvadd = p.do("vector", lambda e, q=q: e.tensor_add(
                                        skv[:, 130 * q:130 * (q + 1)],
                                        skv[:, 130 * q:130 * (q + 1)],
                                        pbank[4 + q][:, 0:130]), inc="v")
                            for q in range(4, 8):
                                bank_free[q] = ev_kvadd
                        if s + 1 < NST:
                            evAB = emit_ln_stats(ST * (s + 1), ST, s3A)
                            p.wait_ev("vector", xn2_last[(s + 1) % 2])
                            apply_evs = emit_ln_apply(
                                ST * (s + 1), ST, xn2, DC * ST * ((s + 1) % 2),
                                s3A, evAB)

                    p.wait_ev("gpsimd", ev_kvadd)
                    p.do("gpsimd", lambda e: e.dma_start(
                        out=cc_in[:, :], in_=skv[:]), inc="dw", k=16)
                    p.wait("gpsimd", "dw", p.cnt["dw"])
                    p.do("gpsimd", lambda e: e.collective_compute(
                        "AllReduce", ALU.add,
                        replica_groups=[[0, 1], [2, 3], [4, 5], [6, 7]],
                        ins=[cc_in.ap().opt()], outs=[cc_out.ap().opt()]),
                        inc="cc")
                    p.wait("gpsimd", "t", p.cnt["t"])
                    wq_in = wqt.rearrange("(c p) f -> p c f", p=128)
                    p.do("gpsimd", lambda e: e.dma_start(
                        out=wslotA[:].rearrange("p (c f) -> p c f", c=DC),
                        in_=wq_in), inc="dw", k=16)
                    p.wait("gpsimd", "dw", p.cnt["dw"])
                    ev_wq = p.now("dw")
                    p.wait("gpsimd", "cc", 1)
                    p.do("gpsimd", lambda e: e.dma_start(
                        out=skv[:], in_=cc_out[:, :]), inc="dw", k=16)
                    p.wait("gpsimd", "dw", p.cnt["dw"])
                    ev_kvtot = p.now("dw")

                es_q = contextlib.ExitStack()
                with es_q:
                    wslotO = sbuf("wslotO", [128, DC * D], F32, es_q)
                    xn2q = sbuf("xn2q", [128, DC * QT], BF16, es_q)
                    qphi = sbuf("qphi", [128, DC * QT], BF16, es_q)
                    outpre = sbuf("outpre", [128, DC * QT], F32, es_q)
                    kvbd = sbuf("kvbd", [128, DC * 128], BF16, es_q)
                    zbd = sbuf("zbd", [128, DC * 16], BF16, es_q)
                    tmpn = sbuf("tmpn", [16, QT], F32, es_q)
                    s3Q = sbuf("s3Q", [128, 2 * QT], F32, es_q)
                    tmpaQ = sbuf("tmpaQ", [128, QT], F32, es_q)
                    tmpeQ = sbuf("tmpeQ", [128, QT], F32, es_q)

                    p.wait("sync", "t", p.cnt["t"])
                    p.wait("sync", "v", p.cnt["v"])
                    p.wait_ev("sync", ev_wq)
                    for k in range(DC):
                        p.do("sync", lambda e, k=k: e.dma_start(
                            out=wslotO[:, D * k:D * (k + 1)],
                            in_=wot[128 * k:128 * (k + 1), :]), inc="din", k=16)
                    p.wait("sync", "din", p.cnt["din"])
                    ev_wo = p.now("din")



                    ev_res_attn = None
                    qphi_free = None
                    outpre_free = None
                    qproj_last = None
                    evABq = emit_ln_stats(0, QT, s3Q)
                    apply_evq = emit_ln_apply(0, QT, xn2q, 0, tmpaQ, evABq)

                    for s in range(NQT):
                        ev_qphi = None
                        for n in range(DC):
                            b = 4 + n % 2
                            p.wait_ev("tensor", bank_free[b])
                            if s == 0 and n == 0:
                                p.wait_ev("tensor", ev_wq)
                            for k in range(DC):
                                if n == 0:
                                    p.wait_ev("tensor", apply_evq[k])
                                evg = p.do("tensor", lambda e, n=n, k=k, b=b: e.matmul(
                                    pbank[b][:, 0:QT],
                                    wslotA[:, D * k + 128 * n:D * k + 128 * (n + 1)],
                                    xn2q[:, QT * k:QT * (k + 1)],
                                    start=(k == 0), stop=(k == DC - 1)),
                                    inc=("t" if k == DC - 1 else None))
                            if n == DC - 1:
                                qproj_last = evg
                            p.wait_ev("vector", evg)
                            p.wait_ev("vector", qphi_free)
                            ev_mn = p.do("vector", lambda e, n=n, b=b: e.tensor_scalar(
                                out=tmpaQ[:], in0=pbank[b][:, 0:QT],
                                scalar1=bq_pp[:, n:n + 1], scalar2=0.0,
                                op0=ALU.add, op1=ALU.min), inc="v")
                            p.wait_ev("scalar", ev_mn)
                            p.wait_ev("scalar", ev_bq)
                            ev_ex = p.do("scalar", lambda e: e.activation(
                                out=tmpeQ[:], in_=tmpaQ[:], func=AF.Exp), inc="s")
                            p.wait_ev("vector", ev_ex)
                            p.do("vector", lambda e, n=n, b=b: e.tensor_scalar(
                                out=tmpaQ[:], in0=pbank[b][:, 0:QT],
                                scalar1=bq_pp[:, n:n + 1], scalar2=0.0,
                                op0=ALU.add, op1=ALU.max))
                            ev_qphi = p.do("vector", lambda e, n=n: e.tensor_add(
                                qphi[:, QT * n:QT * (n + 1)], tmpaQ[:], tmpeQ[:]),
                                inc="v")
                            bank_free[b] = ev_qphi
                        if s == 0:
                            p.wait_ev("vector", ev_kvtot)
                            p.do("vector", lambda e: e.memset(kvbd[:], 0.0))
                            p.do("vector", lambda e: e.memset(zbd[:], 0.0))
                            ev_bd = None
                            for h in range(NH):
                                r0 = 64 * (h % 2)
                                c0 = 130 * (h // 4) + 65 * ((h % 4) // 2)
                                blk = h // 2
                                p.do("vector", lambda e, h=h, r0=r0, c0=c0, blk=blk: e.tensor_copy(
                                    out=kvbd[r0:r0 + 64,
                                             128 * blk + 64 * (h % 2):128 * blk + 64 * (h % 2) + 64],
                                    in_=skv[r0:r0 + 64, c0:c0 + 64]))
                                ev_bd = p.do("vector", lambda e, h=h, r0=r0, c0=c0, blk=blk: e.tensor_copy(
                                    out=zbd[r0:r0 + 64, 16 * blk + h:16 * blk + h + 1],
                                    in_=skv[r0:r0 + 64, c0 + 64:c0 + 65]), inc="v")
                        p.wait_ev("tensor", bank_free[6])
                        p.wait_ev("tensor", ev_bd)
                        p.wait_ev("tensor", ev_qphi)
                        for k in range(DC):
                            evg = p.do("tensor", lambda e, k=k: e.matmul(
                                pbank[6][0:16, 0:QT], zbd[:, 16 * k:16 * (k + 1)],
                                qphi[:, QT * k:QT * (k + 1)],
                                start=(k == 0), stop=(k == DC - 1)),
                                inc=("t" if k == DC - 1 else None))
                        p.wait_ev("vector", evg)
                        p.do("vector", lambda e: e.tensor_single_scalar(
                            out=tmpn[:], in_=pbank[6][0:16, 0:QT],
                            scalar=EPS_ATTN, op=ALU.add))
                        ev_rec = p.do("vector", lambda e: e.reciprocal(
                            out=tmpn[:], in_=tmpn[:]), inc="v")
                        bank_free[6] = ev_rec
                        ev_op = None
                        for k in range(DC):
                            b = 4 + k % 2
                            br = 6 + k % 2
                            p.wait_ev("tensor", bank_free[b])
                            p.wait_ev("tensor", bank_free[br])
                            p.wait_ev("tensor", ev_rec)
                            evg = p.do("tensor", lambda e, k=k, b=b: e.matmul(
                                pbank[b][:, 0:QT], kvbd[:, 128 * k:128 * (k + 1)],
                                qphi[:, QT * k:QT * (k + 1)],
                                start=True, stop=True))
                            evg = p.do("tensor", lambda e, k=k, br=br: e.matmul(
                                pbank[br][:, 0:QT], ebc[:, 128 * k:128 * (k + 1)],
                                tmpn[:], start=True, stop=True), inc="t")
                            p.wait_ev("vector", evg)
                            p.wait_ev("vector", outpre_free)
                            p.do("vector", lambda e, k=k, b=b: e.tensor_copy(
                                out=outpre[:, QT * k:QT * (k + 1)],
                                in_=pbank[b][:, 0:QT]))
                            ev_op = p.do("vector", lambda e, k=k, br=br: e.tensor_mul(
                                outpre[:, QT * k:QT * (k + 1)],
                                outpre[:, QT * k:QT * (k + 1)],
                                pbank[br][:, 0:QT]), inc="v")
                            bank_free[b] = ev_op
                            bank_free[br] = ev_op
                        qphi_free = ev_op
                        if s + 1 < NQT:
                            evABq = emit_ln_stats(QT * (s + 1), QT, s3Q)
                            p.wait_ev("vector", qproj_last)
                            apply_evq = emit_ln_apply(
                                QT * (s + 1), QT, xn2q, 0, tmpaQ, evABq)
                        for n in range(DC):
                            b = n % 2
                            p.wait_ev("tensor", bank_free[b])
                            p.wait_ev("tensor", ev_op)
                            if s == 0 and n == 0:
                                p.wait_ev("tensor", ev_wo)
                            for k in range(DC):
                                evg = p.do("tensor", lambda e, n=n, k=k, b=b: e.matmul(
                                    pbank[b][:, 0:QT],
                                    wslotO[:, D * k + 128 * n:D * k + 128 * (n + 1)],
                                    outpre[:, QT * k:QT * (k + 1)],
                                    start=(k == 0), stop=(k == DC - 1)),
                                    inc=("t" if k == DC - 1 else None))
                            p.wait_ev("vector", evg)
                            p.wait_ev("vector", ev_bo)
                            ev_res_attn = p.do("vector", lambda e, n=n, s=s, b=b: e.scalar_tensor_tensor(
                                out=sX[:, T * n + QT * s:T * n + QT * (s + 1)],
                                in0=pbank[b][:, 0:QT], scalar=bo_pp[:, n:n + 1],
                                in1=sX[:, T * n + QT * s:T * n + QT * (s + 1)],
                                op0=ALU.add, op1=ALU.add), inc="v")
                            bank_free[b] = ev_res_attn
                        outpre_free = evg

        ev_ffn2 = emit_ffn(first=False) if "ffn2" in phases else None

        p.wait_ev("sync", ev_ffn2)
        p.wait("sync", "v", p.cnt["v"])
        ndo = 0
        for k in range(DC):
            p.do("sync", lambda e, k=k: e.dma_start(
                out=outT[128 * k:128 * (k + 1), :],
                in_=sX[:, T * k:T * (k + 1)]), inc="do", k=16)
            ndo += 16
        p.wait("sync", "do", ndo)

        with nc.Block() as block:
            for eng_name, deco in (("sync", block.sync), ("vector", block.vector),
                                   ("scalar", block.scalar), ("tensor", block.tensor),
                                   ("gpsimd", block.gpsimd)):
                ops = p.ops[eng_name]
                def run(e, ops=ops):
                    for f in ops:
                        f(e)
                deco(run)

    return nc


def _threefry2x32(k0, k1, x0, x1):
    x0 = x0.astype(np.uint32).copy()
    x1 = x1.astype(np.uint32).copy()
    ks0, ks1 = np.uint32(k0), np.uint32(k1)
    ks2 = np.uint32(ks0 ^ ks1 ^ np.uint32(0x1BD11BDA))
    rots = [(13, 15, 26, 6), (17, 29, 16, 24)]
    with np.errstate(over="ignore"):
        x0 = (x0 + ks0).astype(np.uint32)
        x1 = (x1 + ks1).astype(np.uint32)
        ks = [ks1, ks2, ks0]
        for r in range(5):
            for d in rots[r % 2]:
                x0 = (x0 + x1).astype(np.uint32)
                x1 = ((x1 << np.uint32(d)) | (x1 >> np.uint32(32 - d))).astype(np.uint32)
                x1 = (x1 ^ x0).astype(np.uint32)
            x0 = (x0 + ks[r % 3]).astype(np.uint32)
            x1 = (x1 + ks[(r + 1) % 3] + np.uint32(r + 1)).astype(np.uint32)
    return x0, x1


def _host_constants():
    n = 4 * 4096 * 1024
    i = np.arange(n, dtype=np.uint64)
    b1, b2 = _threefry2x32(0, 42, (i >> np.uint64(32)).astype(np.uint32),
                           (i & np.uint64(0xFFFFFFFF)).astype(np.uint32))
    bits = (b1 ^ b2).astype(np.uint32)
    u = ((bits >> np.uint32(9)) | np.uint32(0x3F800000)).view(np.float32)         - np.float32(1.0)
    keep = (u < np.float32(0.8)).reshape(4, 4096, 1024)
    mask = np.where(keep, np.float32(0.625), np.float32(0.0))
    ebc = np.zeros((16, 1024), np.float32)
    for h in range(16):
        for k in range(8):
            if h // 2 == k:
                ebc[h, 128 * k + 64 * (h % 2):128 * k + 64 * (h % 2) + 64] = 1.0
    return mask, ebc


def kernel(L0_attn_x, L0_attn_Wq, L0_attn_bq, L0_attn_Wk, L0_attn_bk,
           L0_attn_Wv, L0_attn_bv, L0_attn_Wo, L0_attn_bo,
           L0_ffn_W1, L0_ffn_b1, L0_ffn_W2, L0_ffn_b2):
    x = np.asarray(L0_attn_x, np.float32)
    f8 = ml_dtypes.float8_e5m2

    mask, ebc = _host_constants()

    w1t = np.ascontiguousarray(np.asarray(L0_ffn_W1, np.float32).T)
    w2t = np.ascontiguousarray(np.asarray(L0_ffn_W2, np.float32).T)
    wqt = np.ascontiguousarray(np.asarray(L0_attn_Wq, np.float32).T)
    wkt = np.ascontiguousarray(np.asarray(L0_attn_Wk, np.float32).T)
    wvt = np.ascontiguousarray(np.asarray(L0_attn_Wv, np.float32).T)
    wot = np.ascontiguousarray(np.asarray(L0_attn_Wo, np.float32).T)
    b1_pp = np.ascontiguousarray(np.asarray(L0_ffn_b1, np.float32).reshape(HC, 128).T)
    b2_pp = np.ascontiguousarray(np.asarray(L0_ffn_b2, np.float32).reshape(DC, 128).T)
    bq_pp = np.ascontiguousarray(np.asarray(L0_attn_bq, np.float32).reshape(DC, 128).T)
    bo_pp = np.ascontiguousarray(np.asarray(L0_attn_bo, np.float32).reshape(DC, 128).T)
    rb = np.concatenate([np.asarray(L0_attn_bk, np.float32),
                         np.asarray(L0_attn_bv, np.float32)]).reshape(1, 2 * D)
    rb = np.ascontiguousarray(rb)

    nc = build_nc()
    in_maps = []
    for c in range(8):
        b, h = c // 2, c % 2
        xc = np.ascontiguousarray(x[b, T * h:T * (h + 1), :].T)
        mc = np.ascontiguousarray(mask[b, T * h:T * (h + 1), :].T).astype(f8)
        in_maps.append({
            "xT": xc, "w1t": w1t, "w2t": w2t, "wqt": wqt, "wkt": wkt,
            "wvt": wvt, "wot": wot, "b1_pp": b1_pp, "b2_pp": b2_pp,
            "bq_d": bq_pp, "bo_d": bo_pp, "rb_d": rb, "ebc_d": ebc,
            "maskT": mc,
        })
    res = run_bass_kernel_spmd(nc, in_maps, core_ids=list(range(8)))
    out = np.empty((4, 4096, 1024), np.float32)
    for c in range(8):
        b, h = c // 2, c % 2
        out[b, T * h:T * (h + 1), :] = np.asarray(res.results[c]["outT"]).T
    return out


# version 10
# speedup vs baseline: 1.2323x; 1.0531x over previous
import contextlib

import numpy as np
import ml_dtypes

import concourse.bass as bass
import concourse.mybir as mybir
from concourse.bass_utils import run_bass_kernel_spmd

F32 = mybir.dt.float32
BF16 = mybir.dt.bfloat16
F8 = mybir.dt.float8e5
AF = mybir.ActivationFunctionType
ALU = mybir.AluOpType

D = 1024
DC = 8
HID = 4096
HC = 32
T = 2048
ST = 512
NST = T // ST
QT = 256
NQT = T // QT
NH = 16
HD = 64
EPS_LN = 1e-5
EPS_ATTN = 1e-6
INV_D = 1.0 / D
DRPM = mybir.MatmulPerfMode.DoubleRow
DR_MM1 = False
DR_MM2 = True

ENGS = ("sync", "vector", "scalar", "tensor", "gpsimd")


class Prog:

    def __init__(self):
        self.ops = {e: [] for e in ENGS}
        self.sems = {}
        self.cnt = {}

    def add_sem(self, name, handle):
        self.sems[name] = handle
        self.cnt[name] = 0

    def wait(self, eng, sname, val):
        if val <= 0:
            return
        h = self.sems[sname]
        self.ops[eng].append(lambda e, h=h, v=val: e.wait_ge(h, v))

    def wait_ev(self, eng, ev):
        if ev is not None:
            self.wait(eng, ev[0], ev[1])

    def do(self, eng, fn, inc=None, k=1):
        if inc is None:
            self.ops[eng].append(fn)
            return None
        h = self.sems[inc]
        self.cnt[inc] += k
        def g(e, fn=fn, h=h, k=k):
            fn(e).then_inc(h, k)
        self.ops[eng].append(g)
        return (inc, self.cnt[inc])

    def now(self, sname):
        return (sname, self.cnt[sname])


def build_nc(phases=("ffn1", "attn", "ffn2")):
    nc = bass.Bass("TRN2", target_bir_lowering=False, debug=False,
                   detect_race_conditions=False)

    xT = nc.dram_tensor("xT", [D, T], F32, kind="ExternalInput").ap()
    w1t = nc.dram_tensor("w1t", [D, HID], F32, kind="ExternalInput").ap()
    w2t = nc.dram_tensor("w2t", [HID, D], F32, kind="ExternalInput").ap()
    wqt = nc.dram_tensor("wqt", [D, D], F32, kind="ExternalInput").ap()
    wkt = nc.dram_tensor("wkt", [D, D], F32, kind="ExternalInput").ap()
    wvt = nc.dram_tensor("wvt", [D, D], F32, kind="ExternalInput").ap()
    wot = nc.dram_tensor("wot", [D, D], F32, kind="ExternalInput").ap()
    b1_pp = nc.dram_tensor("b1_pp", [128, HC], F32, kind="ExternalInput").ap()
    b2_pp = nc.dram_tensor("b2_pp", [128, DC], F32, kind="ExternalInput").ap()
    bq_d = nc.dram_tensor("bq_d", [128, DC], F32, kind="ExternalInput").ap()
    bo_d = nc.dram_tensor("bo_d", [128, DC], F32, kind="ExternalInput").ap()
    rb_d = nc.dram_tensor("rb_d", [1, 2 * D], F32, kind="ExternalInput").ap()
    ebc_d = nc.dram_tensor("ebc_d", [16, D], F32, kind="ExternalInput").ap()
    maskT = nc.dram_tensor("maskT", [D, T], F8, kind="ExternalInput").ap()
    outT = nc.dram_tensor("outT", [D, T], F32, kind="ExternalOutput").ap()
    cc_in = nc.dram_tensor("cc_in", [128, 520], F32)
    cc_out = nc.dram_tensor("cc_out", [128, 520], F32)

    p = Prog()
    es0 = contextlib.ExitStack()
    with es0:
        def sbuf(name, shape, dt, stack=es0):
            return stack.enter_context(nc.sbuf_tensor(name, shape, dt))

        def sem(name):
            h = es0.enter_context(nc.semaphore(name))
            p.add_sem(name, h)
            return h

        sX = sbuf("sX", [128, DC * T], F32)
        sW1 = sbuf("sW1", [128, DC * HID], F8)
        wslotA = sbuf("wslotA", [128, DC * D], BF16)
        b1q8 = sbuf("b1q8", [128, HC], F8)
        b2q8 = sbuf("b2q8", [128, DC], F8)
        b1qf = sbuf("b1qf", [128, HC], F32)
        b2qf = sbuf("b2qf", [128, DC], F32)
        bq_pp = sbuf("bq_pp", [128, DC], F32)
        bo_pp = sbuf("bo_pp", [128, DC], F32)
        ebc = sbuf("ebc", [16, D], F32)
        onec_bf = sbuf("onec_bf", [128, 1], BF16)
        onec_f = sbuf("onec_f", [128, 1], F32)
        oner_f = sbuf("oner_f", [1, 128], F32)
        oner_bf = sbuf("oner_bf", [1, 128], BF16)
        skv = sbuf("skv", [128, 520], F32)
        statrow = sbuf("statrow", [1, 4 * 512], F32)
        Abc = sbuf("Abc", [128, 512], F32)
        Bbc = sbuf("Bbc", [128, 512], F32)

        for s in ("din", "dw", "v", "s", "t", "g", "do", "cc"):
            sem(s)

        pbank = [es0.enter_context(nc.psum_tensor(f"pb{i}", [128, 512], F32))
                 for i in range(8)]
        bank_free = [None] * 8

        x_ready = []
        for s in range(NST):
            for k in range(DC):
                p.do("sync", lambda e, k=k, s=s: e.dma_start(
                    out=sX[:, T * k + ST * s: T * k + ST * (s + 1)],
                    in_=xT[128 * k:128 * (k + 1), ST * s:ST * (s + 1)]),
                    inc="din", k=16)
            x_ready.append(p.cnt["din"])
            p.wait("sync", "din", p.cnt["din"])

        p.do("sync", lambda e: e.dma_start(out=bq_pp[:], in_=bq_d),
             inc="din", k=16)
        p.do("sync", lambda e: e.dma_start(out=bo_pp[:], in_=bo_d),
             inc="din", k=16)
        p.wait("sync", "din", p.cnt["din"])
        ev_bq = ev_bo = p.now("din")

        ev_w1a = None
        for half in range(2):
            for k in range(DC):
                p.do("gpsimd", lambda e, k=k, half=half: e.dma_start(
                    out=sW1[:, HID * k + 2048 * half:HID * k + 2048 * (half + 1)],
                    in_=w1t[128 * k:128 * (k + 1),
                            2048 * half:2048 * (half + 1)]), inc="dw", k=16)
            p.wait("gpsimd", "dw", p.cnt["dw"])
            if half == 0:
                ev_w1a = p.now("dw")
        ev_w1 = p.now("dw")
        w2_in = w2t.rearrange("(c p) f -> p c f", p=128)
        p.do("gpsimd", lambda e: e.dma_start(out=b1q8[:], in_=b1_pp),
             inc="dw", k=16)
        p.do("gpsimd", lambda e: e.dma_start(out=b2q8[:], in_=b2_pp),
             inc="dw", k=16)
        p.do("gpsimd", lambda e: e.dma_start(out=ebc[:], in_=ebc_d),
             inc="dw", k=16)
        p.wait("gpsimd", "dw", p.cnt["dw"])
        ev_b1q = ev_b2q = ev_rb = ev_ebc = p.now("dw")

        p.wait_ev("vector", ev_b1q)
        p.wait_ev("vector", ev_b2q)
        p.do("vector", lambda e: e.memset(onec_bf[:], 1.0))
        p.do("vector", lambda e: e.memset(onec_f[:], 1.0))
        p.do("vector", lambda e: e.memset(oner_f[:], 1.0))
        p.do("vector", lambda e: e.memset(oner_bf[:], 1.0))
        p.do("vector", lambda e: e.tensor_copy(out=b1qf[:], in_=b1q8[:]))
        ev_binit = p.do("vector", lambda e: e.tensor_copy(
            out=b2qf[:], in_=b2q8[:]), inc="v")

        ln_state = {"statrow_free": None, "ab_free": None}

        def emit_ln_stats(tok0, ntok, s3c, sb=(4, 5), x_wait_din=None):
            m_row = statrow[0:1, 0:ntok]
            veA = statrow[0:1, 512:512 + ntok]
            Brow = statrow[0:1, 1024:1024 + ntok]
            msq = Brow
            nrow = statrow[0:1, 1536:1536 + ntok]
            n2row = Brow
            sb0, sb1 = sb
            p.wait_ev("tensor", ev_binit)
            p.wait_ev("tensor", bank_free[sb0])
            p.wait_ev("tensor", bank_free[sb1])
            if x_wait_din is not None:
                p.wait("tensor", "din", x_wait_din)
                p.wait("vector", "din", x_wait_din)
            ev_sum = None
            for k in range(DC):
                ev_sum = p.do("tensor", lambda e, k=k: e.matmul(
                    pbank[sb0][0:1, 0:ntok], onec_f[:],
                    sX[:, T * k + tok0:T * k + tok0 + ntok],
                    start=(k == 0), stop=(k == DC - 1)),
                    inc=("t" if k == DC - 1 else None))
            ev_sq = [None] * DC
            for k in range(DC):
                s3 = s3c[:, ntok * (k % 2):ntok * (k % 2 + 1)]
                if k >= 2:
                    p.wait_ev("vector", ev_sq[k - 2])
                ev_v = p.do("vector", lambda e, s3=s3, k=k: e.tensor_mul(
                    s3, sX[:, T * k + tok0:T * k + tok0 + ntok],
                    sX[:, T * k + tok0:T * k + tok0 + ntok]), inc="v")
                p.wait_ev("tensor", ev_v)
                ev_sq[k] = p.do("tensor", lambda e, s3=s3, k=k: e.matmul(
                    pbank[sb1][0:1, 0:ntok], onec_f[:], s3,
                    start=(k == 0), stop=(k == DC - 1)), inc="t")
            p.wait_ev("vector", ev_sum)
            p.wait_ev("vector", ev_sq[DC - 1])
            p.wait_ev("vector", ln_state["statrow_free"])
            p.do("vector", lambda e: e.tensor_single_scalar(
                out=m_row, in_=pbank[sb0][0:1, 0:ntok], scalar=INV_D,
                op=ALU.mult))
            p.do("vector", lambda e: e.tensor_scalar(
                out=veA, in0=pbank[sb1][0:1, 0:ntok], scalar1=INV_D,
                scalar2=EPS_LN, op0=ALU.mult, op1=ALU.add))
            p.do("vector", lambda e: e.tensor_mul(msq, m_row, m_row))
            ev_ve = p.do("vector", lambda e: e.tensor_sub(veA, veA, msq),
                         inc="v")
            bank_free[sb0] = ev_ve
            bank_free[sb1] = ev_ve
            p.wait_ev("scalar", ev_ve)
            p.do("scalar", lambda e: e.activation(out=nrow, in_=veA, func=AF.Ln))
            ev_A = p.do("scalar", lambda e: e.activation(
                out=nrow, in_=nrow, func=AF.Exp, scale=-0.5), inc="s")
            p.wait_ev("vector", ev_A)
            p.do("vector", lambda e: e.tensor_mul(n2row, nrow, nrow))
            p.do("vector", lambda e: e.tensor_mul(n2row, n2row, veA))
            p.do("vector", lambda e: e.tensor_scalar(
                out=n2row, in0=n2row, scalar1=-0.5, scalar2=1.5,
                op0=ALU.mult, op1=ALU.add))
            ev_A = p.do("vector", lambda e: e.tensor_mul(veA, nrow, n2row),
                        inc="v")
            ev_B = p.do("vector", lambda e: e.scalar_tensor_tensor(
                out=Brow, in0=veA, scalar=-1.0, in1=m_row,
                op0=ALU.mult, op1=ALU.mult), inc="v")
            return (ev_A, ev_B)

        def emit_ln_apply(tok0, ntok, out_tile, out_col0, tmpf, evAB,
                          bb=(6, 7)):
            veA = statrow[0:1, 512:512 + ntok]
            Brow = statrow[0:1, 1024:1024 + ntok]
            b0, b1 = bb
            ev_A, ev_B = evAB
            p.wait_ev("tensor", bank_free[b0])
            p.wait_ev("tensor", bank_free[b1])
            p.wait_ev("tensor", ev_A)
            p.wait_ev("tensor", ev_B)
            p.do("tensor", lambda e: e.matmul(
                pbank[b0][:, 0:ntok], oner_f[:], veA, start=True, stop=True))
            ev_bc = p.do("tensor", lambda e: e.matmul(
                pbank[b1][:, 0:ntok], oner_f[:], Brow, start=True, stop=True),
                inc="t")
            ln_state["statrow_free"] = ev_bc
            p.wait_ev("scalar", ev_bc)
            p.wait_ev("scalar", ln_state["ab_free"])
            p.do("scalar", lambda e: e.activation(
                out=Abc[:, 0:ntok], in_=pbank[b0][:, 0:ntok], func=AF.Copy))
            ev_ab = p.do("scalar", lambda e: e.activation(
                out=Bbc[:, 0:ntok], in_=pbank[b1][:, 0:ntok], func=AF.Copy),
                inc="s")
            bank_free[b0] = ev_ab
            bank_free[b1] = ev_ab
            p.wait_ev("vector", ev_ab)
            evs = []
            for k in range(DC):
                p.do("vector", lambda e, k=k: e.tensor_mul(
                    tmpf[:, 0:ntok],
                    sX[:, T * k + tok0:T * k + tok0 + ntok],
                    Abc[:, 0:ntok]))
                evs.append(p.do("vector", lambda e, k=k: e.tensor_add(
                    out_tile[:, out_col0 + ntok * k:out_col0 + ntok * (k + 1)],
                    tmpf[:, 0:ntok],
                    Bbc[:, 0:ntok]), inc="v"))
            ln_state["ab_free"] = evs[-1]
            return evs

        def emit_ln_full(tok0, ntok, s3c, tmpf, out_tile, out_col0,
                         sb=(4, 5), bb=(6, 7), x_wait_din=None):
            evAB = emit_ln_stats(tok0, ntok, s3c, sb, x_wait_din)
            return emit_ln_apply(tok0, ntok, out_tile, out_col0, tmpf,
                                 evAB, bb)[-1]

        def emit_ffn(first):
            if not first:
                p.wait("sync", "v", p.cnt["v"])
                p.wait("sync", "t", p.cnt["t"])
            esf = contextlib.ExitStack()
            with esf:
                sW2 = sbuf("sW2F" + str(first), [128, HC * D], F8, esf)
                mask2 = sbuf("maskF" + str(first), [128, DC * ST], F8, esf)
                xnq = sbuf("xnqF" + str(first), [128, 2 * DC * ST], F8, esf)
                hbf = sbuf("hbfF" + str(first), [128, 2 * 512], BF16, esf)
                hq8 = sbuf("hq8F" + str(first), [128, HC * ST], F8, esf)
                y2b = sbuf("y2bF" + str(first), [128, DC * 512], BF16, esf)
                s3c = sbuf("s3F" + str(first), [128, 2 * 512], F32, esf)
                tmpf = s3c

                if not first:
                    p.wait("gpsimd", "v", p.cnt["v"])
                    p.wait("gpsimd", "t", p.cnt["t"])
                    p.wait("tensor", "v", p.cnt["v"])
                ev_w2q = []
                for q in range(4):
                    p.do("gpsimd", lambda e, q=q: e.dma_start(
                        out=sW2[:, D * 8 * q:D * 8 * (q + 1)]
                            .rearrange("p (c f) -> p c f", c=8),
                        in_=w2_in[:, 8 * q:8 * (q + 1), :]), inc="dw", k=16)
                    p.wait("gpsimd", "dw", p.cnt["dw"])
                    ev_w2q.append(p.now("dw"))

                mask_free = [None, None]
                mm1_last = [None, None]
                hq8_free = None
                hbf_free = [None, None]
                y2_free = [None] * DC

                evAB = emit_ln_stats(0, ST, s3c,
                                     x_wait_din=(x_ready[0] if first else None))
                apply_evs = emit_ln_apply(0, ST, xnq, 0, tmpf, evAB)

                sW1_3 = sW1[:].rearrange("p (c f) -> p c f", c=DC)
                sW2_3 = sW2[:].rearrange("p (c f) -> p c f", c=HC)
                hq8_3 = hq8[:].rearrange("p (c f) -> p c f", c=HC)
                ev_res = None
                for s in range(NST):
                    msl = s % 2
                    p.wait_ev("sync", mask_free[0])
                    p.do("sync", lambda e, s=s: e.dma_start(
                        out=mask2[:].rearrange("p (c f) -> p c f", c=DC),
                        in_=maskT.rearrange("(c p) f -> p c f", p=128)
                            [:, :, ST * s:ST * (s + 1)]), inc="din", k=16)
                    p.wait("sync", "din", p.cnt["din"])
                    ev_mask = p.now("din")

                    xnq_3 = xnq[:, DC * ST * msl:DC * ST * (msl + 1)].rearrange(
                        "p (c f) -> p c f", c=DC)
                    ev_cast32 = None
                    for m in range(HC):
                        b = m % 2
                        p.wait_ev("tensor", bank_free[b])
                        p.wait_ev("tensor", ev_w1a if m < HC // 2 else ev_w1)
                        if DR_MM1:
                            for k in range(DC // 2):
                                if m == 0:
                                    p.wait_ev("tensor", apply_evs[2 * k + 1])
                                evg = p.do("tensor", lambda e, m=m, k=k, b=b, msl=msl: e.matmul(
                                    pbank[b][:],
                                    sW1_3[:, 2 * k:2 * k + 2, 128 * m:128 * (m + 1)],
                                    xnq_3[:, 2 * k:2 * k + 2, :],
                                    start=(k == 0), stop=(k == DC // 2 - 1),
                                    perf_mode=DRPM),
                                    inc=("t" if k == DC // 2 - 1 else None))
                        else:
                            for k in range(DC):
                                if m == 0:
                                    p.wait_ev("tensor", apply_evs[k])
                                evg = p.do("tensor", lambda e, m=m, k=k, b=b, msl=msl: e.matmul(
                                    pbank[b][:],
                                    sW1[:, HID * k + 128 * m:HID * k + 128 * (m + 1)],
                                    xnq[:, DC * ST * msl + ST * k:DC * ST * msl + ST * (k + 1)],
                                    start=(k == 0), stop=(k == DC - 1)),
                                    inc=("t" if k == DC - 1 else None))
                        hsl = m % 2
                        p.wait_ev("scalar", evg)
                        p.wait_ev("scalar", hbf_free[hsl])
                        if first and m == 0 and s == 0:
                            p.wait_ev("scalar", ev_binit)
                        ev_ev = p.do("scalar", lambda e, m=m, b=b, hsl=hsl: e.activation(
                            out=hbf[:, 512 * hsl:512 * (hsl + 1)],
                            in_=pbank[b][:], func=AF.Relu,
                            bias=b1qf[:, m:m + 1]), inc="s")
                        bank_free[b] = ev_ev
                        p.wait_ev("vector", ev_ev)
                        if hq8_free is not None and m == 0:
                            p.wait_ev("vector", hq8_free)
                        ev_cast32 = p.do("vector", lambda e, m=m, hsl=hsl: e.tensor_copy(
                            out=hq8[:, ST * m:ST * (m + 1)],
                            in_=hbf[:, 512 * hsl:512 * (hsl + 1)]), inc="v")
                        hbf_free[hsl] = ev_cast32
                    mm1_last[msl] = evg

                    if s + 1 < NST:
                        evAB = emit_ln_stats(
                            ST * (s + 1), ST, s3c,
                            x_wait_din=(x_ready[s + 1] if first else None))

                    y_evs = []
                    for n in range(DC):
                        b = 2 + n % 2
                        p.wait_ev("tensor", bank_free[b])
                        p.wait_ev("tensor", ev_cast32)
                        if DR_MM2:
                            for j in range(HC // 2):
                                if s == 0:
                                    p.wait_ev("tensor", ev_w2q[(2 * j + 1) // 8])
                                evg = p.do("tensor", lambda e, n=n, j=j, b=b: e.matmul(
                                    pbank[b][:],
                                    sW2_3[:, 2 * j:2 * j + 2, 128 * n:128 * (n + 1)],
                                    hq8_3[:, 2 * j:2 * j + 2, :],
                                    start=(j == 0), stop=(j == HC // 2 - 1),
                                    perf_mode=DRPM),
                                    inc=("t" if j == HC // 2 - 1 else None))
                        else:
                            for j in range(HC):
                                if s == 0:
                                    p.wait_ev("tensor", ev_w2q[j // 8])
                                evg = p.do("tensor", lambda e, n=n, j=j, b=b: e.matmul(
                                    pbank[b][:],
                                    sW2[:, D * j + 128 * n:D * j + 128 * (n + 1)],
                                    hq8[:, ST * j:ST * (j + 1)],
                                    start=(j == 0), stop=(j == HC - 1)),
                                    inc=("t" if j == HC - 1 else None))
                        if n == DC - 1:
                            hq8_free = evg
                        p.wait_ev("scalar", evg)
                        p.wait_ev("scalar", y2_free[n])
                        ev_y = p.do("scalar", lambda e, n=n, b=b: e.activation(
                            out=y2b[:, 512 * n:512 * (n + 1)],
                            in_=pbank[b][:], func=AF.Identity,
                            bias=b2qf[:, n:n + 1]), inc="s")
                        bank_free[b] = ev_y
                        y_evs.append(ev_y)

                    if s + 1 < NST:
                        p.wait_ev("vector", mm1_last[(s + 1) % 2])
                        apply_evs = emit_ln_apply(
                            ST * (s + 1), ST, xnq, DC * ST * ((s + 1) % 2),
                            tmpf, evAB)

                    for n in range(DC):
                        p.wait_ev("vector", y_evs[n])
                        if n == 0:
                            p.wait_ev("vector", ev_mask)
                        p.do("vector", lambda e, n=n: e.tensor_mul(
                            tmpf[:, 0:512],
                            y2b[:, 512 * n:512 * (n + 1)],
                            mask2[:, ST * n:ST * (n + 1)]))
                        ev_res = p.do("vector", lambda e, n=n, s=s: e.tensor_add(
                            sX[:, T * n + ST * s:T * n + ST * (s + 1)],
                            sX[:, T * n + ST * s:T * n + ST * (s + 1)],
                            tmpf[:, 0:512]), inc="v")
                        y2_free[n] = ev_res
                        if n == DC - 1:
                            mask_free[0] = ev_res
            return ev_res

        ev_ffn1 = emit_ffn(first=True) if "ffn1" in phases else None

        wk_in = wkt.rearrange("(c p) f -> p c f", p=128)
        p.do("gpsimd", lambda e: e.dma_start(
            out=wslotA[:].rearrange("p (c f) -> p c f", c=DC), in_=wk_in),
            inc="dw", k=16)
        p.wait("gpsimd", "dw", p.cnt["dw"])
        ev_wk = p.now("dw")

        if "attn" in phases:
            es_a = contextlib.ExitStack()
            with es_a:
                es_kv = contextlib.ExitStack()
                with es_kv:
                    wslotB = sbuf("wslotB", [128, DC * D], BF16, es_kv)
                    rbrow = sbuf("rbrow", [1, 2 * D], F32, es_kv)
                    xn2 = sbuf("xn2", [128, 2 * DC * ST], BF16, es_kv)
                    ktm = sbuf("ktm", [128, D], BF16, es_kv)
                    v65 = sbuf("v65", [128, NH * 65], BF16, es_kv)
                    s3A = sbuf("s3A", [128, 2 * 512], F32, es_kv)
                    tmpa = sbuf("tmpa", [128, 512], F32, es_kv)
                    tmpb = sbuf("tmpb", [128, 512], F32, es_kv)

                    p.wait_ev("sync", ev_ffn1)
                    p.wait("sync", "t", p.cnt["t"])
                    p.do("sync", lambda e: e.dma_start(out=rbrow[:], in_=rb_d),
                         inc="din", k=16)
                    p.wait("sync", "din", p.cnt["din"])
                    ev_rbl = p.now("din")
                    p.wait_ev("gpsimd", ev_ffn1)
                    p.wait("gpsimd", "t", p.cnt["t"])
                    wv_in = wvt.rearrange("(c p) f -> p c f", p=128)
                    p.do("gpsimd", lambda e: e.dma_start(
                        out=wslotB[:].rearrange("p (c f) -> p c f", c=DC),
                        in_=wv_in), inc="dw", k=16)
                    p.wait("gpsimd", "dw", p.cnt["dw"])
                    ev_wv = p.now("dw")

                    p.do("vector", lambda e: e.memset(
                        v65[:].rearrange("p (h c) -> p h c", c=65)[:, :, 64:65], 1.0))

                    ktm_free = None
                    v65_free = None
                    ev_kvadd = None
                    xn2_last = [None, None]
                    p.wait("tensor", "v", p.cnt["v"])
                    evAB = emit_ln_stats(0, ST, s3A)
                    apply_evs = emit_ln_apply(0, ST, xn2, 0, s3A, evAB)
                    for s in range(NST):
                        msl = s % 2
                        xcol = DC * ST * msl
                        for t4 in range(4):
                            t_ = 4 * s + t4
                            ev_kgrp = [None, None]
                            ev_vgrp = [None, None]
                            for proj in range(2):
                                for half in range(2):
                                    b = 2 * proj + half
                                    p.wait_ev("tensor", bank_free[b])
                                    if t_ == 0 and half == 0:
                                        p.wait_ev("tensor",
                                                  ev_wk if proj == 0 else ev_wv)
                                        p.wait_ev("tensor", ev_rbl)
                                    wsl = wslotA if proj == 0 else wslotB
                                    for k in range(DC):
                                        if t4 == 0 and proj == 0:
                                            p.wait_ev("tensor", apply_evs[k])
                                        p.do("tensor", lambda e, k=k, b=b, t4=t4, half=half, wsl=wsl, xcol=xcol: e.matmul(
                                            pbank[b][:],
                                            xn2[:, xcol + ST * k + 128 * t4:xcol + ST * k + 128 * (t4 + 1)],
                                            wsl[:, D * k + 512 * half:D * k + 512 * (half + 1)],
                                            start=(k == 0), stop=False))
                                    evg = p.do("tensor", lambda e, b=b, half=half, proj=proj: e.matmul(
                                        pbank[b][:], oner_f[:],
                                        rbrow[0:1, D * proj + 512 * half:D * proj + 512 * (half + 1)],
                                        start=False, stop=True), inc="t")
                                    if proj == 0:
                                        ev_kgrp[half] = evg
                                    else:
                                        ev_vgrp[half] = evg
                            ev_k = None
                            for half in range(2):
                                b = half
                                p.wait_ev("vector", ev_kgrp[half])
                                p.wait_ev("vector", ktm_free)
                                p.do("vector", lambda e, b=b: e.tensor_scalar_max(
                                    out=tmpa[:], in0=pbank[b][:], scalar1=0.0))
                                ev_mn = p.do("vector", lambda e, b=b: e.tensor_scalar_min(
                                    out=tmpb[:], in0=pbank[b][:], scalar1=0.0),
                                    inc="v")
                                p.wait_ev("scalar", ev_mn)
                                ev_ex = p.do("scalar", lambda e: e.activation(
                                    out=tmpb[:], in_=tmpb[:], func=AF.Exp),
                                    inc="s")
                                p.wait_ev("vector", ev_ex)
                                ev_k = p.do("vector", lambda e, half=half: e.tensor_add(
                                    ktm[:, 512 * half:512 * (half + 1)],
                                    tmpa[:], tmpb[:]), inc="v")
                                bank_free[b] = ev_k
                            ev_v = None
                            for half in range(2):
                                b = 2 + half
                                p.wait_ev("vector", ev_vgrp[half])
                                p.wait_ev("vector", v65_free)
                                ev_v = p.do("vector", lambda e, b=b, half=half: e.tensor_copy(
                                    out=v65[:].rearrange("p (h c) -> p h c", c=65)
                                        [:, 8 * half:8 * (half + 1), 0:64],
                                    in_=pbank[b][:].rearrange("p (h c) -> p h c", c=64)),
                                    inc="v")
                                bank_free[b] = ev_v
                            p.wait_ev("tensor", ev_k)
                            p.wait_ev("tensor", ev_v)
                            for q in range(4, 8):
                                p.wait_ev("tensor", bank_free[q])
                            evkv = None
                            for h in range(NH):
                                bkn = 4 + h // 4
                                r0 = 64 * (h % 2)
                                c0 = 65 * ((h % 4) // 2)
                                evkv = p.do("tensor", lambda e, h=h, bkn=bkn, r0=r0, c0=c0: e.matmul(
                                    pbank[bkn][r0:r0 + 64, c0:c0 + 65],
                                    ktm[:, HD * h:HD * (h + 1)],
                                    v65[:, 65 * h:65 * (h + 1)],
                                    start=True, stop=True,
                                    tile_position=(0, r0)),
                                    inc=("t" if h == NH - 1 else None))
                            ktm_free = evkv
                            v65_free = evkv
                            if t4 == 3:
                                xn2_last[msl] = evkv
                            p.wait_ev("vector", evkv)
                            for q in range(4):
                                if t_ == 0:
                                    ev_kvadd = p.do("vector", lambda e, q=q: e.tensor_copy(
                                        out=skv[:, 130 * q:130 * (q + 1)],
                                        in_=pbank[4 + q][:, 0:130]), inc="v")
                                else:
                                    ev_kvadd = p.do("vector", lambda e, q=q: e.tensor_add(
                                        skv[:, 130 * q:130 * (q + 1)],
                                        skv[:, 130 * q:130 * (q + 1)],
                                        pbank[4 + q][:, 0:130]), inc="v")
                            for q in range(4, 8):
                                bank_free[q] = ev_kvadd
                        if s + 1 < NST:
                            evAB = emit_ln_stats(ST * (s + 1), ST, s3A)
                            p.wait_ev("vector", xn2_last[(s + 1) % 2])
                            apply_evs = emit_ln_apply(
                                ST * (s + 1), ST, xn2, DC * ST * ((s + 1) % 2),
                                s3A, evAB)

                    p.wait_ev("gpsimd", ev_kvadd)
                    p.do("gpsimd", lambda e: e.dma_start(
                        out=cc_in[:, :], in_=skv[:]), inc="dw", k=16)
                    p.wait("gpsimd", "dw", p.cnt["dw"])
                    p.do("gpsimd", lambda e: e.collective_compute(
                        "AllReduce", ALU.add,
                        replica_groups=[[0, 1], [2, 3], [4, 5], [6, 7]],
                        ins=[cc_in.ap().opt()], outs=[cc_out.ap().opt()]),
                        inc="cc")
                    p.wait("gpsimd", "t", p.cnt["t"])
                    wq_in = wqt.rearrange("(c p) f -> p c f", p=128)
                    p.do("gpsimd", lambda e: e.dma_start(
                        out=wslotA[:].rearrange("p (c f) -> p c f", c=DC),
                        in_=wq_in), inc="dw", k=16)
                    p.wait("gpsimd", "dw", p.cnt["dw"])
                    ev_wq = p.now("dw")
                    p.wait("gpsimd", "cc", 1)
                    p.do("gpsimd", lambda e: e.dma_start(
                        out=skv[:], in_=cc_out[:, :]), inc="dw", k=16)
                    p.wait("gpsimd", "dw", p.cnt["dw"])
                    ev_kvtot = p.now("dw")

                es_q = contextlib.ExitStack()
                with es_q:
                    wslotO = sbuf("wslotO", [128, DC * D], F32, es_q)
                    xn2q = sbuf("xn2q", [128, DC * QT], BF16, es_q)
                    qphi = sbuf("qphi", [128, DC * QT], BF16, es_q)
                    outpre = sbuf("outpre", [128, DC * QT], F32, es_q)
                    kvbd = sbuf("kvbd", [128, DC * 128], BF16, es_q)
                    zbd = sbuf("zbd", [128, DC * 16], BF16, es_q)
                    tmpn = sbuf("tmpn", [16, QT], F32, es_q)
                    s3Q = sbuf("s3Q", [128, 2 * QT], F32, es_q)
                    tmpaQ = sbuf("tmpaQ", [128, QT], F32, es_q)
                    tmpeQ = sbuf("tmpeQ", [128, QT], F32, es_q)

                    p.wait("sync", "t", p.cnt["t"])
                    p.wait("sync", "v", p.cnt["v"])
                    p.wait_ev("sync", ev_wq)
                    for k in range(DC):
                        p.do("sync", lambda e, k=k: e.dma_start(
                            out=wslotO[:, D * k:D * (k + 1)],
                            in_=wot[128 * k:128 * (k + 1), :]), inc="din", k=16)
                    p.wait("sync", "din", p.cnt["din"])
                    ev_wo = p.now("din")



                    ev_res_attn = None
                    qphi_free = None
                    outpre_free = None
                    qproj_last = None
                    evABq = emit_ln_stats(0, QT, s3Q)
                    apply_evq = emit_ln_apply(0, QT, xn2q, 0, tmpaQ, evABq)

                    for s in range(NQT):
                        ev_qphi = None
                        for n in range(DC):
                            b = 4 + n % 2
                            p.wait_ev("tensor", bank_free[b])
                            if s == 0 and n == 0:
                                p.wait_ev("tensor", ev_wq)
                            for k in range(DC):
                                if n == 0:
                                    p.wait_ev("tensor", apply_evq[k])
                                evg = p.do("tensor", lambda e, n=n, k=k, b=b: e.matmul(
                                    pbank[b][:, 0:QT],
                                    wslotA[:, D * k + 128 * n:D * k + 128 * (n + 1)],
                                    xn2q[:, QT * k:QT * (k + 1)],
                                    start=(k == 0), stop=(k == DC - 1)),
                                    inc=("t" if k == DC - 1 else None))
                            if n == DC - 1:
                                qproj_last = evg
                            p.wait_ev("vector", evg)
                            p.wait_ev("vector", qphi_free)
                            ev_mn = p.do("vector", lambda e, n=n, b=b: e.tensor_scalar(
                                out=tmpaQ[:], in0=pbank[b][:, 0:QT],
                                scalar1=bq_pp[:, n:n + 1], scalar2=0.0,
                                op0=ALU.add, op1=ALU.min), inc="v")
                            p.wait_ev("scalar", ev_mn)
                            p.wait_ev("scalar", ev_bq)
                            ev_ex = p.do("scalar", lambda e: e.activation(
                                out=tmpeQ[:], in_=tmpaQ[:], func=AF.Exp), inc="s")
                            p.wait_ev("vector", ev_ex)
                            p.do("vector", lambda e, n=n, b=b: e.tensor_scalar(
                                out=tmpaQ[:], in0=pbank[b][:, 0:QT],
                                scalar1=bq_pp[:, n:n + 1], scalar2=0.0,
                                op0=ALU.add, op1=ALU.max))
                            ev_qphi = p.do("vector", lambda e, n=n: e.tensor_add(
                                qphi[:, QT * n:QT * (n + 1)], tmpaQ[:], tmpeQ[:]),
                                inc="v")
                            bank_free[b] = ev_qphi
                        if s == 0:
                            p.wait_ev("vector", ev_kvtot)
                            p.do("vector", lambda e: e.memset(kvbd[:], 0.0))
                            p.do("vector", lambda e: e.memset(zbd[:], 0.0))
                            ev_bd = None
                            for h in range(NH):
                                r0 = 64 * (h % 2)
                                c0 = 130 * (h // 4) + 65 * ((h % 4) // 2)
                                blk = h // 2
                                p.do("vector", lambda e, h=h, r0=r0, c0=c0, blk=blk: e.tensor_copy(
                                    out=kvbd[r0:r0 + 64,
                                             128 * blk + 64 * (h % 2):128 * blk + 64 * (h % 2) + 64],
                                    in_=skv[r0:r0 + 64, c0:c0 + 64]))
                                ev_bd = p.do("vector", lambda e, h=h, r0=r0, c0=c0, blk=blk: e.tensor_copy(
                                    out=zbd[r0:r0 + 64, 16 * blk + h:16 * blk + h + 1],
                                    in_=skv[r0:r0 + 64, c0 + 64:c0 + 65]), inc="v")
                        p.wait_ev("tensor", bank_free[6])
                        p.wait_ev("tensor", ev_bd)
                        p.wait_ev("tensor", ev_qphi)
                        for k in range(DC):
                            evg = p.do("tensor", lambda e, k=k: e.matmul(
                                pbank[6][0:16, 0:QT], zbd[:, 16 * k:16 * (k + 1)],
                                qphi[:, QT * k:QT * (k + 1)],
                                start=(k == 0), stop=(k == DC - 1)),
                                inc=("t" if k == DC - 1 else None))
                        p.wait_ev("vector", evg)
                        p.do("vector", lambda e: e.tensor_single_scalar(
                            out=tmpn[:], in_=pbank[6][0:16, 0:QT],
                            scalar=EPS_ATTN, op=ALU.add))
                        ev_rec = p.do("vector", lambda e: e.reciprocal(
                            out=tmpn[:], in_=tmpn[:]), inc="v")
                        bank_free[6] = ev_rec
                        ev_op = None
                        for k in range(DC):
                            b = 4 + k % 2
                            br = 6 + k % 2
                            p.wait_ev("tensor", bank_free[b])
                            p.wait_ev("tensor", bank_free[br])
                            p.wait_ev("tensor", ev_rec)
                            evg = p.do("tensor", lambda e, k=k, b=b: e.matmul(
                                pbank[b][:, 0:QT], kvbd[:, 128 * k:128 * (k + 1)],
                                qphi[:, QT * k:QT * (k + 1)],
                                start=True, stop=True))
                            evg = p.do("tensor", lambda e, k=k, br=br: e.matmul(
                                pbank[br][:, 0:QT], ebc[:, 128 * k:128 * (k + 1)],
                                tmpn[:], start=True, stop=True), inc="t")
                            p.wait_ev("vector", evg)
                            p.wait_ev("vector", outpre_free)
                            p.do("vector", lambda e, k=k, b=b: e.tensor_copy(
                                out=outpre[:, QT * k:QT * (k + 1)],
                                in_=pbank[b][:, 0:QT]))
                            ev_op = p.do("vector", lambda e, k=k, br=br: e.tensor_mul(
                                outpre[:, QT * k:QT * (k + 1)],
                                outpre[:, QT * k:QT * (k + 1)],
                                pbank[br][:, 0:QT]), inc="v")
                            bank_free[b] = ev_op
                            bank_free[br] = ev_op
                        qphi_free = ev_op
                        if s + 1 < NQT:
                            evABq = emit_ln_stats(QT * (s + 1), QT, s3Q)
                            p.wait_ev("vector", qproj_last)
                            apply_evq = emit_ln_apply(
                                QT * (s + 1), QT, xn2q, 0, tmpaQ, evABq)
                        for n in range(DC):
                            b = n % 2
                            p.wait_ev("tensor", bank_free[b])
                            p.wait_ev("tensor", ev_op)
                            if s == 0 and n == 0:
                                p.wait_ev("tensor", ev_wo)
                            for k in range(DC):
                                evg = p.do("tensor", lambda e, n=n, k=k, b=b: e.matmul(
                                    pbank[b][:, 0:QT],
                                    wslotO[:, D * k + 128 * n:D * k + 128 * (n + 1)],
                                    outpre[:, QT * k:QT * (k + 1)],
                                    start=(k == 0), stop=(k == DC - 1)),
                                    inc=("t" if k == DC - 1 else None))
                            p.wait_ev("vector", evg)
                            p.wait_ev("vector", ev_bo)
                            ev_res_attn = p.do("vector", lambda e, n=n, s=s, b=b: e.scalar_tensor_tensor(
                                out=sX[:, T * n + QT * s:T * n + QT * (s + 1)],
                                in0=pbank[b][:, 0:QT], scalar=bo_pp[:, n:n + 1],
                                in1=sX[:, T * n + QT * s:T * n + QT * (s + 1)],
                                op0=ALU.add, op1=ALU.add), inc="v")
                            bank_free[b] = ev_res_attn
                        outpre_free = evg

        ev_ffn2 = emit_ffn(first=False) if "ffn2" in phases else None

        p.wait_ev("sync", ev_ffn2)
        p.wait("sync", "v", p.cnt["v"])
        ndo = 0
        for k in range(DC):
            p.do("sync", lambda e, k=k: e.dma_start(
                out=outT[128 * k:128 * (k + 1), :],
                in_=sX[:, T * k:T * (k + 1)]), inc="do", k=16)
            ndo += 16
        p.wait("sync", "do", ndo)

        with nc.Block() as block:
            for eng_name, deco in (("sync", block.sync), ("vector", block.vector),
                                   ("scalar", block.scalar), ("tensor", block.tensor),
                                   ("gpsimd", block.gpsimd)):
                ops = p.ops[eng_name]
                def run(e, ops=ops):
                    for f in ops:
                        f(e)
                deco(run)

    return nc


def _threefry2x32(k0, k1, x0, x1):
    x0 = x0.astype(np.uint32).copy()
    x1 = x1.astype(np.uint32).copy()
    ks0, ks1 = np.uint32(k0), np.uint32(k1)
    ks2 = np.uint32(ks0 ^ ks1 ^ np.uint32(0x1BD11BDA))
    rots = [(13, 15, 26, 6), (17, 29, 16, 24)]
    with np.errstate(over="ignore"):
        x0 = (x0 + ks0).astype(np.uint32)
        x1 = (x1 + ks1).astype(np.uint32)
        ks = [ks1, ks2, ks0]
        for r in range(5):
            for d in rots[r % 2]:
                x0 = (x0 + x1).astype(np.uint32)
                x1 = ((x1 << np.uint32(d)) | (x1 >> np.uint32(32 - d))).astype(np.uint32)
                x1 = (x1 ^ x0).astype(np.uint32)
            x0 = (x0 + ks[r % 3]).astype(np.uint32)
            x1 = (x1 + ks[(r + 1) % 3] + np.uint32(r + 1)).astype(np.uint32)
    return x0, x1


def _host_constants():
    n = 4 * 4096 * 1024
    i = np.arange(n, dtype=np.uint64)
    b1, b2 = _threefry2x32(0, 42, (i >> np.uint64(32)).astype(np.uint32),
                           (i & np.uint64(0xFFFFFFFF)).astype(np.uint32))
    bits = (b1 ^ b2).astype(np.uint32)
    u = ((bits >> np.uint32(9)) | np.uint32(0x3F800000)).view(np.float32)         - np.float32(1.0)
    keep = (u < np.float32(0.8)).reshape(4, 4096, 1024)
    mask = np.where(keep, np.float32(0.625), np.float32(0.0))
    ebc = np.zeros((16, 1024), np.float32)
    for h in range(16):
        for k in range(8):
            if h // 2 == k:
                ebc[h, 128 * k + 64 * (h % 2):128 * k + 64 * (h % 2) + 64] = 1.0
    return mask, ebc


def kernel(L0_attn_x, L0_attn_Wq, L0_attn_bq, L0_attn_Wk, L0_attn_bk,
           L0_attn_Wv, L0_attn_bv, L0_attn_Wo, L0_attn_bo,
           L0_ffn_W1, L0_ffn_b1, L0_ffn_W2, L0_ffn_b2):
    x = np.asarray(L0_attn_x, np.float32)
    f8 = ml_dtypes.float8_e5m2

    mask, ebc = _host_constants()

    w1t = np.ascontiguousarray(np.asarray(L0_ffn_W1, np.float32).T)
    w2t = np.ascontiguousarray(np.asarray(L0_ffn_W2, np.float32).T)
    wqt = np.ascontiguousarray(np.asarray(L0_attn_Wq, np.float32).T)
    wkt = np.ascontiguousarray(np.asarray(L0_attn_Wk, np.float32).T)
    wvt = np.ascontiguousarray(np.asarray(L0_attn_Wv, np.float32).T)
    wot = np.ascontiguousarray(np.asarray(L0_attn_Wo, np.float32).T)
    b1_pp = np.ascontiguousarray(np.asarray(L0_ffn_b1, np.float32).reshape(HC, 128).T)
    b2_pp = np.ascontiguousarray(np.asarray(L0_ffn_b2, np.float32).reshape(DC, 128).T)
    bq_pp = np.ascontiguousarray(np.asarray(L0_attn_bq, np.float32).reshape(DC, 128).T)
    bo_pp = np.ascontiguousarray(np.asarray(L0_attn_bo, np.float32).reshape(DC, 128).T)
    rb = np.concatenate([np.asarray(L0_attn_bk, np.float32),
                         np.asarray(L0_attn_bv, np.float32)]).reshape(1, 2 * D)
    rb = np.ascontiguousarray(rb)

    nc = build_nc()
    in_maps = []
    for c in range(8):
        b, h = c // 2, c % 2
        xc = np.ascontiguousarray(x[b, T * h:T * (h + 1), :].T)
        mc = np.ascontiguousarray(mask[b, T * h:T * (h + 1), :].T).astype(f8)
        in_maps.append({
            "xT": xc, "w1t": w1t, "w2t": w2t, "wqt": wqt, "wkt": wkt,
            "wvt": wvt, "wot": wot, "b1_pp": b1_pp, "b2_pp": b2_pp,
            "bq_d": bq_pp, "bo_d": bo_pp, "rb_d": rb, "ebc_d": ebc,
            "maskT": mc,
        })
    res = run_bass_kernel_spmd(nc, in_maps, core_ids=list(range(8)))
    out = np.empty((4, 4096, 1024), np.float32)
    for c in range(8):
        b, h = c // 2, c % 2
        out[b, T * h:T * (h + 1), :] = np.asarray(res.results[c]["outT"]).T
    return out
